# version 02
# baseline (speedup 1.0000x reference)
import contextlib
import numpy as np
import concourse.bass as bass
import concourse.mybir as mybir
import concourse.bass_isa as bass_isa

F32 = mybir.dt.float32
BF16 = mybir.dt.bfloat16
I32 = mybir.dt.int32
AF = mybir.ActivationFunctionType
ALU = mybir.AluOpType
AX = mybir.AxisListType.X
RADD = bass_isa.ReduceOp.add
RABSMAX = bass_isa.ReduceOp.absmax

MAGIC = float(3 << 22)
D = 1024
DC = 8
H0, HC0 = 4096, 32
H1, HC1 = 3072, 24
NCORES = 8
EPS = 1e-5
NHEAD = 16


class Bld:
    ENGINES = ("sync", "tensor", "vector", "scalar", "gpsimd")

    def __init__(self, nc):
        self.nc = nc
        self.plans = {e: [] for e in self.ENGINES}
        self.events = {}
        self.cnt = {}
        self.sems = {}

    def on(self, eng):
        def deco(fn):
            self.plans[eng].append(fn)
        return deco

    def mark(self, semname, evname=None):
        k = 16 if semname.startswith("d_") else 1
        self.cnt[semname] = self.cnt.get(semname, 0) + k
        tok = (semname, self.cnt[semname])
        if evname is not None:
            self.events[evname] = tok
        return tok

    def ev(self, name):
        return self.events[name]

    def get(self, name):
        return self.events.get(name)

    def w(self, e, *toks):
        for tok in toks:
            if tok is None:
                continue
            if isinstance(tok, (list, tuple)) and tok and not isinstance(tok[0], str):
                self.w(e, *tok)
                continue
            sem, val = tok
            e.wait_ge(self.sems[sem], val)

    def i(self, instr, tok):
        sem, _ = tok
        instr.then_inc(self.sems[sem], 16 if sem.startswith("d_") else 1)
        return instr

    def total(self, semname):
        return (semname, self.cnt.get(semname, 0))

    def wn(self, e, *names):
        for n in names:
            tok = self.events.get(n)
            if tok is not None:
                sem, val = tok
                e.wait_ge(self.sems[sem], val)


class Arena:
    def __init__(self, tensor, n):
        self.t, self.n, self.off = tensor, n, 0

    def take(self, *shape):
        sz = int(np.prod(shape))
        assert self.off + sz <= self.n, f"arena overflow: {self.off}+{sz}>{self.n}"
        ap = self.t[:, self.off:self.off + sz]
        self.off += sz
        if len(shape) > 1:
            dims = " ".join(f"a{i}" for i in range(len(shape)))
            kw = {f"a{i}": s for i, s in enumerate(shape)}
            ap = ap.rearrange(f"p ({dims}) -> p {dims}", **kw)
        return ap


WBARG, WBARU, WBARD = 0, 1, 2
SCALEG, SCALEU, SCALED = 3, 4, 5
SLOPE = 6
RTMP = 8
RSTAT = 32
RBQ = 128
RGAM = 144
RBET = 160
RASUM = 224
RIOTA = 352


def build(nbatch, seqlen, upto=6):
    bpc = nbatch // NCORES
    TOK = bpc * seqlen
    spc = seqlen // NCORES
    TB = 256
    NBLK = TOK // TB
    NT = TB // 128
    assert TOK % TB == 0 and seqlen % NCORES == 0

    nc = bass.Bass("TRN2", target_bir_lowering=False, debug=False)
    bb = Bld(nc)

    io = {}

    def inp(name, shape, dt=F32):
        io[name] = nc.dram_tensor(name, shape, dt, kind="ExternalInput").ap()
    inp("x_in", [TOK, D])
    inp("w_qk", [D, D])
    inp("w_v", [D, D])
    inp("alibi", [1])
    inp("w0g", [H0, D])
    inp("w0u", [H0, D])
    inp("w0d", [D, H0])
    inp("wqkv", [3 * D, D])
    inp("bqkv", [3 * D])
    inp("gamma", [NHEAD])
    inp("beta", [NHEAD])
    inp("w1g", [H1, D], I32)
    inp("w1u", [H1, D], I32)
    inp("w1d", [D, H1], I32)
    io["y_out"] = nc.dram_tensor("y_out", [TOK, D], F32, kind="ExternalOutput").ap()

    sc = {}

    def scratch(name, shape, dt=BF16):
        sc[name] = nc.dram_tensor(name, shape, dt, kind="Internal").ap()
    scratch("w0gT", [HC0 // 4, 128, 4, DC, 128])
    scratch("w0uT", [HC0 // 4, 128, 4, DC, 128])
    scratch("w0dT", [HC0 // 4, 128, 4, D])
    scratch("wvT", [DC // 4, 128, 4, D])
    scratch("wqkb", [DC, 128, DC, 128])
    scratch("wqT", [DC // 4, 128, 4, DC, 128])
    scratch("wkT", [DC // 4, 128, 4, DC, 128])
    scratch("wvT1", [DC // 4, 128, 4, D])
    scratch("w1gT", [HC1 // 4, 128, 4, DC, 128])
    scratch("w1uT", [HC1 // 4, 128, 4, DC, 128])
    scratch("w1dT", [HC1 // 4, 128, 4, D])
    scratch("s_a", [TOK, D], F32)
    scratch("s_b", [TOK, D], F32)
    scratch("cc_src", [TOK, D], F32)
    half = TOK // 2
    scratch("cc_d0", [NCORES * half, D], F32)
    scratch("cc_d1", [NCORES * half, D], F32)
    scratch("s_g", [TOK, D], F32)
    scratch("s_d", [TOK, D], F32)
    scratch("s_e", [TOK, D], F32)

    with contextlib.ExitStack() as stack:
        F32A = stack.enter_context(nc.sbuf_tensor("F32A", [128, 14336], F32))
        BF16A = stack.enter_context(nc.sbuf_tensor("BF16A", [128, 69632], BF16))
        ROWS = stack.enter_context(nc.sbuf_tensor("ROWS", [128, 640], F32))
        ONESF = stack.enter_context(nc.sbuf_tensor("ONESF", [128, 128], F32))
        IDB = stack.enter_context(nc.sbuf_tensor("IDB", [128, 128], BF16))
        IDF = stack.enter_context(nc.sbuf_tensor("IDF", [128, 128], F32))
        PSBIG = stack.enter_context(nc.psum_tensor("PSBIG", [128, 3584], F32))
        PTB = stack.enter_context(nc.psum_tensor("PTB", [128, 1024], BF16))

        ctx = dict(nc=nc, bb=bb, io=io, sc=sc, F32A=F32A, BF16A=BF16A, ROWS=ROWS, ONESF=ONESF,
                   IDB=IDB, IDF=IDF, PSBIG=PSBIG, PTB=PTB,
                   TOK=TOK, TB=TB, NBLK=NBLK, NT=NT, bpc=bpc, seqlen=seqlen,
                   spc=spc, nbatch=nbatch, half=half, upto=upto)

        plan_consts(ctx)
        plan_prologue(ctx)
        plan_phases(ctx)

        for n in sorted(bb.cnt.keys()):
            bb.sems[n] = stack.enter_context(nc.semaphore(n))

        blk = stack.enter_context(nc.Block())
        for eng in Bld.ENGINES:
            def runner(e, eng=eng):
                for f in bb.plans[eng]:
                    f(e)
            getattr(blk, eng)(runner)

    return nc


def bank(ctx, i, w=512):
    return ctx["PSBIG"][:, i * 512:i * 512 + w]


def plan_consts(ctx):
    bb, ROWS, IDB, IDF = ctx["bb"], ctx["ROWS"], ctx["IDB"], ctx["IDF"]
    ONESF = ctx["ONESF"]
    t_iota = bb.mark("sg", "iota_id")

    @bb.on("gpsimd")
    def _(g):
        bb.i(g.iota(ROWS[:, RIOTA:RIOTA + 128], pattern=[[1, 128]], base=0,
                    channel_multiplier=-1, allow_small_or_imprecise_dtypes=True), t_iota)
    t_id = bb.mark("sv", "consts")

    @bb.on("vector")
    def _(v):
        bb.w(v, bb.ev("iota_id"))
        v.tensor_single_scalar(out=IDB[:], in_=ROWS[:, RIOTA:RIOTA + 128], scalar=0.0, op=ALU.is_equal)
        v.tensor_single_scalar(out=IDF[:], in_=ROWS[:, RIOTA:RIOTA + 128], scalar=0.0, op=ALU.is_equal)
        v.memset(ONESF[:], 1.0)
        bb.i(v.engine_nop(), t_id)




def plan_prologue(ctx):
    bb, io, sc = ctx["bb"], ctx["io"], ctx["sc"]
    ROWS, PTB, IDB = ctx["ROWS"], ctx["PTB"], ctx["IDB"]
    upto = ctx["upto"]
    fa = Arena(ctx["F32A"], 14336)
    ba = Arena(ctx["BF16A"], 69632)
    wld = fa.take(2, 2048)
    tern = ba.take(2, 2048)
    asm = ba.take(2, 1024)
    resid = ba.take(8, 4096)

    seq = [0]
    aseq = [0]

    def asm_slot():
        k = aseq[0]
        aseq[0] += 1
        slot = k % 2
        prev_store = bb.get(f"pasmst_s{slot}_{k // 2 - 1}")
        return slot, k, prev_store

    def load_tile(src_ap, width, extra_wait=None, cast=False):
        i = seq[0]
        seq[0] += 1
        par = i % 2
        eng = "gpsimd" if cast else "sync"
        freev = bb.get(f"pwfree{i - 2}")

        if hasattr(src_ap, "shape") and len(src_ap.shape) == 3:
            extra_toks = [bb.mark(f"d_pg{par}" if cast else f"d_pw{par}")
                          for _ in range(src_ap.shape[0] - 1)]
            tok = bb.mark(f"d_pg{par}" if cast else f"d_pw{par}", f"pwld{i}")
        else:
            extra_toks = []
            tok = bb.mark(f"d_pg{par}" if cast else f"d_pw{par}", f"pwld{i}")

        @bb.on(eng)
        def _(s, src_ap=src_ap, par=par, width=width, tok=tok, freev=freev,
              extra_wait=extra_wait, extra_toks=extra_toks):
            bb.w(s, freev, extra_wait)
            if len(src_ap.shape) == 3:
                ng = src_ap.shape[0]
                rp = 128 // ng
                for h in range(ng - 1):
                    bb.i(s.dma_start(out=wld[h * rp:(h + 1) * rp, par, 0:width], in_=src_ap[h]),
                         extra_toks[h])
                bb.i(s.dma_start(out=wld[(ng - 1) * rp:128, par, 0:width], in_=src_ap[ng - 1]), tok)
            else:
                bb.i(s.dma_start(out=wld[:, par, 0:width], in_=src_ap), tok)
        return par, f"pwld{i}", i

    def q_tern(v, par, width, scale_col):
        r = ROWS
        v.tensor_scalar(out=wld[:, par, 0:width], in0=wld[:, par, 0:width],
                        scalar1=r[:, scale_col:scale_col + 1], scalar2=MAGIC, op0=ALU.mult, op1=ALU.add)
        v.drain()
        v.tensor_scalar(out=wld[:, par, 0:width], in0=wld[:, par, 0:width],
                        scalar1=MAGIC, scalar2=1.0, op0=ALU.subtract, op1=ALU.min)
        v.drain()

    for name, src, W, ntile, c0 in (("wg", io["w0g"], D, HC0, 0), ("wu", io["w0u"], D, HC0, 32)):
        for i in range(ntile):
            par, ev, si = load_tile(src[i * 128:(i + 1) * 128, :], W)
            tok = bb.mark("sv", f"pwfree{si}")

            @bb.on("vector")
            def _(v, par=par, W=W, c=c0 + i, ev=ev, tok=tok):
                bb.w(v, bb.ev(ev))
                bb.i(v.reduce_sum(out=ROWS[:, RASUM + c:RASUM + c + 1], in_=wld[:, par, 0:W],
                                  axis=AX, apply_absolute_value=True), tok)
    for i in range(8):
        for hh in range(2):
            par, ev, si = load_tile(io["w0d"][i * 128:(i + 1) * 128, hh * 2048:(hh + 1) * 2048], 2048)
            tok = bb.mark("sv", f"pwfree{si}")

            @bb.on("vector")
            def _(v, par=par, c=64 + i * 2 + hh, ev=ev, tok=tok):
                bb.w(v, bb.ev(ev))
                bb.i(v.reduce_sum(out=ROWS[:, RASUM + c:RASUM + c + 1], in_=wld[:, par, 0:2048],
                                  axis=AX, apply_absolute_value=True), tok)

    sums_done = bb.total("sv")
    t1 = bb.mark("sv", "psum1")

    @bb.on("vector")
    def _(v):
        bb.w(v, sums_done)
        v.drain()
        r = ROWS
        v.reduce_sum(out=r[:, RTMP:RTMP + 1], in_=r[:, RASUM:RASUM + 32], axis=AX)
        v.reduce_sum(out=r[:, RTMP + 1:RTMP + 2], in_=r[:, RASUM + 32:RASUM + 64], axis=AX)
        v.reduce_sum(out=r[:, RTMP + 2:RTMP + 3], in_=r[:, RASUM + 64:RASUM + 80], axis=AX)
        bb.i(v.engine_nop(), t1)
    ONESF = ctx["ONESF"]
    PSB = ctx["PSBIG"]
    t2a = bb.mark("sq", "psum2a")

    @bb.on("tensor")
    def _(t):
        bb.w(t, bb.ev("psum1"), bb.ev("consts"))
        bb.i(t.matmul(out=PSB[:, 3072:3072 + 3][0:1], lhsT=ONESF[:, 0:1], rhs=ROWS[:, RTMP:RTMP + 3],
                      start=True, stop=True), t2a)
    t2b = bb.mark("sv", "psum2b")

    @bb.on("vector")
    def _(v):
        bb.w(v, bb.ev("psum2a"))
        bb.i(v.tensor_copy(out=ROWS[0:1, RTMP + 16:RTMP + 19], in_=PSB[0:1, 3072:3075]), t2b)
    t2c = bb.mark("sq", "psum2c")

    @bb.on("tensor")
    def _(t):
        bb.w(t, bb.ev("psum2b"))
        bb.i(t.matmul(out=PSB[:, 3072:3072 + 3], lhsT=ONESF[0:1, 0:128], rhs=ROWS[0:1, RTMP + 16:RTMP + 19],
                      start=True, stop=True), t2c)
    t2 = bb.mark("sv", "psum2")

    @bb.on("vector")
    def _(v):
        bb.w(v, bb.ev("psum2c"))
        bb.i(v.tensor_copy(out=ROWS[:, RTMP + 4:RTMP + 7], in_=PSB[:, 3072:3075]), t2)
    t3 = bb.mark("sv", "wscales")

    @bb.on("vector")
    def _(v):
        bb.w(v, bb.ev("psum2"))
        v.drain()
        r = ROWS
        inv_n = 1.0 / (H0 * D)
        for wb, src in ((WBARG, RTMP + 4), (WBARU, RTMP + 5), (WBARD, RTMP + 6)):
            v.tensor_scalar(out=r[:, wb:wb + 1], in0=r[:, src:src + 1], scalar1=inv_n,
                            scalar2=EPS, op0=ALU.mult, op1=ALU.max)
        v.drain()
        for wb, scl in ((WBARG, SCALEG), (WBARU, SCALEU), (WBARD, SCALED)):
            v.reciprocal(out=r[:, scl:scl + 1], in_=r[:, wb:wb + 1])
        v.drain()
        bb.i(v.engine_nop(), t3)
    wsc = bb.ev("wscales")

    def stationary(src_fn, dstT, ntile, mode, scale_col=None, cast=False):
        for i in range(ntile):
            par, ev, si = load_tile(src_fn(i), D, extra_wait=(wsc if mode == "tern" else None), cast=cast)
            tq = bb.mark("sv", f"pwfree{si}")

            @bb.on("vector")
            def _(v, par=par, ev=ev, tq=tq, mode=mode, scale_col=scale_col,
                  ternfree=bb.get(f"ternlast{par}")):
                bb.w(v, bb.ev(ev), ternfree)
                if mode == "tern":
                    q_tern(v, par, D, scale_col)
                    bb.i(v.tensor_scalar(out=tern[:, par, 0:D], in0=wld[:, par, 0:D],
                                         scalar1=-1.0, scalar2=None, op0=ALU.max), tq)
                else:
                    bb.i(v.tensor_copy(out=tern[:, par, 0:D], in_=wld[:, par, 0:D]), tq)
            tt = bb.mark("sq")
            bb.events[f"ternlast{par}"] = tt

            @bb.on("tensor")
            def _(t, par=par, tq=tq, tt=tt, prev_evac=bb.get("pasm_prev")):
                bb.w(t, tq, bb.ev("consts"), prev_evac)
                for j in range(DC):
                    mm = t.transpose(out=PTB[:, j * 128:(j + 1) * 128],
                                     in_=tern[:, par, j * 128:(j + 1) * 128], identity=IDB[:])
                bb.i(mm, tt)
            apar, ak, prev_store = asm_slot()
            te = bb.mark("sv")
            bb.events["pasm_prev"] = te

            @bb.on("vector")
            def _(v, tt=tt, te=te, apar=apar, stfree=prev_store):
                bb.w(v, tt, stfree)
                bb.i(v.tensor_copy(out=asm[:, apar, :], in_=PTB[:]), te)
            td = bb.mark(f"d_pst{apar}", f"pasmst_s{apar}_{ak // 2}")

            @bb.on("sync")
            def _(s, i=i, dstT=dstT, te=te, td=td, apar=apar):
                bb.w(s, te)
                bb.i(s.dma_start(out=dstT[i // 4, :, i % 4], in_=asm[:, apar, :].rearrange("p (dc b) -> p dc b", dc=DC)), td)

    def moving(src_ap, dstT, n_dtile, W, n_oc, mode, scale_col=None, cast=False, src_fn=None):
        for i in range(n_dtile):
            nh = (W + 2047) // 2048
            for hh in range(nh):
                w0 = hh * 2048
                ww = min(W, w0 + 2048) - w0
                if src_fn is not None:
                    sap = src_fn(i)
                    assert ww == W
                else:
                    sap = src_ap[i * 128:(i + 1) * 128, w0:w0 + ww]
                par, ev, si = load_tile(sap, ww,
                                        extra_wait=(wsc if mode == "tern" else None), cast=cast)
                tq = bb.mark("sv", f"pwfree{si}")

                @bb.on("vector")
                def _(v, par=par, ev=ev, tq=tq, i=i, w0=w0, ww=ww, mode=mode, scale_col=scale_col):
                    bb.w(v, bb.ev(ev))
                    if mode == "tern":
                        q_tern(v, par, ww, scale_col)
                        bb.i(v.tensor_scalar(out=resid[:, i, w0:w0 + ww], in0=wld[:, par, 0:ww],
                                             scalar1=-1.0, scalar2=None, op0=ALU.max), tq)
                    else:
                        bb.i(v.tensor_copy(out=resid[:, i, w0:w0 + ww], in_=wld[:, par, 0:ww]), tq)
        qdone = bb.total("sv")
        for oc in range(n_oc):
            tt = bb.mark("sq")

            @bb.on("tensor")
            def _(t, oc=oc, tt=tt, prev_evac=bb.get("pasm_prev"), qdone=qdone, n_dtile=n_dtile):
                bb.w(t, qdone, bb.ev("consts"), prev_evac)
                for i in range(n_dtile):
                    mm = t.transpose(out=PTB[:, i * 128:(i + 1) * 128],
                                     in_=resid[:, i, oc * 128:(oc + 1) * 128], identity=IDB[:])
                bb.i(mm, tt)
            apar, ak, prev_store = asm_slot()
            te = bb.mark("sv")
            bb.events["pasm_prev"] = te

            @bb.on("vector")
            def _(v, tt=tt, te=te, apar=apar, n_dtile=n_dtile, stfree=prev_store):
                bb.w(v, tt, stfree)
                bb.i(v.tensor_copy(out=asm[:, apar, 0:n_dtile * 128], in_=PTB[:, 0:n_dtile * 128]), te)
            td = bb.mark(f"d_pst{apar}", f"pasmst_s{apar}_{ak // 2}")

            @bb.on("sync")
            def _(s, oc=oc, dstT=dstT, te=te, td=td, apar=apar, n_dtile=n_dtile):
                bb.w(s, te)
                bb.i(s.dma_start(out=dstT[oc // 4, :, oc % 4, 0:n_dtile * 128], in_=asm[:, apar, 0:n_dtile * 128]), td)

    stationary(lambda i: io["w0g"][i * 128:(i + 1) * 128, :], sc["w0gT"], HC0, "tern", SCALEG)
    stationary(lambda i: io["w0u"][i * 128:(i + 1) * 128, :], sc["w0uT"], HC0, "tern", SCALEU)
    moving(io["w0d"], sc["w0dT"], DC, H0, HC0, "tern", SCALED)

    if upto >= 2:
        for i in range(DC):
            par, ev, si = load_tile(io["w_qk"][i * 128:(i + 1) * 128, :], D)
            tq = bb.mark("sv", f"pwfree{si}")

            @bb.on("vector")
            def _(v, par=par, ev=ev, tq=tq, ternfree=bb.get(f"ternlast{par}")):
                bb.w(v, bb.ev(ev), ternfree)
                bb.i(v.tensor_copy(out=tern[:, par, 0:D], in_=wld[:, par, 0:D]), tq)
            td = bb.mark(f"d_pqk{si % 2}", f"pqkst{si}")
            bb.events[f"ternlast{par}"] = td

            @bb.on("sync")
            def _(s, i=i, par=par, tq=tq, td=td):
                bb.w(s, tq)
                bb.i(s.dma_start(out=sc["wqkb"].rearrange("jc a dc b -> dc a jc b")[i],
                                 in_=tern[:, par, 0:D].rearrange("p (jc b) -> p jc b", jc=DC)), td)
        moving(io["w_v"], sc["wvT"], DC, D, DC, "cast")
        tda = bb.mark("d_pb", "alibi_ld")

        @bb.on("sync")
        def _(s):
            bb.i(s.dma_start(out=ROWS[0:1, SLOPE:SLOPE + 1], in_=io["alibi"][:, None]), tda)
        tga1 = bb.mark("sq", "alibi_mm")

        @bb.on("tensor")
        def _(t):
            bb.w(t, bb.ev("alibi_ld"), bb.ev("consts"))
            bb.i(t.matmul(out=ctx["PSBIG"][:, 3072:3073], lhsT=ONESF[0:1, 0:128],
                          rhs=ROWS[0:1, SLOPE:SLOPE + 1], start=True, stop=True), tga1)
        tga = bb.mark("sv", "alibi_bc")

        @bb.on("vector")
        def _(v):
            bb.w(v, bb.ev("alibi_mm"))
            bb.i(v.tensor_copy(out=ROWS[:, SLOPE:SLOPE + 1], in_=ctx["PSBIG"][:, 3072:3073]), tga)

    if upto >= 4:
        stationary(lambda i: io["w1g"][i * 128:(i + 1) * 128, :], sc["w1gT"], HC1, "cast", cast=True)
        stationary(lambda i: io["w1u"][i * 128:(i + 1) * 128, :], sc["w1uT"], HC1, "cast", cast=True)
        moving(io["w1d"], sc["w1dT"], DC, H1, HC1, "cast", cast=True)

    _esel = 3
    if upto >= 5 or upto in (45, 50):
        def qkv_rows(kind, i):
            off = {"q": 0, "k": 1, "v": 2}[kind]
            return (io["wqkv"].rearrange("(h s r) d -> h s r d", h=NHEAD, s=3, r=64)
                    [2 * i:2 * i + 2, off])
        if _esel & 1:
            stationary(lambda i: qkv_rows("q", i), sc["wqT"], DC, "cast")
            stationary(lambda i: qkv_rows("k", i), sc["wkT"], DC, "cast")
            vsrc = io["wqkv"].rearrange("(h s r) d -> h s r d", h=NHEAD, s=3, r=64)[:, 2]
            moving(None, sc["wvT1"], DC, D, DC, "cast", src_fn=lambda i: vsrc[2 * i:2 * i + 2])
        for kind, cbase, off in (("q", 0, 0), ("k", 8, 1)) if _esel & 2 else ():
            for jc in range(DC):
                td0 = bb.mark("d_pb")
                td = bb.mark("d_pb", f"bq_{kind}{jc}")

                @bb.on("sync")
                def _(s, jc=jc, off=off, cbase=cbase, td0=td0, td=td):
                    srca = (io["bqkv"].rearrange("(h s r) -> h s r", h=NHEAD, s=3, r=64)
                            [2 * jc:2 * jc + 2, off])
                    bb.i(s.dma_start(out=ROWS[0:64, RBQ + cbase + jc:RBQ + cbase + jc + 1],
                                     in_=srca[0][:, None]), td0)
                    bb.i(s.dma_start(out=ROWS[64:128, RBQ + cbase + jc:RBQ + cbase + jc + 1],
                                     in_=srca[1][:, None]), td)
        if not (_esel & 2):
            ctx["prologue_sv"] = bb.total("sv")
            ctx["prologue_st"] = bb.total("d_pst0")
            ctx["prologue_st1"] = bb.total("d_pst1")
            ctx["prologue_fin"] = [bb.total(n) for n in
                                   ("d_pst0", "d_pst1", "d_pqk0", "d_pqk1", "d_pb") if bb.cnt.get(n)]
            _prol_quiet(ctx)
            return
        tg0 = bb.mark("d_pb")
        tg1 = bb.mark("d_pb", "gb_ld")

        @bb.on("sync")
        def _(s):
            bb.i(s.dma_start(out=ROWS[0:1, RGAM:RGAM + NHEAD], in_=io["gamma"][None, :]), tg0)
            bb.i(s.dma_start(out=ROWS[0:1, RBET:RBET + NHEAD], in_=io["beta"][None, :]), tg1)
        tg2a = bb.mark("sq", "gb_mm")

        @bb.on("tensor")
        def _(t):
            bb.w(t, bb.ev("gb_ld"), bb.ev("consts"))
            bb.i(t.matmul(out=ctx["PSBIG"][:, 3072:3072 + 2 * NHEAD], lhsT=ONESF[0:1, 0:128],
                          rhs=ROWS[0:1, RGAM:RGAM + 2 * NHEAD], start=True, stop=True), tg2a)
        tg2 = bb.mark("sv", "gb_bc")

        @bb.on("vector")
        def _(v):
            bb.w(v, bb.ev("gb_mm"))
            bb.i(v.tensor_copy(out=ROWS[:, RGAM:RGAM + 2 * NHEAD], in_=ctx["PSBIG"][:, 3072:3072 + 2 * NHEAD]), tg2)

    ctx["prologue_sv"] = bb.total("sv")
    ctx["prologue_st"] = bb.total("d_pst0")
    ctx["prologue_st1"] = bb.total("d_pst1")
    ctx["prologue_fin"] = [bb.total(n) for n in
                           ("d_pst0", "d_pst1", "d_pqk0", "d_pqk1", "d_pb") if bb.cnt.get(n)]
    _prol_quiet(ctx)


def _prol_quiet(ctx):
    bb = ctx["bb"]
    pq_v = bb.mark("sv", "prolq_v")

    @bb.on("vector")
    def _(v, pq_v=pq_v):
        bb.i(v.engine_nop(), pq_v)
    pq_a = bb.mark("sa", "prolq_a")

    @bb.on("scalar")
    def _(a, pq_a=pq_a):
        bb.i(a.nop(), pq_a)
    ctx["psum_quiet"] = [pq_v, pq_a]



def plan_ffn0(ctx, tag, src, dst):
    bb, sc = ctx["bb"], ctx["sc"]
    ROWS, PTB, IDB, IDF, PSBIG = ctx["ROWS"], ctx["PTB"], ctx["IDB"], ctx["IDF"], ctx["PSBIG"]
    NBLK, TB, NT = ctx["NBLK"], ctx["TB"], ctx["NT"]

    fa = Arena(ctx["F32A"], 14336)
    xblk = fa.take(NT, D)
    prod = fa.take(HC0, TB)
    gbuf = fa.take(2, TB)
    obuf = fa.take(2, D)
    m2c = fa.take(TB)
    r2 = fa.take(TB)
    ba = Arena(ctx["BF16A"], 69632)
    prodflat = prod.rearrange("p h tb -> p (h tb)")
    xcbuf = prodflat[:, 4 * D:5 * D]
    xq_tm = ba.take(NT, D)
    xq_fm = ba.take(DC, TB)
    xq2 = ba.take(HC0, TB)
    wgs = ba.take(2, 4, DC, 128)
    wus = ba.take(2, 4, DC, 128)
    wds = ba.take(2, 4, D)

    CS = RSTAT
    WG, WU, WD = sc["w0gT"], sc["w0uT"], sc["w0dT"]
    NGB = HC0 // 4
    NDG = HC0 // 4

    prol_sv = ctx.get("prologue_sv")
    prol_st = ctx.get("prologue_st")

    for b in range(NBLK):
        xl = bb.mark(f"d_x{tag}", f"{tag}xl{b}")

        @bb.on("sync")
        def _(s, b=b, xl=xl, freev=bb.get(f"{tag}xfree{b - 1}"),
              prol=ctx.get("prologue_fin") if b == 0 else None,
              pprev=ctx.get(f"{tag}_prev_done") if b == 0 else None):
            bb.w(s, freev, prol, pprev)
            bb.i(s.dma_start(out=xblk,
                             in_=src[b * TB:(b + 1) * TB, :].rearrange("(a p) d -> p a d", p=128)), xl)

        for g in range(NGB):
            gi = b * NGB + g
            par = gi % 2
            wl0 = bb.mark(f"d_w{tag}{par}")
            wl = bb.mark(f"d_w{tag}{par}", f"{tag}wl{gi}")

            @bb.on("sync")
            def _(s, g=g, par=par, wl0=wl0, wl=wl, fname=f"{tag}wfree{gi - 2}",
                  pprev=ctx.get(f"{tag}_prev_done") if gi < 2 else None):
                bb.wn(s, fname)
                bb.w(s, pprev)
                bb.i(s.dma_start(out=wgs[:, par], in_=WG[g]), wl0)
                bb.i(s.dma_start(out=wus[:, par], in_=WU[g]), wl)

        for tp in range(NT // 2):
            for g in range(NDG):
                di = (b * (NT // 2) + tp) * NDG + g
                par = di % 2
                dl = bb.mark(f"d_d{tag}{par}", f"{tag}dl{di}")

                @bb.on("sync")
                def _(s, g=g, par=par, dl=dl, fname=f"{tag}dfree{di - 2}",
                      pprev=ctx.get(f"{tag}_prev_done") if di < 2 else None):
                    bb.wn(s, fname)
                    bb.w(s, pprev)
                    bb.i(s.dma_start(out=wds[:, par], in_=WD[g]), dl)

        for t in range(NT):
            st = bb.mark("sv", f"{tag}sum{b}_{t}")

            @bb.on("vector")
            def _(v, t=t, xl=xl, st=st):
                bb.w(v, xl)
                bb.i(v.reduce_sum(out=ROWS[:, CS + t:CS + t + 1], in_=xblk[:, t], axis=AX), st)
            sq = bb.mark("sa", f"{tag}sq{b}_{t}")

            @bb.on("scalar")
            def _(a, t=t, xl=xl, sq=sq, vqprev=bb.get(f"{tag}qq{b - 1}_1") if t == 0 else None,
                  prol=prol_sv if (b == 0 and t == 0) else None):
                bb.w(a, xl, vqprev, prol)
                bb.i(a.activation(out=prodflat[:, t * D:(t + 1) * D], in_=xblk[:, t], func=AF.Square,
                                  accum_out=ROWS[:, CS + 4 + t:CS + 5 + t]), sq)
        r1 = bb.mark("sv", f"{tag}rows1_{b}")

        @bb.on("vector")
        def _(v, r1=r1, sa_need=bb.ev(f"{tag}sq{b}_{NT - 1}")):
            bb.w(v, sa_need)
            v.drain()
            r = ROWS
            c = CS
            v.tensor_scalar_mul(out=r[:, c:c + NT], in0=r[:, c:c + NT], scalar1=1.0 / D)
            v.drain()
            v.tensor_scalar_mul(out=r[:, c + 12:c + 12 + NT], in0=r[:, c:c + NT], scalar1=-1.0)
            v.tensor_mul(out=r[:, c + 16:c + 16 + NT], in0=r[:, c:c + NT], in1=r[:, c:c + NT])
            v.drain()
            v.scalar_tensor_tensor(out=r[:, c + 16:c + 16 + NT], in0=r[:, c + 4:c + 4 + NT], scalar=1.0 / D,
                                   in1=r[:, c + 16:c + 16 + NT], op0=ALU.mult, op1=ALU.subtract)
            v.drain()
            v.tensor_scalar_add(out=r[:, c + 16:c + 16 + NT], in0=r[:, c + 16:c + 16 + NT], scalar1=EPS)
            v.drain()
            bb.i(v.engine_nop(), r1)
        r2e = bb.mark("sa", f"{tag}rows2_{b}")

        @bb.on("scalar")
        def _(a, r1=r1, r2e=r2e):
            bb.w(a, r1)
            bb.i(a.activation(out=ROWS[:, CS + 16:CS + 16 + NT], in_=ROWS[:, CS + 16:CS + 16 + NT],
                              func=AF.Sqrt), r2e)
        r3 = bb.mark("sv", f"{tag}rows3_{b}")

        @bb.on("vector")
        def _(v, r2e=r2e, r3=r3):
            bb.w(v, r2e)
            v.reciprocal(out=ROWS[:, CS + 16:CS + 16 + NT], in_=ROWS[:, CS + 16:CS + 16 + NT])
            v.drain()
            bb.i(v.engine_nop(), r3)

        for t in range(NT):
            xc = bb.mark("sa", f"{tag}xc{b}_{t}")

            @bb.on("scalar")
            def _(a, t=t, xc=xc, r1=bb.ev(f"{tag}rows1_{b}"), qzprev=bb.get(f"{tag}qz{b}_{t - 1}")):
                bb.w(a, r1, qzprev)
                bb.i(a.activation(out=xcbuf[:], in_=xblk[:, t], func=AF.Identity,
                                  bias=ROWS[:, CS + 12 + t:CS + 13 + t], scale=1.0), xc)
            mm_ = bb.mark("sv", f"{tag}mx{b}_{t}")

            @bb.on("vector")
            def _(v, t=t, xc=xc, mm_=mm_):
                bb.w(v, xc)
                bb.i(v.reduce_max(out=ROWS[:, CS + 20 + t:CS + 21 + t], in_=xcbuf[:], axis=AX,
                                  apply_absolute_value=True), mm_)
            qz = bb.mark("sv", f"{tag}qz{b}_{t}")

            @bb.on("vector")
            def _(v, t=t, qz=qz, r3=bb.ev(f"{tag}rows3_{b}"), trprev=bb.get(f"{tag}tr{b - 1}_{t}")):
                bb.w(v, r3, trprev)
                v.drain()
                r = ROWS
                c = CS
                v.tensor_mul(out=r[:, c + 24 + t:c + 25 + t], in0=r[:, c + 16 + t:c + 17 + t],
                             in1=r[:, c + 20 + t:c + 21 + t])
                v.drain()
                v.tensor_scalar_max(out=r[:, c + 24 + t:c + 25 + t], in0=r[:, c + 24 + t:c + 25 + t],
                                    scalar1=EPS)
                v.drain()
                v.tensor_scalar_mul(out=r[:, c + 28 + t:c + 29 + t], in0=r[:, c + 24 + t:c + 25 + t],
                                    scalar1=1.0 / 127.0)
                v.reciprocal(out=r[:, c + 24 + t:c + 25 + t], in_=r[:, c + 24 + t:c + 25 + t])
                v.drain()
                v.tensor_scalar_mul(out=r[:, c + 24 + t:c + 25 + t], in0=r[:, c + 24 + t:c + 25 + t],
                                    scalar1=127.0)
                v.drain()
                v.tensor_mul(out=r[:, c + 24 + t:c + 25 + t], in0=r[:, c + 24 + t:c + 25 + t],
                             in1=r[:, c + 16 + t:c + 17 + t])
                v.drain()
                v.tensor_mul(out=r[:, c + 8 + t:c + 9 + t], in0=r[:, c + 12 + t:c + 13 + t],
                             in1=r[:, c + 24 + t:c + 25 + t])
                v.drain()
                v.tensor_scalar(out=xcbuf[:], in0=xblk[:, t], scalar1=r[:, c + 24 + t:c + 25 + t],
                                scalar2=r[:, c + 8 + t:c + 9 + t], op0=ALU.mult, op1=ALU.add)
                v.drain()
                v.tensor_scalar(out=xcbuf[:], in0=xcbuf[:], scalar1=MAGIC, scalar2=MAGIC,
                                op0=ALU.add, op1=ALU.subtract)
                v.drain()
                bb.i(v.tensor_scalar_mul(out=xq_tm[:, t], in0=xcbuf[:],
                                         scalar1=r[:, c + 28 + t:c + 29 + t]), qz)
            tr = bb.mark("sq", f"{tag}tr{b}_{t}")

            @bb.on("tensor")
            def _(t_, t=t, qz=qz, tr=tr, prevev=bb.get(f"{tag}trev{b}_{t - 1}")
                  or bb.get(f"{tag}trev{b - 1}_{NT - 1}") or ctx.get("ptb_chain"),
                  pquiet=ctx.pop("psum_quiet", None)):
                bb.w(t_, qz, bb.ev("consts"), prevev, pquiet)
                for j in range(DC):
                    mm = t_.transpose(out=PTB[:, j * 128:(j + 1) * 128],
                                      in_=xq_tm[:, t, j * 128:(j + 1) * 128], identity=IDB[:])
                bb.i(mm, tr)
            tre = bb.mark("sa", f"{tag}trev{b}_{t}")

            @bb.on("scalar")
            def _(a, t=t, tr=tr, tre=tre, mmu_prev=bb.get(f"{tag}mmu{b - 1}_{HC0 - 1}") if t == 0 else None):
                bb.w(a, tr, mmu_prev)
                bb.i(a.copy(out=xq_fm[:, :, t * 128:(t + 1) * 128],
                            in_=PTB[:].rearrange("p (dc c) -> p dc c", dc=DC)), tre)

        xq_ready = bb.ev(f"{tag}trev{b}_{NT - 1}")

        for hc in range(HC0):
            g = hc // 4
            gi = b * NGB + g
            par = gi % 2
            sl = hc % 4
            pg, pu = (0, 1) if hc % 2 == 0 else (4, 5)
            mmg = bb.mark("sq", f"{tag}mmg{b}_{hc}")
            mmu = bb.mark("sq", f"{tag}mmu{b}_{hc}")

            @bb.on("tensor")
            def _(t_, hc=hc, par=par, sl=sl, pg=pg, pu=pu, mmg=mmg, mmu=mmu,
                  wl=bb.ev(f"{tag}wl{gi}"), xq_ready=xq_ready,
                  bankfree=bb.get(f"{tag}prd{b}_{hc - 2}"),
                  dnprev=bb.get(f"{tag}dnall{b - 1}") if hc == 0 else None):
                bb.w(t_, wl, xq_ready, bankfree, dnprev)
                for d in range(DC):
                    mm = t_.matmul(out=bank(ctx, pg, TB), lhsT=wgs[:, par, sl, d], rhs=xq_fm[:, d],
                                   start=(d == 0), stop=(d == DC - 1))
                bb.i(mm, mmg)
                for d in range(DC):
                    mm = t_.matmul(out=bank(ctx, pu, TB), lhsT=wus[:, par, sl, d], rhs=xq_fm[:, d],
                                   start=(d == 0), stop=(d == DC - 1))
                bb.i(mm, mmu)
            if hc % 4 == 3:
                bb.events[f"{tag}wfree{gi}"] = mmu
            sil = bb.mark("sa", f"{tag}sil{b}_{hc}")

            @bb.on("scalar")
            def _(a, hc=hc, pg=pg, sil=sil, mmg=mmg, gfree=bb.get(f"{tag}prd{b}_{hc - 2}")):
                bb.w(a, mmg, gfree)
                bb.i(a.activation(out=gbuf[:, hc % 2], in_=bank(ctx, pg, TB), func=AF.Silu,
                                  scale=ROWS[:, WBARG:WBARG + 1]), sil)
            prd = bb.mark("sv", f"{tag}prd{b}_{hc}")

            @bb.on("vector")
            def _(v, hc=hc, pu=pu, sil=sil, mmu=mmu, prd=prd):
                bb.w(v, sil, mmu)
                bb.i(v.scalar_tensor_tensor(out=prod[:, hc], in0=bank(ctx, pu, TB),
                                            scalar=ROWS[:, WBARU:WBARU + 1], in1=gbuf[:, hc % 2],
                                            op0=ALU.mult, op1=ALU.mult), prd)

        sm = bb.mark("sv", f"{tag}sm{b}")

        @bb.on("vector")
        def _(v, sm=sm):
            v.drain()
            bb.i(v.reduce_max(out=m2c[:], in_=prod.rearrange("p h tb -> p tb h"), axis=AX,
                              apply_absolute_value=True), sm)
        a2t = bb.mark("sq", f"{tag}a2t{b}")

        @bb.on("tensor")
        def _(t_, a2t=a2t, sm=sm, b6free=bb.get(f"{tag}r2e{b - 1}")):
            bb.w(t_, sm, bb.ev("consts"), b6free)
            for t in range(NT):
                mm = t_.transpose(out=PSBIG[:, 3072 + t * 128:3072 + (t + 1) * 128],
                                  in_=m2c[:, t * 128:(t + 1) * 128], identity=IDF[:])
            bb.i(mm, a2t)
        a2c = bb.mark("sv", f"{tag}a2c{b}")

        @bb.on("vector")
        def _(v, a2c=a2c, a2t=a2t):
            bb.w(v, a2t)
            r = ROWS
            c = CS
            for t in range(NT):
                v.reduce_max(out=r[:, c + 32 + t:c + 33 + t],
                             in_=PSBIG[:, 3072 + t * 128:3072 + (t + 1) * 128], axis=AX)
            v.drain()
            v.tensor_scalar_max(out=r[:, c + 32:c + 32 + NT], in0=r[:, c + 32:c + 32 + NT], scalar1=EPS)
            v.drain()
            for t in range(NT):
                v.tensor_mul(out=r[:, c + 36 + t:c + 37 + t], in0=r[:, c + 32 + t:c + 33 + t],
                             in1=r[:, WBARD:WBARD + 1])
            v.reciprocal(out=r[:, c + 40:c + 40 + NT], in_=r[:, c + 32:c + 32 + NT])
            v.drain()
            v.tensor_scalar_mul(out=r[:, c + 36:c + 36 + NT], in0=r[:, c + 36:c + 36 + NT], scalar1=0.5 / 127.0)
            v.tensor_scalar_mul(out=r[:, c + 40:c + 40 + NT], in0=r[:, c + 40:c + 40 + NT], scalar1=127.0)
            v.drain()
            bb.i(v.engine_nop(), a2c)
        r2t = bb.mark("sq", f"{tag}r2t{b}")

        @bb.on("tensor")
        def _(t_, r2t=r2t, a2c=a2c):
            bb.w(t_, a2c)
            bb.i(t_.transpose(out=PSBIG[:, 3072:3072 + 128],
                              in_=ROWS[:, CS + 40:CS + 40 + 128], identity=IDF[:]), r2t)
        r2s = bb.mark("sv", f"{tag}r2s{b}")

        @bb.on("vector")
        def _(v, r2s=r2s, r2t=r2t):
            bb.w(v, r2t)
            bb.i(v.tensor_copy(out=ROWS[0:NT, 480:608], in_=PSBIG[0:NT, 3072:3200]), r2s)
        r2d = bb.mark(f"d_r2{tag}", f"{tag}r2d{b}")

        @bb.on("scalar")
        def _(a, r2d=r2d, r2s=r2s):
            bb.w(a, r2s)
            bb.i(a.dma_start(out=r2[0:1, :].rearrange("p (a c) -> p a c", a=NT),
                             in_=ROWS[0:NT, 480:608][:, None, :]), r2d)
        r2b = bb.mark("sq", f"{tag}r2b{b}")

        @bb.on("tensor")
        def _(t_, r2b=r2b, r2d=r2d):
            bb.w(t_, r2d)
            bb.i(t_.matmul(out=PSBIG[:, 3072:3072 + TB], lhsT=ctx["ONESF"][0:1, 0:128],
                           rhs=r2[0:1, :], start=True, stop=True), r2b)
        rr = bb.mark("sv", f"{tag}r2e{b}")

        @bb.on("vector")
        def _(v, rr=rr, r2b=r2b):
            bb.w(v, r2b)
            bb.i(v.tensor_copy(out=r2[:], in_=PSBIG[:, 3072:3072 + TB]), rr)
        for hc in range(HC0):
            qq = bb.mark("sv", f"{tag}qq{b}_{hc}")

            @bb.on("vector")
            def _(v, hc=hc, qq=qq, dnprev=bb.get(f"{tag}dnall{b - 1}") if hc == 0 else None):
                bb.w(v, dnprev)
                if hc == 0:
                    v.drain()
                v.tensor_mul(out=prod[:, hc], in0=prod[:, hc], in1=r2[:])
                v.drain()
                bb.i(v.tensor_scalar(out=xq2[:, hc], in0=prod[:, hc], scalar1=MAGIC, scalar2=MAGIC,
                                     op0=ALU.add, op1=ALU.subtract), qq)

        qlast = bb.ev(f"{tag}qq{b}_{HC0 - 1}")
        for tp in range(NT // 2):
            t0, t1 = 2 * tp, 2 * tp + 1
            gtoks = [bb.mark("sq", f"{tag}dnw{(b * (NT // 2) + tp) * NDG + g}") for g in range(NDG - 1)]
            dn = bb.mark("sq", f"{tag}dn{b}_{tp}")
            for g in range(NDG - 1):
                bb.events[f"{tag}dfree{(b * (NT // 2) + tp) * NDG + g}"] = gtoks[g]
            bb.events[f"{tag}dfree{(b * (NT // 2) + tp) * NDG + (NDG - 1)}"] = dn

            @bb.on("tensor")
            def _(t_, tp=tp, t0=t0, t1=t1, dn=dn, qlast=qlast, b=b, gtoks=gtoks,
                  evprev=bb.get(f"{tag}oev{b}_{2 * tp - 1}") or bb.get(f"{tag}oev{b - 1}_{NT - 1}")):
                bb.w(t_, qlast, evprev)
                for g in range(NDG):
                    di = (b * (NT // 2) + tp) * NDG + g
                    par = di % 2
                    bb.w(t_, bb.ev(f"{tag}dl{di}"))
                    for sl in range(4):
                        hc = g * 4 + sl
                        first = hc == 0
                        last = hc == HC0 - 1
                        for ti, t in enumerate((t0, t1)):
                            for dh in range(2):
                                mm = t_.matmul(out=bank(ctx, 2 + ti * 2 + dh),
                                               lhsT=xq2[:, hc, t * 128:(t + 1) * 128],
                                               rhs=wds[:, par, sl, dh * 512:(dh + 1) * 512],
                                               start=first, stop=last)
                    if g < NDG - 1:
                        bb.i(mm, gtoks[g])
                bb.i(mm, dn)
            for ti, t in enumerate((t0, t1)):
                ev = bb.mark("sv", f"{tag}oev{b}_{t}")

                @bb.on("vector")
                def _(v, t=t, ti=ti, dn=dn, ev=ev, a2c=a2c,
                      stfree=bb.get(f"{tag}ost{b}_{t - 2}") or bb.get(f"{tag}ost{b - 1}_{t - 2 + NT}")):
                    bb.w(v, dn, a2c, stfree)
                    r = ROWS
                    c = CS
                    for dh in range(2):
                        ins = v.scalar_tensor_tensor(out=obuf[:, t % 2, dh * 512:(dh + 1) * 512],
                                                     in0=bank(ctx, 2 + ti * 2 + dh),
                                                     scalar=r[:, c + 36 + t:c + 37 + t],
                                                     in1=xblk[:, t, dh * 512:(dh + 1) * 512],
                                                     op0=ALU.mult, op1=ALU.add)
                    bb.i(ins, ev)
                st = bb.mark(f"d_o{tag}{t % 2}", f"{tag}ost{b}_{t}")

                @bb.on("sync")
                def _(s, t=t, b=b, ev=ev, st=st):
                    bb.w(s, ev)
                    bb.i(s.dma_start(out=dst[b * TB + t * 128:b * TB + (t + 1) * 128, :],
                                     in_=obuf[:, t % 2]), st)
        bb.events[f"{tag}dnall{b}"] = bb.ev(f"{tag}dn{b}_{NT // 2 - 1}")
        bb.events[f"{tag}xfree{b}"] = bb.ev(f"{tag}oev{b}_{NT - 1}")
        if b == NBLK // 2 - 1:
            ctx[f"{tag}_half_st"] = [bb.total(f"d_o{tag}0"), bb.total(f"d_o{tag}1")]

    ctx["ptb_chain"] = bb.ev(f"{tag}trev{NBLK - 1}_{NT - 1}")
    ctx[f"{tag}_fin"] = [bb.total(f"d_o{tag}0"), bb.total(f"d_o{tag}1")]



def plan_attn0(ctx, tag, src, dst):
    bb, sc, io = ctx["bb"], ctx["sc"], ctx["io"]
    ROWS, PTB, IDB, IDF, PSBIG = ctx["ROWS"], ctx["PTB"], ctx["IDB"], ctx["IDF"], ctx["PSBIG"]
    S = ctx["seqlen"]
    bpc = ctx["bpc"]
    NTL = S // 128
    fa = Arena(ctx["F32A"], 14336)
    alibuf = fa.take(2 * S)
    xbuf = fa.take(2, D)
    obuf = fa.take(2, D)
    dump = fa.take(2, D)
    ba = Arena(ctx["BF16A"], 69632)
    xln_fm = ba.take(DC, S)
    q_fm = ba.take(DC, S)
    v_tm = ba.take(NTL, D)
    pbuf = ba.take(S)
    xlns = ba.take(2, D)
    wv_sb = ba.take(DC, D)
    wqk_sb = ba.take(2, DC, 128)
    CS = RSTAT

    ab1 = bb.mark("sg", f"{tag}ab1")

    @bb.on("gpsimd")
    def _(g, pprev=ctx.get(f"{tag}_prev_done")):
        bb.w(g, pprev)
        bb.i(g.iota(alibuf[:], pattern=[[1, 2 * S]], base=-S, channel_multiplier=-1,
                    allow_small_or_imprecise_dtypes=True), ab1)
    ab2 = bb.mark("sv", f"{tag}ab2")

    @bb.on("vector")
    def _(v):
        bb.w(v, bb.ev(f"{tag}ab1"), bb.ev("alibi_bc"))
        v.drain()
        bb.i(v.tensor_scalar_mul(out=alibuf[:], in0=alibuf[:],
                                 scalar1=ROWS[:, SLOPE:SLOPE + 1]), ab2)
    ab3 = bb.mark("sg", f"{tag}ab3")

    @bb.on("gpsimd")
    def _(g):
        bb.w(g, bb.ev(f"{tag}ab2"))
        bb.i(g.affine_select(alibuf[:], alibuf[:], pattern=[[-1, 2 * S]], base=S,
                             channel_multiplier=1, compare_op=ALU.is_ge, fill=-1e30), ab3)
    wvl = bb.mark(f"d_wv{tag}", f"{tag}wvl")

    @bb.on("sync")
    def _(s, prol=ctx.get("prologue_fin"), pprev=ctx.get(f"{tag}_prev_done")):
        bb.w(s, prol, pprev)
        bb.i(s.dma_start(out=wv_sb.rearrange("p (g a) d -> p g a d", g=DC // 4), in_=sc["wvT"].rearrange("g p a d -> p g a d")), wvl)

    for sq_ in range(bpc):
        t0 = sq_ * S
        pfx = f"{tag}s{sq_}"

        for tt in range(NTL):
            xl = bb.mark(f"d_x{tag}{tt % 2}", f"{pfx}xl{tt}")

            @bb.on("sync")
            def _(s, tt=tt, xl=xl, t0=t0, fname=f"{pfx}lx{tt - 2}",
                  pprev=ctx.get(f"{tag}_prev_done") if (sq_ == 0 and tt < 2) else None,
                  sprev=bb.get(f"{tag}s{sq_ - 1}oev{NTL - 2 + tt}") if (sq_ > 0 and tt < 2) else None):
                bb.wn(s, fname)
                bb.w(s, pprev, sprev)
                bb.i(s.dma_start(out=xbuf[:, tt % 2], in_=src[t0 + tt * 128:t0 + (tt + 1) * 128, :]), xl)
            g4 = tt % 4
            st = bb.mark("sv", f"{pfx}sum{tt}")

            @bb.on("vector")
            def _(v, tt=tt, xl=xl, st=st, g4=g4):
                bb.w(v, xl)
                bb.i(v.reduce_sum(out=ROWS[:, CS + g4:CS + g4 + 1], in_=xbuf[:, tt % 2], axis=AX), st)
            sqv = bb.mark("sa", f"{pfx}sq{tt}")

            @bb.on("scalar")
            def _(a, tt=tt, xl=xl, sqv=sqv, g4=g4):
                bb.w(a, xl)
                bb.i(a.activation(out=dump[:, tt % 2], in_=xbuf[:, tt % 2], func=AF.Square,
                                  accum_out=ROWS[:, CS + 4 + g4:CS + 5 + g4]), sqv)
            rw = bb.mark("sv", f"{pfx}rw{tt}")

            @bb.on("vector")
            def _(v, rw=rw, sqv=sqv, g4=g4):
                bb.w(v, sqv)
                v.drain()
                r = ROWS
                c = CS
                v.tensor_scalar_mul(out=r[:, c + g4:c + g4 + 1], in0=r[:, c + g4:c + g4 + 1], scalar1=1.0 / D)
                v.drain()
                v.tensor_mul(out=r[:, c + 16 + g4:c + 17 + g4], in0=r[:, c + g4:c + g4 + 1],
                             in1=r[:, c + g4:c + g4 + 1])
                v.drain()
                v.scalar_tensor_tensor(out=r[:, c + 16 + g4:c + 17 + g4], in0=r[:, c + 4 + g4:c + 5 + g4],
                                       scalar=1.0 / D, in1=r[:, c + 16 + g4:c + 17 + g4],
                                       op0=ALU.mult, op1=ALU.subtract)
                v.drain()
                bb.i(v.tensor_scalar_add(out=r[:, c + 16 + g4:c + 17 + g4],
                                         in0=r[:, c + 16 + g4:c + 17 + g4], scalar1=EPS), rw)
            rq = bb.mark("sa", f"{pfx}rq{tt}")

            @bb.on("scalar")
            def _(a, rw=rw, rq=rq, g4=g4):
                bb.w(a, rw)
                bb.i(a.activation(out=ROWS[:, CS + 16 + g4:CS + 17 + g4],
                                  in_=ROWS[:, CS + 16 + g4:CS + 17 + g4], func=AF.Sqrt), rq)
            rv = bb.mark("sv", f"{pfx}rv{tt}")

            @bb.on("vector")
            def _(v, rq=rq, rv=rv, g4=g4):
                bb.w(v, rq)
                v.drain()
                r = ROWS
                c = CS
                v.reciprocal(out=r[:, c + 16 + g4:c + 17 + g4], in_=r[:, c + 16 + g4:c + 17 + g4])
                v.drain()
                v.tensor_mul(out=r[:, c + 8 + g4:c + 9 + g4], in0=r[:, c + g4:c + g4 + 1],
                             in1=r[:, c + 16 + g4:c + 17 + g4])
                v.drain()
                v.tensor_scalar_mul(out=r[:, c + 8 + g4:c + 9 + g4], in0=r[:, c + 8 + g4:c + 9 + g4],
                                    scalar1=-1.0)
                v.drain()
                bb.i(v.engine_nop(), rv)
            lx = bb.mark("sa", f"{pfx}lx{tt}")

            @bb.on("scalar")
            def _(a, tt=tt, rv=rv, lx=lx, g4=g4, trprev=bb.get(f"{pfx}tr{tt - 2}")):
                bb.w(a, rv, trprev)
                bb.i(a.activation(out=xlns[:, tt % 2], in_=xbuf[:, tt % 2], func=AF.Identity,
                                  bias=ROWS[:, CS + 8 + g4:CS + 9 + g4],
                                  scale=ROWS[:, CS + 16 + g4:CS + 17 + g4]), lx)
            tr = bb.mark("sq", f"{pfx}tr{tt}")

            @bb.on("tensor")
            def _(t_, tt=tt, lx=lx, tr=tr,
                  prevev=bb.get(f"{pfx}trev{tt - 1}") or bb.get(f"{tag}ptb_prev") or ctx.get("ptb_chain"),
                  pquiet=ctx.pop("psum_quiet", None)):
                bb.w(t_, lx, bb.ev("consts"), prevev, pquiet)
                for j in range(DC):
                    mm = t_.transpose(out=PTB[:, j * 128:(j + 1) * 128],
                                      in_=xlns[:, tt % 2, j * 128:(j + 1) * 128], identity=IDB[:])
                bb.i(mm, tr)
            tre = bb.mark("sa", f"{pfx}trev{tt}")

            @bb.on("scalar")
            def _(a, tt=tt, tr=tr, tre=tre):
                bb.w(a, tr)
                bb.i(a.copy(out=xln_fm[:, :, tt * 128:(tt + 1) * 128],
                            in_=PTB[:].rearrange("p (dc c) -> p dc c", dc=DC)), tre)
        bb.events[f"{tag}ptb_prev"] = bb.ev(f"{pfx}trev{NTL - 1}")
        xln_done = bb.ev(f"{pfx}trev{NTL - 1}")

        NG = max(1, S // 512)
        for jc in range(DC):
            ql = bb.mark(f"d_qk{tag}{jc % 2}", f"{pfx}qkl{jc}")

            @bb.on("sync")
            def _(s, jc=jc, ql=ql, fname=f"{pfx}qmm{jc - 2}_{NG - 1}", wvl=wvl,
                  first=(jc < 2 and sq_ == 0)):
                bb.wn(s, fname)
                if first:
                    bb.w(s, wvl)
                bb.i(s.dma_start(out=wqk_sb[:, jc % 2], in_=sc["wqkb"][jc]), ql)
            for grp in range(NG):
                w = min(512, S - grp * 512)
                qmm = bb.mark("sq", f"{pfx}qmm{jc}_{grp}")

                @bb.on("tensor")
                def _(t_, jc=jc, grp=grp, w=w, qmm=qmm, ql=ql, xln_done=xln_done,
                      evprev=bb.get(f"{pfx}qe{jc}_{grp - 1}") or bb.get(f"{pfx}qe{jc - 1}_{NG - 1}")):
                    bb.w(t_, ql, xln_done, evprev)
                    for d in range(DC):
                        mm = t_.matmul(out=PSBIG[:, 0:w],
                                       lhsT=wqk_sb[:, jc % 2, d],
                                       rhs=xln_fm[:, d, grp * 512:grp * 512 + w],
                                       start=(d == 0), stop=(d == DC - 1))
                    bb.i(mm, qmm)
                qe = bb.mark("sa", f"{pfx}qe{jc}_{grp}")

                @bb.on("scalar")
                def _(a, jc=jc, grp=grp, w=w, qmm=qmm, qe=qe):
                    bb.w(a, qmm)
                    bb.i(a.mul(out=q_fm[:, jc, grp * 512:grp * 512 + w],
                               in_=PSBIG[:, 0:w], mul=1.0 / 32.0), qe)
        qdone = bb.ev(f"{pfx}qe{DC - 1}_{NG - 1}")

        for tt in range(NTL):
            vmm = bb.mark("sq", f"{pfx}vmm{tt}")

            @bb.on("tensor")
            def _(t_, tt=tt, vmm=vmm, xln_done=xln_done, wvl=wvl,
                  evprev=bb.get(f"{pfx}ve{tt - 1}") or bb.get(f"{tag}s{sq_ - 1}oev{NTL - 1}")):
                bb.w(t_, xln_done, wvl, evprev)
                for dh in range(2):
                    for d in range(DC):
                        mm = t_.matmul(out=PSBIG[:, 2048 + dh * 512:2048 + (dh + 1) * 512],
                                       lhsT=xln_fm[:, d, tt * 128:(tt + 1) * 128],
                                       rhs=wv_sb[:, d, dh * 512:(dh + 1) * 512],
                                       start=(d == 0), stop=(d == DC - 1))
                bb.i(mm, vmm)
            ve = bb.mark("sv", f"{pfx}ve{tt}")

            @bb.on("vector")
            def _(v, tt=tt, vmm=vmm, ve=ve):
                bb.w(v, vmm)
                bb.i(v.tensor_copy(out=v_tm[:, tt, :], in_=PSBIG[:, 2048:3072]), ve)
        vdone = bb.ev(f"{pfx}ve{NTL - 1}")

        def emit_ptr_ctx(i):
            nch = i + 1
            ngp = (nch + 7) // 8
            for gp in range(ngp):
                ptr = bb.mark("sq", f"{pfx}ptr{i}_{gp}")

                @bb.on("tensor")
                def _(t_, i=i, gp=gp, nch=nch, ptr=ptr, ex=bb.ev(f"{pfx}ex{i}"),
                      pev=bb.get(f"{pfx}pte{i}_{gp - 1}") or bb.get(f"{pfx}pte{i - 1}_last") or bb.get(f"{tag}ptb_prev")):
                    bb.w(t_, ex, bb.ev("consts"), pev)
                    for j in range(min(8, nch - gp * 8)):
                        c = gp * 8 + j
                        mm = t_.transpose(out=PTB[:, j * 128:(j + 1) * 128],
                                          in_=pbuf[:, c * 128:(c + 1) * 128], identity=IDB[:])
                    bb.i(mm, ptr)
                pte = bb.mark("sa", f"{pfx}pte{i}_{gp}")

                @bb.on("scalar")
                def _(a, i=i, gp=gp, nch=nch, ptr=ptr, pte=pte):
                    bb.w(a, ptr)
                    nw = min(8, nch - gp * 8) * 128
                    bb.i(a.copy(out=pbuf[:, gp * 1024:gp * 1024 + nw], in_=PTB[:, 0:nw]), pte)
            bb.events[f"{pfx}pte{i}_last"] = bb.ev(f"{pfx}pte{i}_{ngp - 1}")
            ctx_mm = bb.mark("sq", f"{pfx}ctx{i}")

            @bb.on("tensor")
            def _(t_, i=i, nch=nch, ctx_mm=ctx_mm, pte=bb.ev(f"{pfx}pte{i}_{ngp - 1}"),
                  oevprev=bb.get(f"{pfx}oev{i - 1}"), vdone=vdone):
                bb.w(t_, pte, oevprev, vdone)
                for kc in range(nch):
                    for dh in range(2):
                        mm = t_.matmul(out=PSBIG[:, 2048 + dh * 512:2048 + (dh + 1) * 512],
                                       lhsT=pbuf[:, kc * 128:(kc + 1) * 128],
                                       rhs=v_tm[:, kc, dh * 512:(dh + 1) * 512],
                                       start=(kc == 0), stop=(kc == nch - 1))
                bb.i(mm, ctx_mm)
            xr = bb.mark(f"d_xr{tag}{i % 2}", f"{pfx}xr{i}")

            @bb.on("sync")
            def _(s, i=i, xr=xr, t0=t0, fname=f"{pfx}oev{i - 2}",
                  lxlast=bb.get(f"{pfx}lx{NTL - 2 + i}") if i < 2 else None):
                bb.wn(s, fname)
                bb.w(s, lxlast)
                bb.i(s.dma_start(out=xbuf[:, i % 2], in_=src[t0 + i * 128:t0 + (i + 1) * 128, :]), xr)
            oev = bb.mark("sv", f"{pfx}oev{i}")

            @bb.on("vector")
            def _(v, i=i, oev=oev, ctx_mm=ctx_mm, xr=xr,
                  stprev=bb.get(f"{pfx}ost{i - 2}") or bb.get(f"{tag}s{sq_ - 1}ost{NTL - 2 + i}")):
                bb.w(v, ctx_mm, xr, stprev)
                r = ROWS
                for dh in range(2):
                    ins = v.scalar_tensor_tensor(out=obuf[:, i % 2, dh * 512:(dh + 1) * 512],
                                                 in0=PSBIG[:, 2048 + dh * 512:2048 + (dh + 1) * 512],
                                                 scalar=r[:, CS + 40 + (i % 8):CS + 41 + (i % 8)],
                                                 in1=xbuf[:, i % 2, dh * 512:(dh + 1) * 512],
                                                 op0=ALU.mult, op1=ALU.add)
                bb.i(ins, oev)
            ost = bb.mark(f"d_o{tag}{i % 2}", f"{pfx}ost{i}")

            @bb.on("sync")
            def _(s, i=i, oev=oev, ost=ost, t0=t0):
                bb.w(s, oev)
                bb.i(s.dma_start(out=dst[t0 + i * 128:t0 + (i + 1) * 128, :], in_=obuf[:, i % 2]), ost)

        for i in range(NTL):
            ki = 128 * (i + 1)
            nkb = (ki + 511) // 512
            smm = bb.mark("sq", f"{pfx}smm{i}")

            @bb.on("tensor")
            def _(t_, i=i, ki=ki, nkb=nkb, smm=smm, qdone=qdone, vdone=vdone,
                  exprev=bb.get(f"{pfx}ex{i - 1}")):
                bb.w(t_, qdone, vdone, exprev)
                for kb in range(nkb):
                    w = min(512, ki - kb * 512)
                    for d in range(DC):
                        mm = t_.matmul(out=PSBIG[:, kb * 512:kb * 512 + w],
                                       lhsT=q_fm[:, d, i * 128:(i + 1) * 128],
                                       rhs=xln_fm[:, d, kb * 512:kb * 512 + w],
                                       start=(d == 0), stop=(d == DC - 1))
                bb.i(mm, smm)
            al = bb.mark("sv", f"{pfx}al{i}")

            @bb.on("vector")
            def _(v, i=i, ki=ki, al=al, smm=smm, ab3=bb.ev(f"{tag}ab3")):
                bb.w(v, smm, ab3)
                v.tensor_add(out=PSBIG[:, 0:ki], in0=PSBIG[:, 0:ki],
                             in1=alibuf[:, S - i * 128:S - i * 128 + ki])
                v.drain()
                v.reduce_max(out=ROWS[:, CS + 24 + (i % 8):CS + 25 + (i % 8)],
                             in_=PSBIG[:, 0:ki], axis=AX, negate=True)
                bb.i(v.engine_nop(), al)
            ex = bb.mark("sa", f"{pfx}ex{i}")

            @bb.on("scalar")
            def _(a, i=i, ki=ki, ex=ex, al=al,
                  ctxprev=bb.get(f"{pfx}ctx{i - 1}") or bb.get(f"{tag}s{sq_ - 1}ctx{NTL - 1}")):
                bb.w(a, al, ctxprev)
                bb.i(a.activation(out=pbuf[:, 0:ki], in_=PSBIG[:, 0:ki], func=AF.Exp,
                                  bias=ROWS[:, CS + 24 + (i % 8):CS + 25 + (i % 8)], scale=1.0,
                                  accum_out=ROWS[:, CS + 32 + (i % 8):CS + 33 + (i % 8)]), ex)
            rc = bb.mark("sv", f"{pfx}rc{i}")

            @bb.on("vector")
            def _(v, i=i, rc=rc, ex=ex):
                bb.w(v, ex)
                v.drain()
                v.reciprocal(out=ROWS[:, CS + 40 + (i % 8):CS + 41 + (i % 8)],
                             in_=ROWS[:, CS + 32 + (i % 8):CS + 33 + (i % 8)])
                v.drain()
                bb.i(v.engine_nop(), rc)
            emit_ptr_ctx(i)
        bb.events[f"{tag}ptb_prev"] = bb.ev(f"{pfx}pte{NTL - 1}_last")
    ctx["ptb_chain"] = bb.ev(f"{tag}ptb_prev"[0:0] + f"{tag}s{bpc - 1}pte{NTL - 1}_last"[0:99]) if False else bb.events[f"{tag}ptb_prev"]
    ctx[f"{tag}_fin"] = [bb.total(f"d_o{tag}0"), bb.total(f"d_o{tag}1")]



def plan_phases(ctx):
    bb, io, sc = ctx["bb"], ctx["io"], ctx["sc"]
    upto = ctx["upto"]
    fin = []
    if upto == 50:
        plan_attn1(ctx, "E", io["x_in"], io["y_out"])
        fin += ctx["E_fin"]
    elif upto == 1:
        plan_ffn0(ctx, "A", io["x_in"], io["y_out"])
        fin += ctx["A_fin"]
    elif upto == 2:
        plan_ffn0(ctx, "A", io["x_in"], sc["s_a"])
        ctx["B_prev_done"] = ctx["A_fin"]
        plan_attn0(ctx, "B", sc["s_a"], io["y_out"])
        fin += ctx["B_fin"]
    else:
        plan_ffn0(ctx, "A", io["x_in"], sc["s_a"])
        ctx["B_prev_done"] = ctx["A_fin"]
        plan_attn0(ctx, "B", sc["s_a"], sc["s_b"])
        ctx["C_prev_done"] = ctx["B_fin"]
        plan_ffn0(ctx, "C", sc["s_b"], sc["cc_src"])
        plan_gather(ctx)
        plan_relayout(ctx)
        ctx["D_prev_done"] = ctx["C_fin"] + ctx["gather_fin"]
        if upto == 45:
            plan_ffn1(ctx, "D", sc["s_g"], io["y_out"])
            fin += ctx["D_fin"]
        elif upto == 4:
            plan_ffn1(ctx, "D", sc["s_g"], io["y_out"])
            fin += ctx["D_fin"]
        elif upto == 5:
            plan_ffn1(ctx, "D", sc["s_g"], sc["s_d"])
            ctx["E_prev_done"] = ctx["D_fin"]
            plan_attn1(ctx, "E", sc["s_d"], io["y_out"])
            fin += ctx["E_fin"]
        else:
            plan_ffn1(ctx, "D", sc["s_g"], sc["s_d"])
            ctx["E_prev_done"] = ctx["D_fin"]
            plan_attn1(ctx, "E", sc["s_d"], sc["s_e"])
            ctx["F_prev_done"] = ctx["E_fin"]
            plan_ffn1(ctx, "F", sc["s_e"], io["y_out"])
            fin += ctx["F_fin"]

    @bb.on("sync")
    def _(s):
        for tok in fin:
            bb.w(s, tok)



def plan_ffn1(ctx, tag, src, dst):
    bb, sc = ctx["bb"], ctx["sc"]
    ROWS, PTB, IDB, PSBIG = ctx["ROWS"], ctx["PTB"], ctx["IDB"], ctx["PSBIG"]
    NBLK, TB, NT = ctx["NBLK"], ctx["TB"], ctx["NT"]
    fa = Arena(ctx["F32A"], 14336)
    xblk = fa.take(NT, D)
    gbuf = fa.take(2, TB)
    obuf = fa.take(2, D)
    dump = fa.take(2, D)
    ba = Arena(ctx["BF16A"], 69632)
    xlns = ba.take(2, D)
    xln_fm = ba.take(DC, TB)
    harena = ba.take(HC1, TB)
    wgs = ba.take(2, 4, DC, 128)
    wus = ba.take(2, 4, DC, 128)
    wds = ba.take(2, 4, D)
    CS = RSTAT
    WG, WU, WD = sc["w1gT"], sc["w1uT"], sc["w1dT"]
    NGB = HC1 // 4

    for b in range(NBLK):
        xl = bb.mark(f"d_x{tag}", f"{tag}xl{b}")

        @bb.on("sync")
        def _(s, b=b, xl=xl, freev=bb.get(f"{tag}xfree{b - 1}"),
              pprev=ctx.get(f"{tag}_prev_done") if b == 0 else None):
            bb.w(s, freev, pprev)
            bb.i(s.dma_start(out=xblk,
                             in_=src[b * TB:(b + 1) * TB, :].rearrange("(a p) d -> p a d", p=128)), xl)
        for g in range(NGB):
            gi = b * NGB + g
            par = gi % 2
            wl0 = bb.mark(f"d_w{tag}{par}")
            wl = bb.mark(f"d_w{tag}{par}", f"{tag}wl{gi}")

            @bb.on("sync")
            def _(s, g=g, par=par, wl0=wl0, wl=wl, fname=f"{tag}wfree{gi - 2}",
                  pprev=ctx.get(f"{tag}_prev_done") if gi < 2 else None):
                bb.wn(s, fname)
                bb.w(s, pprev)
                bb.i(s.dma_start(out=wgs[:, par], in_=WG[g]), wl0)
                bb.i(s.dma_start(out=wus[:, par], in_=WU[g]), wl)
        for g in range(NGB):
            di = b * NGB + g
            par = di % 2
            dl = bb.mark(f"d_d{tag}{par}", f"{tag}dl{di}")

            @bb.on("sync")
            def _(s, g=g, par=par, dl=dl, fname=f"{tag}dfree{di - 2}",
                  pprev=ctx.get(f"{tag}_prev_done") if di < 2 else None):
                bb.wn(s, fname)
                bb.w(s, pprev)
                bb.i(s.dma_start(out=wds[:, par], in_=WD[g]), dl)

        for t in range(NT):
            st = bb.mark("sv", f"{tag}sum{b}_{t}")

            @bb.on("vector")
            def _(v, t=t, xl=xl, st=st):
                bb.w(v, xl)
                bb.i(v.reduce_sum(out=ROWS[:, CS + t:CS + t + 1], in_=xblk[:, t], axis=AX), st)
            sq = bb.mark("sa", f"{tag}sq{b}_{t}")

            @bb.on("scalar")
            def _(a, t=t, xl=xl, sq=sq):
                bb.w(a, xl)
                bb.i(a.activation(out=dump[:, t % 2], in_=xblk[:, t], func=AF.Square,
                                  accum_out=ROWS[:, CS + 4 + t:CS + 5 + t]), sq)
        r1 = bb.mark("sv", f"{tag}rows1_{b}")

        @bb.on("vector")
        def _(v, r1=r1, sa_need=bb.ev(f"{tag}sq{b}_{NT - 1}")):
            bb.w(v, sa_need)
            v.drain()
            r = ROWS
            c = CS
            v.tensor_scalar_mul(out=r[:, c:c + NT], in0=r[:, c:c + NT], scalar1=1.0 / D)
            v.drain()
            v.tensor_mul(out=r[:, c + 16:c + 16 + NT], in0=r[:, c:c + NT], in1=r[:, c:c + NT])
            v.drain()
            v.scalar_tensor_tensor(out=r[:, c + 16:c + 16 + NT], in0=r[:, c + 4:c + 4 + NT],
                                   scalar=1.0 / D, in1=r[:, c + 16:c + 16 + NT],
                                   op0=ALU.mult, op1=ALU.subtract)
            v.drain()
            v.tensor_scalar_add(out=r[:, c + 16:c + 16 + NT], in0=r[:, c + 16:c + 16 + NT], scalar1=EPS)
            v.drain()
            bb.i(v.engine_nop(), r1)
        r2e = bb.mark("sa", f"{tag}rows2_{b}")

        @bb.on("scalar")
        def _(a, r1=r1, r2e=r2e):
            bb.w(a, r1)
            bb.i(a.activation(out=ROWS[:, CS + 16:CS + 16 + NT], in_=ROWS[:, CS + 16:CS + 16 + NT],
                              func=AF.Sqrt), r2e)
        r3 = bb.mark("sv", f"{tag}rows3_{b}")

        @bb.on("vector")
        def _(v, r2e=r2e, r3=r3):
            bb.w(v, r2e)
            v.drain()
            r = ROWS
            c = CS
            v.reciprocal(out=r[:, c + 16:c + 16 + NT], in_=r[:, c + 16:c + 16 + NT])
            v.drain()
            v.tensor_mul(out=r[:, c + 8:c + 8 + NT], in0=r[:, c:c + NT], in1=r[:, c + 16:c + 16 + NT])
            v.drain()
            v.tensor_scalar_mul(out=r[:, c + 8:c + 8 + NT], in0=r[:, c + 8:c + 8 + NT], scalar1=-1.0)
            v.drain()
            bb.i(v.engine_nop(), r3)
        for t in range(NT):
            lx = bb.mark("sa", f"{tag}lx{b}_{t}")

            @bb.on("scalar")
            def _(a, t=t, r3=r3, lx=lx, trprev=bb.get(f"{tag}tr{b}_{t - 2}") or bb.get(f"{tag}tr{b - 1}_{t}")):
                bb.w(a, r3, trprev)
                bb.i(a.activation(out=xlns[:, t % 2], in_=xblk[:, t], func=AF.Identity,
                                  bias=ROWS[:, CS + 8 + t:CS + 9 + t],
                                  scale=ROWS[:, CS + 16 + t:CS + 17 + t]), lx)
            tr = bb.mark("sq", f"{tag}tr{b}_{t}")

            @bb.on("tensor")
            def _(t_, t=t, lx=lx, tr=tr, prevev=bb.get(f"{tag}trev{b}_{t - 1}")
                  or bb.get(f"{tag}trev{b - 1}_{NT - 1}") or ctx.get("ptb_chain"),
                  pquiet=ctx.pop("psum_quiet", None)):
                bb.w(t_, lx, bb.ev("consts"), prevev, pquiet)
                for j in range(DC):
                    mm = t_.transpose(out=PTB[:, j * 128:(j + 1) * 128],
                                      in_=xlns[:, t % 2, j * 128:(j + 1) * 128], identity=IDB[:])
                bb.i(mm, tr)
            tre = bb.mark("sa", f"{tag}trev{b}_{t}")

            @bb.on("scalar")
            def _(a, t=t, tr=tr, tre=tre, mmu_prev=bb.get(f"{tag}mmu{b - 1}_{HC1 - 1}") if t == 0 else None):
                bb.w(a, tr, mmu_prev)
                bb.i(a.copy(out=xln_fm[:, :, t * 128:(t + 1) * 128],
                            in_=PTB[:].rearrange("p (dc c) -> p dc c", dc=DC)), tre)
        xq_ready = bb.ev(f"{tag}trev{b}_{NT - 1}")

        for hc in range(HC1):
            g = hc // 4
            gi = b * NGB + g
            par = gi % 2
            sl = hc % 4
            pg, pu = (0, 1) if hc % 2 == 0 else (4, 5)
            mmg = bb.mark("sq", f"{tag}mmg{b}_{hc}")
            mmu = bb.mark("sq", f"{tag}mmu{b}_{hc}")

            @bb.on("tensor")
            def _(t_, hc=hc, par=par, sl=sl, pg=pg, pu=pu, mmg=mmg, mmu=mmu,
                  wl=bb.ev(f"{tag}wl{gi}"), xq_ready=xq_ready,
                  bankfree=bb.get(f"{tag}prd{b}_{hc - 2}"),
                  dnprev=bb.get(f"{tag}dnall{b - 1}") if hc == 0 else None):
                bb.w(t_, wl, xq_ready, bankfree, dnprev)
                for d in range(DC):
                    mm = t_.matmul(out=bank(ctx, pg, TB), lhsT=wgs[:, par, sl, d], rhs=xln_fm[:, d],
                                   start=(d == 0), stop=(d == DC - 1))
                bb.i(mm, mmg)
                for d in range(DC):
                    mm = t_.matmul(out=bank(ctx, pu, TB), lhsT=wus[:, par, sl, d], rhs=xln_fm[:, d],
                                   start=(d == 0), stop=(d == DC - 1))
                bb.i(mm, mmu)
            if hc % 4 == 3:
                bb.events[f"{tag}wfree{gi}"] = mmu
            sil = bb.mark("sa", f"{tag}sil{b}_{hc}")

            @bb.on("scalar")
            def _(a, hc=hc, pg=pg, sil=sil, mmg=mmg, gfree=bb.get(f"{tag}prd{b}_{hc - 2}")):
                bb.w(a, mmg, gfree)
                bb.i(a.activation(out=gbuf[:, hc % 2], in_=bank(ctx, pg, TB), func=AF.Silu,
                                  scale=1.0 / 127.0), sil)
            prd = bb.mark("sv", f"{tag}prd{b}_{hc}")

            @bb.on("vector")
            def _(v, hc=hc, pu=pu, sil=sil, mmu=mmu, prd=prd):
                bb.w(v, sil, mmu)
                bb.i(v.scalar_tensor_tensor(out=harena[:, hc], in0=bank(ctx, pu, TB),
                                            scalar=1.0 / 127.0, in1=gbuf[:, hc % 2],
                                            op0=ALU.mult, op1=ALU.mult), prd)
        hdone = bb.ev(f"{tag}prd{b}_{HC1 - 1}")

        for tp in range(NT // 2):
            t0, t1 = 2 * tp, 2 * tp + 1
            gtoks = [bb.mark("sq", f"{tag}dnw{(b * (NT // 2) + tp) * NGB + g}") for g in range(NGB - 1)]
            dn = bb.mark("sq", f"{tag}dn{b}_{tp}")
            for g in range(NGB - 1):
                bb.events[f"{tag}dfree{(b * (NT // 2) + tp) * NGB + g}"] = gtoks[g]
            bb.events[f"{tag}dfree{(b * (NT // 2) + tp) * NGB + (NGB - 1)}"] = dn

            @bb.on("tensor")
            def _(t_, tp=tp, t0=t0, t1=t1, dn=dn, hdone=hdone, b=b, gtoks=gtoks,
                  evprev=bb.get(f"{tag}oev{b}_{2 * tp - 1}") or bb.get(f"{tag}oev{b - 1}_{NT - 1}")):
                bb.w(t_, hdone, evprev)
                for g in range(NGB):
                    di = (b * (NT // 2) + tp) * NGB + g
                    par = di % 2
                    bb.w(t_, bb.ev(f"{tag}dl{di}"))
                    for sl in range(4):
                        hc = g * 4 + sl
                        first = hc == 0
                        last = hc == HC1 - 1
                        for ti, t in enumerate((t0, t1)):
                            for dh in range(2):
                                mm = t_.matmul(out=bank(ctx, 2 + ti * 2 + dh),
                                               lhsT=harena[:, hc, t * 128:(t + 1) * 128],
                                               rhs=wds[:, par, sl, dh * 512:(dh + 1) * 512],
                                               start=first, stop=last)
                    if g < NGB - 1:
                        bb.i(mm, gtoks[g])
                bb.i(mm, dn)
            for ti, t in enumerate((t0, t1)):
                ev = bb.mark("sv", f"{tag}oev{b}_{t}")

                @bb.on("vector")
                def _(v, t=t, ti=ti, dn=dn, ev=ev,
                      stfree=bb.get(f"{tag}ost{b}_{t - 2}") or bb.get(f"{tag}ost{b - 1}_{t - 2 + NT}")):
                    bb.w(v, dn, stfree)
                    for dh in range(2):
                        ins = v.scalar_tensor_tensor(out=obuf[:, t % 2, dh * 512:(dh + 1) * 512],
                                                     in0=bank(ctx, 2 + ti * 2 + dh),
                                                     scalar=0.5 / 127.0,
                                                     in1=xblk[:, t, dh * 512:(dh + 1) * 512],
                                                     op0=ALU.mult, op1=ALU.add)
                    bb.i(ins, ev)
                st = bb.mark(f"d_o{tag}{t % 2}", f"{tag}ost{b}_{t}")

                @bb.on("sync")
                def _(s, t=t, b=b, ev=ev, st=st):
                    bb.w(s, ev)
                    bb.i(s.dma_start(out=dst[b * TB + t * 128:b * TB + (t + 1) * 128, :],
                                     in_=obuf[:, t % 2]), st)
        bb.events[f"{tag}dnall{b}"] = bb.ev(f"{tag}dn{b}_{NT // 2 - 1}")
        bb.events[f"{tag}xfree{b}"] = bb.ev(f"{tag}oev{b}_{NT - 1}")

    ctx["ptb_chain"] = bb.ev(f"{tag}trev{NBLK - 1}_{NT - 1}")
    ctx[f"{tag}_fin"] = [bb.total(f"d_o{tag}0"), bb.total(f"d_o{tag}1")]



def plan_gather(ctx):
    bb, sc = ctx["bb"], ctx["sc"]
    TOK, half, spc = ctx["TOK"], ctx["half"], ctx["spc"]
    S = ctx["seqlen"]
    cc0 = bb.mark("sg", "cc0")

    @bb.on("gpsimd")
    def _(g, h0=ctx.get("C_half_st"), glibn=None):
        bb.w(g, h0)
        bb.i(g.collective_compute("AllGather", ALU.bypass, ins=[sc["cc_src"][0:half]],
                                  outs=[sc["cc_d0"][:]], replica_groups=[list(range(NCORES))]), cc0)
    cc1 = bb.mark("sg", "cc1")

    @bb.on("gpsimd")
    def _(g, h1=ctx.get("C_fin")):
        bb.w(g, h1)
        bb.i(g.collective_compute("AllGather", ALU.bypass, ins=[sc["cc_src"][half:TOK]],
                                  outs=[sc["cc_d1"][:]], replica_groups=[list(range(NCORES))]), cc1)
    ctx["cc_done"] = [bb.ev("cc0"), bb.ev("cc1")]


def plan_relayout(ctx):
    bb, sc = ctx["bb"], ctx["sc"]
    TOK, half, spc = ctx["TOK"], ctx["half"], ctx["spc"]
    S = ctx["seqlen"]
    bpc = ctx["bpc"]
    assert bpc % 2 == 0, "gather halves need even batches-per-core"
    LO = bpc // 2
    NCH = 4 if spc % 4 == 0 else 1
    sw = spc // NCH
    for h2 in range(2):
        src = sc["cc_d0"] if h2 == 0 else sc["cc_d1"]
        ccev = bb.ev("cc0") if h2 == 0 else bb.ev("cc1")
        for ch in range(NCH):
            gl = bb.mark(f"d_gl{h2}", f"gl{h2}_{ch}")

            @bb.on("sync")
            def _(s, h2=h2, src=src, ch=ch, gl=gl, ccev=ccev, sw=sw, LO=LO):
                bb.w(s, ccev)
                if "pid_sync" not in ctx:
                    ctx["pid_sync"] = s.partition_id()
                pid = ctx["pid_sync"]
                dstv = (sc["s_g"].rearrange("(sv hi h2 lo) d -> sv hi h2 lo d", hi=8, h2=2, lo=LO)
                        [ch * sw:(ch + 1) * sw, :, h2])
                srcv = (src.rearrange("(hi lo sblk sv) d -> sblk sv hi lo d", hi=8, lo=LO, sv=sw)
                        [bass.ts(pid * (spc // sw) + ch, 1)][0])
                bb.i(s.dma_start(out=dstv, in_=srcv), gl)
    ctx["gather_fin"] = [bb.total("d_gl0"), bb.total("d_gl1")]



def plan_attn1(ctx, tag, src, dst):
    bb, sc, io = ctx["bb"], ctx["sc"], ctx["io"]
    ROWS, PTB, IDB, PSBIG = ctx["ROWS"], ctx["PTB"], ctx["IDB"], ctx["PSBIG"]
    ONESF = ctx["ONESF"]
    TOK, TB, NT = ctx["TOK"], ctx["TB"], ctx["NT"]
    NSB = TOK // TB
    NPG = TB // 128
    fa = Arena(ctx["F32A"], 14336)
    xblk = fa.take(NT, D)
    obuf = fa.take(2, D)
    dump = fa.take(2, D)
    usb = fa.take(512)
    wsb = fa.take(512)
    ssb = fa.take(512)
    grow = fa.take(512)
    brow = fa.take(512)
    ba = Arena(ctx["BF16A"], 69632)
    xlns = ba.take(2, D)
    xln_fm = ba.take(DC, TB)
    q_fm = ba.take(DC, TB)
    k_fe = ba.take(DC, TB)
    k_fo = ba.take(DC, TB)
    v_tm = ba.take(NT, D)
    probs = ba.take(2, NHEAD * 128)
    bvrow = ba.take(D)
    wq_sb = ba.take(DC, DC, 128)
    wk_sb = ba.take(DC, DC, 128)
    wv_sb = ba.take(DC, D)
    CS = RSTAT
    BQ8 = RTMP + 12

    wla = bb.mark(f"d_wq{tag}")
    wlb = bb.mark(f"d_wq{tag}")
    wl = bb.mark(f"d_wq{tag}", f"{tag}wl")

    @bb.on("sync")
    def _(s, prol=ctx.get("prologue_fin"), pprev=ctx.get(f"{tag}_prev_done"), wla=wla, wlb=wlb, wl=wl):
        bb.w(s, prol, pprev)
        bb.i(s.dma_start(out=wq_sb.rearrange("p (g a) dc c -> p g a dc c", g=DC // 4),
                         in_=sc["wqT"].rearrange("g p a dc c -> p g a dc c")), wla)
        bb.i(s.dma_start(out=wk_sb.rearrange("p (g a) dc c -> p g a dc c", g=DC // 4),
                         in_=sc["wkT"].rearrange("g p a dc c -> p g a dc c")), wlb)
        bb.i(s.dma_start(out=wv_sb.rearrange("p (g a) d -> p g a d", g=DC // 4),
                         in_=sc["wvT1"].rearrange("g p a d -> p g a d")), wl)
    pre_v = bb.mark("sv", f"{tag}prev")

    @bb.on("vector")
    def _(v, pprev=ctx.get(f"{tag}_prev_done")):
        bb.w(v, bb.ev("gb_bc"), pprev)
        v.memset(probs.rearrange("p a b -> p (a b)"), 0.0)
        v.memset(k_fe.rearrange("p a b -> p (a b)")[64:128, :], 0.0)
        v.memset(k_fo.rearrange("p a b -> p (a b)")[0:64, :], 0.0)
        v.drain()
        r = ROWS
        v.tensor_scalar_mul(out=r[:, BQ8:BQ8 + DC], in0=r[:, RBQ:RBQ + DC], scalar1=1.0 / 8.0)
        v.drain()
        for h in range(NHEAD):
            v.tensor_scalar_mul(out=grow[:, h * 32:(h + 1) * 32], in0=ONESF[:, 0:32],
                                scalar1=r[:, RGAM + h:RGAM + h + 1])
            v.tensor_scalar_mul(out=brow[:, h * 32:(h + 1) * 32], in0=ONESF[:, 0:32],
                                scalar1=r[:, RBET + h:RBET + h + 1])
        v.drain()
        bb.i(v.engine_nop(), pre_v)
    bvl2 = bb.mark("d_bv", f"{tag}bvl2")

    @bb.on("scalar")
    def _(a, pprev=ctx.get(f"{tag}_prev_done")):
        bb.w(a, pprev)
        srca = io["bqkv"].rearrange("(h s r) -> h s r", h=NHEAD, s=3, r=64)[:, 2]
        bb.i(a.dma_start(out=dump[0:1, 0, :].rearrange("p (h r) -> p h r", h=NHEAD),
                         in_=srca[None, :, :]), bvl2)
    bvm = bb.mark("sq", f"{tag}bvm")

    @bb.on("tensor")
    def _(t_, bvl2=bvl2, pquiet=ctx.pop("psum_quiet", None)):
        bb.w(t_, bvl2, bb.ev("consts"), pquiet)
        bb.i(t_.matmul(out=PSBIG[:, 3072:3072 + 512], lhsT=ONESF[0:1, 0:128],
                       rhs=dump[0:1, 0, 0:512], start=True, stop=True), bvm)
    bvm2 = bb.mark("sq", f"{tag}bvm2")

    @bb.on("tensor")
    def _(t_):
        bb.w(t_, bb.ev(f"{tag}bve1"))
        bb.i(t_.matmul(out=PSBIG[:, 3072:3072 + 512], lhsT=ONESF[0:1, 0:128],
                       rhs=dump[0:1, 0, 512:1024], start=True, stop=True), bvm2)
    bve1 = bb.mark("sv", f"{tag}bve1")

    @bb.on("vector")
    def _(v, bvm=bvm):
        bb.w(v, bvm)
        bb.i(v.tensor_copy(out=bvrow[:, 0:512], in_=PSBIG[:, 3072:3584]), bve1)
    bve2 = bb.mark("sv", f"{tag}bve2")

    @bb.on("vector")
    def _(v, bvm2=bvm2):
        bb.w(v, bvm2)
        bb.i(v.tensor_copy(out=bvrow[:, 512:1024], in_=PSBIG[:, 3072:3584]), bve2)

    for sb in range(NSB):
        pfx = f"{tag}b{sb}"
        xl = bb.mark(f"d_x{tag}", f"{pfx}xl")

        @bb.on("sync")
        def _(s, sb=sb, xl=xl, freev=bb.get(f"{tag}b{sb - 1}oev1"), wl=wl if sb == 0 else None):
            bb.w(s, freev, wl)
            bb.i(s.dma_start(out=xblk,
                             in_=src[sb * TB:(sb + 1) * TB, :].rearrange("(a p) d -> p a d", p=128)), xl)
        for t in range(NT):
            st = bb.mark("sv", f"{pfx}sum{t}")

            @bb.on("vector")
            def _(v, t=t, xl=xl, st=st):
                bb.w(v, xl)
                bb.i(v.reduce_sum(out=ROWS[:, CS + t:CS + t + 1], in_=xblk[:, t], axis=AX), st)
            sq = bb.mark("sa", f"{pfx}sq{t}")

            @bb.on("scalar")
            def _(a, t=t, xl=xl, sq=sq, bv0=bb.ev(f"{tag}bve2") if sb == 0 and t == 0 else None):
                bb.w(a, xl, bv0)
                bb.i(a.activation(out=dump[:, t % 2], in_=xblk[:, t], func=AF.Square,
                                  accum_out=ROWS[:, CS + 4 + t:CS + 5 + t]), sq)
        r1 = bb.mark("sv", f"{pfx}rows1")

        @bb.on("vector")
        def _(v, r1=r1, sa_need=bb.ev(f"{pfx}sq{NT - 1}")):
            bb.w(v, sa_need)
            v.drain()
            r = ROWS
            c = CS
            v.tensor_scalar_mul(out=r[:, c:c + NT], in0=r[:, c:c + NT], scalar1=1.0 / D)
            v.drain()
            v.tensor_mul(out=r[:, c + 16:c + 16 + NT], in0=r[:, c:c + NT], in1=r[:, c:c + NT])
            v.drain()
            v.scalar_tensor_tensor(out=r[:, c + 16:c + 16 + NT], in0=r[:, c + 4:c + 4 + NT],
                                   scalar=1.0 / D, in1=r[:, c + 16:c + 16 + NT],
                                   op0=ALU.mult, op1=ALU.subtract)
            v.drain()
            v.tensor_scalar_add(out=r[:, c + 16:c + 16 + NT], in0=r[:, c + 16:c + 16 + NT], scalar1=EPS)
            v.drain()
            bb.i(v.engine_nop(), r1)
        r2e = bb.mark("sa", f"{pfx}rows2")

        @bb.on("scalar")
        def _(a, r1=r1, r2e=r2e):
            bb.w(a, r1)
            bb.i(a.activation(out=ROWS[:, CS + 16:CS + 16 + NT], in_=ROWS[:, CS + 16:CS + 16 + NT],
                              func=AF.Sqrt), r2e)
        r3 = bb.mark("sv", f"{pfx}rows3")

        @bb.on("vector")
        def _(v, r2e=r2e, r3=r3):
            bb.w(v, r2e)
            v.drain()
            r = ROWS
            c = CS
            v.reciprocal(out=r[:, c + 16:c + 16 + NT], in_=r[:, c + 16:c + 16 + NT])
            v.drain()
            v.tensor_mul(out=r[:, c + 8:c + 8 + NT], in0=r[:, c:c + NT], in1=r[:, c + 16:c + 16 + NT])
            v.drain()
            v.tensor_scalar_mul(out=r[:, c + 8:c + 8 + NT], in0=r[:, c + 8:c + 8 + NT], scalar1=-1.0)
            v.drain()
            bb.i(v.engine_nop(), r3)
        for t in range(NT):
            lx = bb.mark("sa", f"{pfx}lx{t}")

            @bb.on("scalar")
            def _(a, t=t, r3=r3, lx=lx,
                  trprev=bb.get(f"{pfx}tr{t - 2}") or bb.get(f"{tag}b{sb - 1}tr{t}")):
                bb.w(a, r3, trprev)
                bb.i(a.activation(out=xlns[:, t % 2], in_=xblk[:, t], func=AF.Identity,
                                  bias=ROWS[:, CS + 8 + t:CS + 9 + t],
                                  scale=ROWS[:, CS + 16 + t:CS + 17 + t]), lx)
            tr = bb.mark("sq", f"{pfx}tr{t}")

            @bb.on("tensor")
            def _(t_, t=t, lx=lx, tr=tr, prevev=bb.get(f"{pfx}trev{t - 1}")
                  or bb.get(f"{tag}b{sb - 1}trev{NT - 1}") or ctx.get("ptb_chain")):
                bb.w(t_, lx, bb.ev("consts"), prevev)
                for j in range(DC):
                    mm = t_.transpose(out=PTB[:, j * 128:(j + 1) * 128],
                                      in_=xlns[:, t % 2, j * 128:(j + 1) * 128], identity=IDB[:])
                bb.i(mm, tr)
            tre = bb.mark("sa", f"{pfx}trev{t}")

            @bb.on("scalar")
            def _(a, t=t, tr=tr, tre=tre):
                bb.w(a, tr)
                bb.i(a.copy(out=xln_fm[:, :, t * 128:(t + 1) * 128],
                            in_=PTB[:].rearrange("p (dc c) -> p dc c", dc=DC)), tre)
        xln_done = bb.ev(f"{pfx}trev{NT - 1}")

        for kind, wsb_ in (("q", wq_sb), ("k", wk_sb)):
            for jc in range(DC):
                qmm = bb.mark("sq", f"{pfx}{kind}mm{jc}")

                @bb.on("tensor")
                def _(t_, kind=kind, wsb_=wsb_, jc=jc, qmm=qmm, xln_done=xln_done, wl=wl,
                      evprev=bb.get(f"{pfx}{kind}e{jc - 1}")
                      or (bb.get(f"{pfx}qe{DC - 1}") if kind == "k"
                          else bb.get(f"{tag}b{sb - 1}scq{NPG - 1}"))):
                    bb.w(t_, xln_done, wl, evprev)
                    for d in range(DC):
                        mm = t_.matmul(out=PSBIG[:, 0:TB], lhsT=wsb_[:, jc, d], rhs=xln_fm[:, d],
                                       start=(d == 0), stop=(d == DC - 1))
                    bb.i(mm, qmm)
                qe = bb.mark("sa", f"{pfx}{kind}e{jc}")

                @bb.on("scalar")
                def _(a, kind=kind, jc=jc, qmm=qmm, qe=qe):
                    bb.w(a, qmm)
                    if kind == "q":
                        bb.i(a.activation(out=q_fm[:, jc], in_=PSBIG[:, 0:TB], func=AF.Identity,
                                          bias=ROWS[:, BQ8 + jc:BQ8 + jc + 1], scale=1.0 / 8.0), qe)
                    else:
                        a.activation(out=k_fe[0:64, jc], in_=PSBIG[0:64, 0:TB], func=AF.Identity,
                                     bias=ROWS[0:64, RBQ + 8 + jc:RBQ + 9 + jc], scale=1.0)
                        bb.i(a.activation(out=k_fo[64:128, jc], in_=PSBIG[64:128, 0:TB],
                                          func=AF.Identity,
                                          bias=ROWS[64:128, RBQ + 8 + jc:RBQ + 9 + jc], scale=1.0), qe)
        qkdone = bb.ev(f"{pfx}ke{DC - 1}")

        for t in range(NT):
            vmm = bb.mark("sq", f"{pfx}vmm{t}")

            @bb.on("tensor")
            def _(t_, t=t, vmm=vmm, xln_done=xln_done, wl=wl,
                  evprev=bb.get(f"{pfx}ve{t - 1}") or bb.get(f"{tag}b{sb - 1}oev1")):
                bb.w(t_, xln_done, wl, evprev)
                for dh in range(2):
                    for d in range(DC):
                        mm = t_.matmul(out=PSBIG[:, 2048 + dh * 512:2048 + (dh + 1) * 512],
                                       lhsT=xln_fm[:, d, t * 128:(t + 1) * 128],
                                       rhs=wv_sb[:, d, dh * 512:(dh + 1) * 512],
                                       start=(d == 0), stop=(d == DC - 1))
                bb.i(mm, vmm)
            ve = bb.mark("sv", f"{pfx}ve{t}")

            @bb.on("vector")
            def _(v, t=t, vmm=vmm, ve=ve):
                bb.w(v, vmm)
                bb.i(v.scalar_tensor_tensor(out=v_tm[:, t], in0=PSBIG[:, 2048:3072],
                                            scalar=1.0, in1=bvrow, op0=ALU.mult, op1=ALU.add), ve)
        vdone = bb.ev(f"{pfx}ve{NT - 1}")

        for pg in range(NPG):
            smm = bb.mark("sq", f"{pfx}smm{pg}")

            @bb.on("tensor")
            def _(t_, pg=pg, smm=smm, qkdone=qkdone,
                  scqprev=bb.get(f"{pfx}scq{pg - 1}") or bb.get(f"{tag}b{sb - 1}scq{NPG - 1}"),
                  prevv=bb.ev(f"{tag}prev") if sb == 0 and pg == 0 else None):
                bb.w(t_, qkdone, scqprev, prevv)
                for h in range(NHEAD):
                    kf = k_fe if h % 2 == 0 else k_fo
                    mm = t_.matmul(out=PSBIG[:, h * 128:(h + 1) * 128],
                                   lhsT=kf[:, h // 2, pg * 128:(pg + 1) * 128],
                                   rhs=q_fm[:, h // 2, pg * 128:(pg + 1) * 128],
                                   start=True, stop=True)
                bb.i(mm, smm)
            scq = bb.mark("sa", f"{pfx}scq{pg}")

            @bb.on("scalar")
            def _(a, pg=pg, smm=smm, scq=scq,
                  polprev=bb.get(f"{pfx}pol{pg - 1}") or bb.get(f"{tag}b{sb - 1}pol{NPG - 1}")):
                bb.w(a, smm, polprev)
                for pp in range(4):
                    pv = (PSBIG[32 * pp:32 * pp + 32, 0:NHEAD * 128]
                          .rearrange("p (h c) -> p h c", h=NHEAD)[:, :, 32 * pp:32 * pp + 32])
                    a.activation(out=usb[32 * pp:32 * pp + 32, :]
                                 .rearrange("p (h c) -> p h c", h=NHEAD),
                                 in_=pv, func=AF.Square)
                    ins = a.copy(out=ssb[32 * pp:32 * pp + 32, :]
                                 .rearrange("p (h c) -> p h c", h=NHEAD), in_=pv)
                bb.i(ins, scq)
            pol = bb.mark("sv", f"{pfx}pol{pg}")

            @bb.on("vector")
            def _(v, pg=pg, scq=scq, pol=pol, prevv=bb.ev(f"{tag}prev") if sb == 0 and pg == 0 else None):
                bb.w(v, scq, prevv)
                v.drain()
                v.tensor_scalar(out=wsb[:], in0=usb[:], scalar1=1.0 / 480.0, scalar2=-1.0 / 48.0,
                                op0=ALU.mult, op1=ALU.add)
                v.drain()
                v.tensor_mul(out=wsb[:], in0=wsb[:], in1=usb[:])
                v.drain()
                v.tensor_scalar_add(out=wsb[:], in0=wsb[:], scalar1=0.25)
                v.drain()
                v.tensor_mul(out=wsb[:], in0=wsb[:], in1=ssb[:])
                v.drain()
                v.tensor_scalar_add(out=wsb[:], in0=wsb[:], scalar1=0.5)
                v.drain()
                v.tensor_mul(out=wsb[:], in0=wsb[:], in1=grow[:])
                v.drain()
                for pp in range(4):
                    ins = v.tensor_add(
                        out=probs[32 * pp:32 * pp + 32, pg, :]
                        .rearrange("p (h c) -> p h c", h=NHEAD)[:, :, 32 * pp:32 * pp + 32],
                        in0=wsb[32 * pp:32 * pp + 32, :].rearrange("p (h s) -> p h s", h=NHEAD),
                        in1=brow[32 * pp:32 * pp + 32, :].rearrange("p (h s) -> p h s", h=NHEAD))
                bb.i(ins, pol)
            cmm = bb.mark("sq", f"{pfx}cmm{pg}")

            @bb.on("tensor")
            def _(t_, pg=pg, cmm=cmm, pol=pol, vdone=vdone,
                  oevprev=(bb.get(f"{tag}b{sb - 1}oev{NT - 1}") if pg == 0
                           else bb.get(f"{pfx}oev{pg - 1}"))):
                bb.w(t_, pol, vdone, oevprev)
                for h in range(NHEAD):
                    mm = t_.matmul(out=PSBIG[:, 2048 + h * 64:2048 + (h + 1) * 64],
                                   lhsT=probs[:, pg, h * 128:(h + 1) * 128],
                                   rhs=v_tm[:, pg, h * 64:(h + 1) * 64],
                                   start=True, stop=True)
                bb.i(mm, cmm)
            oev = bb.mark("sv", f"{pfx}oev{pg}")

            @bb.on("vector")
            def _(v, pg=pg, oev=oev, cmm=cmm, stprev=bb.get(f"{tag}b{sb - 1}ost{pg}")):
                bb.w(v, cmm, stprev)
                bb.i(v.scalar_tensor_tensor(out=obuf[:, pg], in0=PSBIG[:, 2048:3072],
                                            scalar=1.0, in1=xblk[:, pg],
                                            op0=ALU.mult, op1=ALU.add), oev)
            ost = bb.mark(f"d_o{tag}{pg}", f"{pfx}ost{pg}")

            @bb.on("sync")
            def _(s, pg=pg, sb=sb, oev=oev, ost=ost):
                bb.w(s, oev)
                bb.i(s.dma_start(out=dst[sb * TB + pg * 128:sb * TB + (pg + 1) * 128, :],
                                 in_=obuf[:, pg]), ost)
        bb.events[f"{pfx}oev1"] = bb.ev(f"{pfx}oev{NT - 1}")

    ctx["ptb_chain"] = bb.ev(f"{tag}b{NSB - 1}trev{NT - 1}")
    ctx[f"{tag}_fin"] = [bb.total(f"d_o{tag}0"), bb.total(f"d_o{tag}1")]


def kernel(L0_attn_x, L0_attn_W_qk, L0_attn_W_v, L0_attn_alibi_slope,
           L0_ffn_w_gate, L0_ffn_w_up, L0_ffn_w_down,
           L1_attn_Wqkv, L1_attn_bqkv, L1_attn_seq_gamma, L1_attn_seq_beta,
           L1_ffn_w_gate, L1_ffn_w_up, L1_ffn_w_down):
    from concourse.bass_utils import run_bass_kernel_spmd
    NB, SL = L0_attn_x.shape[0], L0_attn_x.shape[1]
    bpc = NB // NCORES
    spc = SL // NCORES
    TOK = bpc * SL

    x = np.ascontiguousarray(L0_attn_x, dtype=np.float32)
    common = dict(
        w_qk=np.ascontiguousarray(L0_attn_W_qk, np.float32),
        w_v=np.ascontiguousarray(L0_attn_W_v, np.float32),
        alibi=np.ascontiguousarray(L0_attn_alibi_slope, np.float32).reshape(1),
        w0g=np.ascontiguousarray(L0_ffn_w_gate, np.float32),
        w0u=np.ascontiguousarray(L0_ffn_w_up, np.float32),
        w0d=np.ascontiguousarray(L0_ffn_w_down, np.float32),
        wqkv=np.ascontiguousarray(L1_attn_Wqkv, np.float32),
        bqkv=np.ascontiguousarray(L1_attn_bqkv, np.float32).reshape(3 * D),
        gamma=np.ascontiguousarray(L1_attn_seq_gamma, np.float32).reshape(NHEAD),
        beta=np.ascontiguousarray(L1_attn_seq_beta, np.float32).reshape(NHEAD),
        w1g=np.ascontiguousarray(L1_ffn_w_gate, np.int32),
        w1u=np.ascontiguousarray(L1_ffn_w_up, np.int32),
        w1d=np.ascontiguousarray(L1_ffn_w_down, np.int32),
    )
    in_maps = []
    for c in range(NCORES):
        xc = np.ascontiguousarray(x[c * bpc:(c + 1) * bpc].reshape(TOK, D))
        in_maps.append(dict(x_in=xc, **common))

    nc = build(NB, SL, upto=6)
    res = run_bass_kernel_spmd(nc, in_maps, core_ids=list(range(NCORES)))

    out = np.empty((NB, SL, D), np.float32)
    for c in range(NCORES):
        sh = np.asarray(res.results[c]["y_out"]).reshape(spc, NB, D)
        out[:, c * spc:(c + 1) * spc, :] = sh.transpose(1, 0, 2)
    return out


# version 04
# speedup vs baseline: 1.0404x; 1.0404x over previous
import contextlib
import numpy as np
import concourse.bass as bass
import concourse.mybir as mybir
import concourse.bass_isa as bass_isa

F32 = mybir.dt.float32
BF16 = mybir.dt.bfloat16
I32 = mybir.dt.int32
AF = mybir.ActivationFunctionType
ALU = mybir.AluOpType
AX = mybir.AxisListType.X
RADD = bass_isa.ReduceOp.add
RABSMAX = bass_isa.ReduceOp.absmax

MAGIC = float(3 << 22)
D = 1024
DC = 8
H0, HC0 = 4096, 32
H1, HC1 = 3072, 24
NCORES = 8
EPS = 1e-5
NHEAD = 16


class Bld:
    ENGINES = ("sync", "tensor", "vector", "scalar", "gpsimd")

    def __init__(self, nc):
        self.nc = nc
        self.plans = {e: [] for e in self.ENGINES}
        self.events = {}
        self.cnt = {}
        self.sems = {}

    def on(self, eng):
        def deco(fn):
            self.plans[eng].append(fn)
        return deco

    def mark(self, semname, evname=None):
        k = 16 if semname.startswith("d_") else 1
        self.cnt[semname] = self.cnt.get(semname, 0) + k
        tok = (semname, self.cnt[semname])
        if evname is not None:
            self.events[evname] = tok
        return tok

    def ev(self, name):
        return self.events[name]

    def get(self, name):
        return self.events.get(name)

    def w(self, e, *toks):
        for tok in toks:
            if tok is None:
                continue
            if isinstance(tok, (list, tuple)) and tok and not isinstance(tok[0], str):
                self.w(e, *tok)
                continue
            sem, val = tok
            e.wait_ge(self.sems[sem], val)

    def i(self, instr, tok):
        sem, _ = tok
        instr.then_inc(self.sems[sem], 16 if sem.startswith("d_") else 1)
        return instr

    def total(self, semname):
        return (semname, self.cnt.get(semname, 0))

    def wn(self, e, *names):
        for n in names:
            tok = self.events.get(n)
            if tok is not None:
                sem, val = tok
                e.wait_ge(self.sems[sem], val)


class Arena:
    def __init__(self, tensor, n):
        self.t, self.n, self.off = tensor, n, 0

    def take(self, *shape):
        sz = int(np.prod(shape))
        assert self.off + sz <= self.n, f"arena overflow: {self.off}+{sz}>{self.n}"
        ap = self.t[:, self.off:self.off + sz]
        self.off += sz
        if len(shape) > 1:
            dims = " ".join(f"a{i}" for i in range(len(shape)))
            kw = {f"a{i}": s for i, s in enumerate(shape)}
            ap = ap.rearrange(f"p ({dims}) -> p {dims}", **kw)
        return ap


WBARG, WBARU, WBARD = 0, 1, 2
SCALEG, SCALEU, SCALED = 3, 4, 5
SLOPE = 6
RTMP = 8
RSTAT = 32
RBQ = 128
RGAM = 144
RBET = 160
RASUM = 224
RIOTA = 352


def build(nbatch, seqlen, upto=6):
    bpc = nbatch // NCORES
    TOK = bpc * seqlen
    spc = seqlen // NCORES
    TB = 256
    NBLK = TOK // TB
    NT = TB // 128
    assert TOK % TB == 0 and seqlen % NCORES == 0

    nc = bass.Bass("TRN2", target_bir_lowering=False, debug=False)
    bb = Bld(nc)

    io = {}

    def inp(name, shape, dt=F32):
        io[name] = nc.dram_tensor(name, shape, dt, kind="ExternalInput").ap()
    inp("x_in", [TOK, D])
    inp("w_qk", [D, D])
    inp("w_v", [D, D])
    inp("alibi", [1])
    inp("w0g", [H0, D])
    inp("w0u", [H0, D])
    inp("w0d", [D, H0])
    inp("wqkv", [3 * D, D])
    inp("bqkv", [3 * D])
    inp("gamma", [NHEAD])
    inp("beta", [NHEAD])
    inp("w1g", [H1, D], I32)
    inp("w1u", [H1, D], I32)
    inp("w1d", [D, H1], I32)
    io["y_out"] = nc.dram_tensor("y_out", [TOK, D], F32, kind="ExternalOutput").ap()

    sc = {}

    def scratch(name, shape, dt=BF16, **kw):
        sc[name] = nc.dram_tensor(name, shape, dt, kind="Internal", **kw).ap()
    scratch("w0gT", [HC0 // 4, 128, 4, DC, 128])
    scratch("w0uT", [HC0 // 4, 128, 4, DC, 128])
    scratch("w0dT", [HC0 // 4, 128, 4, D])
    scratch("wvT", [DC // 4, 128, 4, D])
    scratch("wqkb", [DC, 128, DC, 128])
    scratch("wqT", [DC // 4, 128, 4, DC, 128])
    scratch("wkT", [DC // 4, 128, 4, DC, 128])
    scratch("wvT1", [DC // 4, 128, 4, D])
    scratch("w1gT", [HC1 // 4, 128, 4, DC, 128])
    scratch("w1uT", [HC1 // 4, 128, 4, DC, 128])
    scratch("w1dT", [HC1 // 4, 128, 4, D])
    scratch("s_a", [TOK, D], F32)
    scratch("s_b", [TOK, D], F32)
    scratch("cc_src", [TOK, D], F32)
    half = TOK // 2
    scratch("cc_d0", [NCORES * half, D], F32, addr_space="Shared")
    scratch("cc_d1", [NCORES * half, D], F32, addr_space="Shared")
    scratch("s_g", [TOK, D], F32)
    scratch("s_d", [TOK, D], F32)
    scratch("s_e", [TOK, D], F32)

    with contextlib.ExitStack() as stack:
        F32A = stack.enter_context(nc.sbuf_tensor("F32A", [128, 14336], F32))
        BF16A = stack.enter_context(nc.sbuf_tensor("BF16A", [128, 69632], BF16))
        ROWS = stack.enter_context(nc.sbuf_tensor("ROWS", [128, 640], F32))
        ONESF = stack.enter_context(nc.sbuf_tensor("ONESF", [128, 128], F32))
        IDB = stack.enter_context(nc.sbuf_tensor("IDB", [128, 128], BF16))
        IDF = stack.enter_context(nc.sbuf_tensor("IDF", [128, 128], F32))
        PSBIG = stack.enter_context(nc.psum_tensor("PSBIG", [128, 3584], F32))
        PTB = stack.enter_context(nc.psum_tensor("PTB", [128, 1024], BF16))

        ctx = dict(nc=nc, bb=bb, io=io, sc=sc, F32A=F32A, BF16A=BF16A, ROWS=ROWS, ONESF=ONESF,
                   IDB=IDB, IDF=IDF, PSBIG=PSBIG, PTB=PTB,
                   TOK=TOK, TB=TB, NBLK=NBLK, NT=NT, bpc=bpc, seqlen=seqlen,
                   spc=spc, nbatch=nbatch, half=half, upto=upto)

        plan_consts(ctx)
        plan_prologue(ctx)
        plan_phases(ctx)

        for n in sorted(bb.cnt.keys()):
            bb.sems[n] = stack.enter_context(nc.semaphore(n))

        blk = stack.enter_context(nc.Block())
        for eng in Bld.ENGINES:
            def runner(e, eng=eng):
                for f in bb.plans[eng]:
                    f(e)
            getattr(blk, eng)(runner)

    return nc


def bank(ctx, i, w=512):
    return ctx["PSBIG"][:, i * 512:i * 512 + w]


def plan_consts(ctx):
    bb, ROWS, IDB, IDF = ctx["bb"], ctx["ROWS"], ctx["IDB"], ctx["IDF"]
    ONESF = ctx["ONESF"]
    t_iota = bb.mark("sg", "iota_id")

    @bb.on("gpsimd")
    def _(g):
        bb.i(g.iota(ROWS[:, RIOTA:RIOTA + 128], pattern=[[1, 128]], base=0,
                    channel_multiplier=-1, allow_small_or_imprecise_dtypes=True), t_iota)
    t_id = bb.mark("sv", "consts")

    @bb.on("vector")
    def _(v):
        bb.w(v, bb.ev("iota_id"))
        v.tensor_single_scalar(out=IDB[:], in_=ROWS[:, RIOTA:RIOTA + 128], scalar=0.0, op=ALU.is_equal)
        v.tensor_single_scalar(out=IDF[:], in_=ROWS[:, RIOTA:RIOTA + 128], scalar=0.0, op=ALU.is_equal)
        v.memset(ONESF[:], 1.0)
        bb.i(v.engine_nop(), t_id)




def plan_prologue(ctx):
    bb, io, sc = ctx["bb"], ctx["io"], ctx["sc"]
    ROWS, PTB, IDB = ctx["ROWS"], ctx["PTB"], ctx["IDB"]
    upto = ctx["upto"]
    fa = Arena(ctx["F32A"], 14336)
    ba = Arena(ctx["BF16A"], 69632)
    wld = fa.take(2, 2048)
    tern = ba.take(2, 2048)
    asm = ba.take(2, 1024)
    resid = ba.take(8, 4096)

    seq = [0]
    aseq = [0]

    def asm_slot():
        k = aseq[0]
        aseq[0] += 1
        slot = k % 2
        prev_store = bb.get(f"pasmst_s{slot}_{k // 2 - 1}")
        return slot, k, prev_store

    def load_tile(src_ap, width, extra_wait=None, cast=False):
        i = seq[0]
        seq[0] += 1
        par = i % 2
        eng = "gpsimd" if cast else "sync"
        freev = bb.get(f"pwfree{i - 2}")

        if hasattr(src_ap, "shape") and len(src_ap.shape) == 3:
            extra_toks = [bb.mark(f"d_pg{par}" if cast else f"d_pw{par}")
                          for _ in range(src_ap.shape[0] - 1)]
            tok = bb.mark(f"d_pg{par}" if cast else f"d_pw{par}", f"pwld{i}")
        else:
            extra_toks = []
            tok = bb.mark(f"d_pg{par}" if cast else f"d_pw{par}", f"pwld{i}")

        @bb.on(eng)
        def _(s, src_ap=src_ap, par=par, width=width, tok=tok, freev=freev,
              extra_wait=extra_wait, extra_toks=extra_toks):
            bb.w(s, freev, extra_wait)
            if len(src_ap.shape) == 3:
                ng = src_ap.shape[0]
                rp = 128 // ng
                for h in range(ng - 1):
                    bb.i(s.dma_start(out=wld[h * rp:(h + 1) * rp, par, 0:width], in_=src_ap[h]),
                         extra_toks[h])
                bb.i(s.dma_start(out=wld[(ng - 1) * rp:128, par, 0:width], in_=src_ap[ng - 1]), tok)
            else:
                bb.i(s.dma_start(out=wld[:, par, 0:width], in_=src_ap), tok)
        return par, f"pwld{i}", i

    def q_tern(v, par, width, scale_col):
        r = ROWS
        v.tensor_scalar(out=wld[:, par, 0:width], in0=wld[:, par, 0:width],
                        scalar1=r[:, scale_col:scale_col + 1], scalar2=MAGIC, op0=ALU.mult, op1=ALU.add)
        v.drain()
        v.tensor_scalar(out=wld[:, par, 0:width], in0=wld[:, par, 0:width],
                        scalar1=MAGIC, scalar2=1.0, op0=ALU.subtract, op1=ALU.min)
        v.drain()

    for name, src, W, ntile, c0 in (("wg", io["w0g"], D, HC0, 0), ("wu", io["w0u"], D, HC0, 32)):
        for i in range(ntile):
            par, ev, si = load_tile(src[i * 128:(i + 1) * 128, :], W)
            tok = bb.mark("sv", f"pwfree{si}")

            @bb.on("vector")
            def _(v, par=par, W=W, c=c0 + i, ev=ev, tok=tok):
                bb.w(v, bb.ev(ev))
                bb.i(v.reduce_sum(out=ROWS[:, RASUM + c:RASUM + c + 1], in_=wld[:, par, 0:W],
                                  axis=AX, apply_absolute_value=True), tok)
    for i in range(8):
        for hh in range(2):
            par, ev, si = load_tile(io["w0d"][i * 128:(i + 1) * 128, hh * 2048:(hh + 1) * 2048], 2048)
            tok = bb.mark("sv", f"pwfree{si}")

            @bb.on("vector")
            def _(v, par=par, c=64 + i * 2 + hh, ev=ev, tok=tok):
                bb.w(v, bb.ev(ev))
                bb.i(v.reduce_sum(out=ROWS[:, RASUM + c:RASUM + c + 1], in_=wld[:, par, 0:2048],
                                  axis=AX, apply_absolute_value=True), tok)

    sums_done = bb.total("sv")
    t1 = bb.mark("sv", "psum1")

    @bb.on("vector")
    def _(v):
        bb.w(v, sums_done)
        v.drain()
        r = ROWS
        v.reduce_sum(out=r[:, RTMP:RTMP + 1], in_=r[:, RASUM:RASUM + 32], axis=AX)
        v.reduce_sum(out=r[:, RTMP + 1:RTMP + 2], in_=r[:, RASUM + 32:RASUM + 64], axis=AX)
        v.reduce_sum(out=r[:, RTMP + 2:RTMP + 3], in_=r[:, RASUM + 64:RASUM + 80], axis=AX)
        bb.i(v.engine_nop(), t1)
    ONESF = ctx["ONESF"]
    PSB = ctx["PSBIG"]
    t2a = bb.mark("sq", "psum2a")

    @bb.on("tensor")
    def _(t):
        bb.w(t, bb.ev("psum1"), bb.ev("consts"))
        bb.i(t.matmul(out=PSB[:, 3072:3072 + 3][0:1], lhsT=ONESF[:, 0:1], rhs=ROWS[:, RTMP:RTMP + 3],
                      start=True, stop=True), t2a)
    t2b = bb.mark("sv", "psum2b")

    @bb.on("vector")
    def _(v):
        bb.w(v, bb.ev("psum2a"))
        bb.i(v.tensor_copy(out=ROWS[0:1, RTMP + 16:RTMP + 19], in_=PSB[0:1, 3072:3075]), t2b)
    t2c = bb.mark("sq", "psum2c")

    @bb.on("tensor")
    def _(t):
        bb.w(t, bb.ev("psum2b"))
        bb.i(t.matmul(out=PSB[:, 3072:3072 + 3], lhsT=ONESF[0:1, 0:128], rhs=ROWS[0:1, RTMP + 16:RTMP + 19],
                      start=True, stop=True), t2c)
    t2 = bb.mark("sv", "psum2")

    @bb.on("vector")
    def _(v):
        bb.w(v, bb.ev("psum2c"))
        bb.i(v.tensor_copy(out=ROWS[:, RTMP + 4:RTMP + 7], in_=PSB[:, 3072:3075]), t2)
    t3 = bb.mark("sv", "wscales")

    @bb.on("vector")
    def _(v):
        bb.w(v, bb.ev("psum2"))
        v.drain()
        r = ROWS
        inv_n = 1.0 / (H0 * D)
        for wb, src in ((WBARG, RTMP + 4), (WBARU, RTMP + 5), (WBARD, RTMP + 6)):
            v.tensor_scalar(out=r[:, wb:wb + 1], in0=r[:, src:src + 1], scalar1=inv_n,
                            scalar2=EPS, op0=ALU.mult, op1=ALU.max)
        v.drain()
        for wb, scl in ((WBARG, SCALEG), (WBARU, SCALEU), (WBARD, SCALED)):
            v.reciprocal(out=r[:, scl:scl + 1], in_=r[:, wb:wb + 1])
        v.drain()
        bb.i(v.engine_nop(), t3)
    wsc = bb.ev("wscales")

    def stationary(src_fn, dstT, ntile, mode, scale_col=None, cast=False):
        for i in range(ntile):
            par, ev, si = load_tile(src_fn(i), D, extra_wait=(wsc if mode == "tern" else None), cast=cast)
            tq = bb.mark("sv", f"pwfree{si}")

            @bb.on("vector")
            def _(v, par=par, ev=ev, tq=tq, mode=mode, scale_col=scale_col,
                  ternfree=bb.get(f"ternlast{par}")):
                bb.w(v, bb.ev(ev), ternfree)
                if mode == "tern":
                    q_tern(v, par, D, scale_col)
                    bb.i(v.tensor_scalar(out=tern[:, par, 0:D], in0=wld[:, par, 0:D],
                                         scalar1=-1.0, scalar2=None, op0=ALU.max), tq)
                else:
                    bb.i(v.tensor_copy(out=tern[:, par, 0:D], in_=wld[:, par, 0:D]), tq)
            tt = bb.mark("sq")
            bb.events[f"ternlast{par}"] = tt

            @bb.on("tensor")
            def _(t, par=par, tq=tq, tt=tt, prev_evac=bb.get("pasm_prev")):
                bb.w(t, tq, bb.ev("consts"), prev_evac)
                for j in range(DC):
                    mm = t.transpose(out=PTB[:, j * 128:(j + 1) * 128],
                                     in_=tern[:, par, j * 128:(j + 1) * 128], identity=IDB[:])
                bb.i(mm, tt)
            apar, ak, prev_store = asm_slot()
            te = bb.mark("sv")
            bb.events["pasm_prev"] = te

            @bb.on("vector")
            def _(v, tt=tt, te=te, apar=apar, stfree=prev_store):
                bb.w(v, tt, stfree)
                bb.i(v.tensor_copy(out=asm[:, apar, :], in_=PTB[:]), te)
            td = bb.mark(f"d_pst{apar}", f"pasmst_s{apar}_{ak // 2}")

            @bb.on("sync")
            def _(s, i=i, dstT=dstT, te=te, td=td, apar=apar):
                bb.w(s, te)
                bb.i(s.dma_start(out=dstT[i // 4, :, i % 4], in_=asm[:, apar, :].rearrange("p (dc b) -> p dc b", dc=DC)), td)

    def moving(src_ap, dstT, n_dtile, W, n_oc, mode, scale_col=None, cast=False, src_fn=None):
        for i in range(n_dtile):
            nh = (W + 2047) // 2048
            for hh in range(nh):
                w0 = hh * 2048
                ww = min(W, w0 + 2048) - w0
                if src_fn is not None:
                    sap = src_fn(i)
                    assert ww == W
                else:
                    sap = src_ap[i * 128:(i + 1) * 128, w0:w0 + ww]
                par, ev, si = load_tile(sap, ww,
                                        extra_wait=(wsc if mode == "tern" else None), cast=cast)
                tq = bb.mark("sv", f"pwfree{si}")

                @bb.on("vector")
                def _(v, par=par, ev=ev, tq=tq, i=i, w0=w0, ww=ww, mode=mode, scale_col=scale_col):
                    bb.w(v, bb.ev(ev))
                    if mode == "tern":
                        q_tern(v, par, ww, scale_col)
                        bb.i(v.tensor_scalar(out=resid[:, i, w0:w0 + ww], in0=wld[:, par, 0:ww],
                                             scalar1=-1.0, scalar2=None, op0=ALU.max), tq)
                    else:
                        bb.i(v.tensor_copy(out=resid[:, i, w0:w0 + ww], in_=wld[:, par, 0:ww]), tq)
        qdone = bb.total("sv")
        for oc in range(n_oc):
            tt = bb.mark("sq")

            @bb.on("tensor")
            def _(t, oc=oc, tt=tt, prev_evac=bb.get("pasm_prev"), qdone=qdone, n_dtile=n_dtile):
                bb.w(t, qdone, bb.ev("consts"), prev_evac)
                for i in range(n_dtile):
                    mm = t.transpose(out=PTB[:, i * 128:(i + 1) * 128],
                                     in_=resid[:, i, oc * 128:(oc + 1) * 128], identity=IDB[:])
                bb.i(mm, tt)
            apar, ak, prev_store = asm_slot()
            te = bb.mark("sv")
            bb.events["pasm_prev"] = te

            @bb.on("vector")
            def _(v, tt=tt, te=te, apar=apar, n_dtile=n_dtile, stfree=prev_store):
                bb.w(v, tt, stfree)
                bb.i(v.tensor_copy(out=asm[:, apar, 0:n_dtile * 128], in_=PTB[:, 0:n_dtile * 128]), te)
            td = bb.mark(f"d_pst{apar}", f"pasmst_s{apar}_{ak // 2}")

            @bb.on("sync")
            def _(s, oc=oc, dstT=dstT, te=te, td=td, apar=apar, n_dtile=n_dtile):
                bb.w(s, te)
                bb.i(s.dma_start(out=dstT[oc // 4, :, oc % 4, 0:n_dtile * 128], in_=asm[:, apar, 0:n_dtile * 128]), td)

    stationary(lambda i: io["w0g"][i * 128:(i + 1) * 128, :], sc["w0gT"], HC0, "tern", SCALEG)
    stationary(lambda i: io["w0u"][i * 128:(i + 1) * 128, :], sc["w0uT"], HC0, "tern", SCALEU)
    moving(io["w0d"], sc["w0dT"], DC, H0, HC0, "tern", SCALED)

    if upto >= 2:
        for i in range(DC):
            par, ev, si = load_tile(io["w_qk"][i * 128:(i + 1) * 128, :], D)
            tq = bb.mark("sv", f"pwfree{si}")

            @bb.on("vector")
            def _(v, par=par, ev=ev, tq=tq, ternfree=bb.get(f"ternlast{par}")):
                bb.w(v, bb.ev(ev), ternfree)
                bb.i(v.tensor_copy(out=tern[:, par, 0:D], in_=wld[:, par, 0:D]), tq)
            td = bb.mark(f"d_pqk{si % 2}", f"pqkst{si}")
            bb.events[f"ternlast{par}"] = td

            @bb.on("sync")
            def _(s, i=i, par=par, tq=tq, td=td):
                bb.w(s, tq)
                bb.i(s.dma_start(out=sc["wqkb"].rearrange("jc a dc b -> dc a jc b")[i],
                                 in_=tern[:, par, 0:D].rearrange("p (jc b) -> p jc b", jc=DC)), td)
        moving(io["w_v"], sc["wvT"], DC, D, DC, "cast")
        tda = bb.mark("d_pb", "alibi_ld")

        @bb.on("sync")
        def _(s):
            bb.i(s.dma_start(out=ROWS[0:1, SLOPE:SLOPE + 1], in_=io["alibi"][:, None]), tda)
        tga1 = bb.mark("sq", "alibi_mm")

        @bb.on("tensor")
        def _(t):
            bb.w(t, bb.ev("alibi_ld"), bb.ev("consts"))
            bb.i(t.matmul(out=ctx["PSBIG"][:, 3072:3073], lhsT=ONESF[0:1, 0:128],
                          rhs=ROWS[0:1, SLOPE:SLOPE + 1], start=True, stop=True), tga1)
        tga = bb.mark("sv", "alibi_bc")

        @bb.on("vector")
        def _(v):
            bb.w(v, bb.ev("alibi_mm"))
            bb.i(v.tensor_copy(out=ROWS[:, SLOPE:SLOPE + 1], in_=ctx["PSBIG"][:, 3072:3073]), tga)

    if upto >= 4:
        stationary(lambda i: io["w1g"][i * 128:(i + 1) * 128, :], sc["w1gT"], HC1, "cast", cast=True)
        stationary(lambda i: io["w1u"][i * 128:(i + 1) * 128, :], sc["w1uT"], HC1, "cast", cast=True)
        moving(io["w1d"], sc["w1dT"], DC, H1, HC1, "cast", cast=True)

    _esel = 3
    if upto >= 5 or upto in (45, 50):
        def qkv_rows(kind, i):
            off = {"q": 0, "k": 1, "v": 2}[kind]
            return (io["wqkv"].rearrange("(h s r) d -> h s r d", h=NHEAD, s=3, r=64)
                    [2 * i:2 * i + 2, off])
        if _esel & 1:
            stationary(lambda i: qkv_rows("q", i), sc["wqT"], DC, "cast")
            stationary(lambda i: qkv_rows("k", i), sc["wkT"], DC, "cast")
            vsrc = io["wqkv"].rearrange("(h s r) d -> h s r d", h=NHEAD, s=3, r=64)[:, 2]
            moving(None, sc["wvT1"], DC, D, DC, "cast", src_fn=lambda i: vsrc[2 * i:2 * i + 2])
        for kind, cbase, off in (("q", 0, 0), ("k", 8, 1)) if _esel & 2 else ():
            for jc in range(DC):
                td0 = bb.mark("d_pb")
                td = bb.mark("d_pb", f"bq_{kind}{jc}")

                @bb.on("sync")
                def _(s, jc=jc, off=off, cbase=cbase, td0=td0, td=td):
                    srca = (io["bqkv"].rearrange("(h s r) -> h s r", h=NHEAD, s=3, r=64)
                            [2 * jc:2 * jc + 2, off])
                    bb.i(s.dma_start(out=ROWS[0:64, RBQ + cbase + jc:RBQ + cbase + jc + 1],
                                     in_=srca[0][:, None]), td0)
                    bb.i(s.dma_start(out=ROWS[64:128, RBQ + cbase + jc:RBQ + cbase + jc + 1],
                                     in_=srca[1][:, None]), td)
        if not (_esel & 2):
            ctx["prologue_sv"] = bb.total("sv")
            ctx["prologue_st"] = bb.total("d_pst0")
            ctx["prologue_st1"] = bb.total("d_pst1")
            ctx["prologue_fin"] = [bb.total(n) for n in
                                   ("d_pst0", "d_pst1", "d_pqk0", "d_pqk1", "d_pb") if bb.cnt.get(n)]
            _prol_quiet(ctx)
            return
        tg0 = bb.mark("d_pb")
        tg1 = bb.mark("d_pb", "gb_ld")

        @bb.on("sync")
        def _(s):
            bb.i(s.dma_start(out=ROWS[0:1, RGAM:RGAM + NHEAD], in_=io["gamma"][None, :]), tg0)
            bb.i(s.dma_start(out=ROWS[0:1, RBET:RBET + NHEAD], in_=io["beta"][None, :]), tg1)
        tg2a = bb.mark("sq", "gb_mm")

        @bb.on("tensor")
        def _(t):
            bb.w(t, bb.ev("gb_ld"), bb.ev("consts"))
            bb.i(t.matmul(out=ctx["PSBIG"][:, 3072:3072 + 2 * NHEAD], lhsT=ONESF[0:1, 0:128],
                          rhs=ROWS[0:1, RGAM:RGAM + 2 * NHEAD], start=True, stop=True), tg2a)
        tg2 = bb.mark("sv", "gb_bc")

        @bb.on("vector")
        def _(v):
            bb.w(v, bb.ev("gb_mm"))
            bb.i(v.tensor_copy(out=ROWS[:, RGAM:RGAM + 2 * NHEAD], in_=ctx["PSBIG"][:, 3072:3072 + 2 * NHEAD]), tg2)

    ctx["prologue_sv"] = bb.total("sv")
    ctx["prologue_st"] = bb.total("d_pst0")
    ctx["prologue_st1"] = bb.total("d_pst1")
    ctx["prologue_fin"] = [bb.total(n) for n in
                           ("d_pst0", "d_pst1", "d_pqk0", "d_pqk1", "d_pb") if bb.cnt.get(n)]
    _prol_quiet(ctx)


def _prol_quiet(ctx):
    bb = ctx["bb"]
    pq_v = bb.mark("sv", "prolq_v")

    @bb.on("vector")
    def _(v, pq_v=pq_v):
        bb.i(v.engine_nop(), pq_v)
    pq_a = bb.mark("sa", "prolq_a")

    @bb.on("scalar")
    def _(a, pq_a=pq_a):
        bb.i(a.nop(), pq_a)
    ctx["psum_quiet"] = [pq_v, pq_a]



def plan_ffn0(ctx, tag, src, dst):
    bb, sc = ctx["bb"], ctx["sc"]
    ROWS, PTB, IDB, IDF, PSBIG = ctx["ROWS"], ctx["PTB"], ctx["IDB"], ctx["IDF"], ctx["PSBIG"]
    NBLK, TB, NT = ctx["NBLK"], ctx["TB"], ctx["NT"]

    fa = Arena(ctx["F32A"], 14336)
    xblk = fa.take(NT, D)
    prod = fa.take(HC0, TB)
    gbuf = fa.take(2, TB)
    obuf = fa.take(2, D)
    m2c = fa.take(TB)
    r2 = fa.take(TB)
    ba = Arena(ctx["BF16A"], 69632)
    prodflat = prod.rearrange("p h tb -> p (h tb)")
    xcbuf = prodflat[:, 4 * D:5 * D]
    xq_tm = ba.take(NT, D)
    xq_fm = ba.take(DC, TB)
    xq2 = ba.take(HC0, TB)
    wgs = ba.take(2, 4, DC, 128)
    wus = ba.take(2, 4, DC, 128)
    wds = ba.take(2, 4, D)

    CS = RSTAT
    WG, WU, WD = sc["w0gT"], sc["w0uT"], sc["w0dT"]
    NGB = HC0 // 4
    NDG = HC0 // 4

    prol_sv = ctx.get("prologue_sv")
    prol_st = ctx.get("prologue_st")

    for b in range(NBLK):
        xl = bb.mark(f"d_x{tag}", f"{tag}xl{b}")

        @bb.on("sync")
        def _(s, b=b, xl=xl, freev=bb.get(f"{tag}xfree{b - 1}"),
              prol=ctx.get("prologue_fin") if b == 0 else None,
              pprev=ctx.get(f"{tag}_prev_done") if b == 0 else None):
            bb.w(s, freev, prol, pprev)
            bb.i(s.dma_start(out=xblk,
                             in_=src[b * TB:(b + 1) * TB, :].rearrange("(a p) d -> p a d", p=128)), xl)

        for g in range(NGB):
            gi = b * NGB + g
            par = gi % 2
            wl0 = bb.mark(f"d_w{tag}{par}")
            wl = bb.mark(f"d_w{tag}{par}", f"{tag}wl{gi}")

            @bb.on("sync")
            def _(s, g=g, par=par, wl0=wl0, wl=wl, fname=f"{tag}wfree{gi - 2}",
                  pprev=ctx.get(f"{tag}_prev_done") if gi < 2 else None):
                bb.wn(s, fname)
                bb.w(s, pprev)
                bb.i(s.dma_start(out=wgs[:, par], in_=WG[g]), wl0)
                bb.i(s.dma_start(out=wus[:, par], in_=WU[g]), wl)

        for tp in range(NT // 2):
            for g in range(NDG):
                di = (b * (NT // 2) + tp) * NDG + g
                par = di % 2
                dl = bb.mark(f"d_d{tag}{par}", f"{tag}dl{di}")

                @bb.on("sync")
                def _(s, g=g, par=par, dl=dl, fname=f"{tag}dfree{di - 2}",
                      pprev=ctx.get(f"{tag}_prev_done") if di < 2 else None):
                    bb.wn(s, fname)
                    bb.w(s, pprev)
                    bb.i(s.dma_start(out=wds[:, par], in_=WD[g]), dl)

        for t in range(NT):
            st = bb.mark("sv", f"{tag}sum{b}_{t}")

            @bb.on("vector")
            def _(v, t=t, xl=xl, st=st):
                bb.w(v, xl)
                bb.i(v.reduce_sum(out=ROWS[:, CS + t:CS + t + 1], in_=xblk[:, t], axis=AX), st)
            sq = bb.mark("sa", f"{tag}sq{b}_{t}")

            @bb.on("scalar")
            def _(a, t=t, xl=xl, sq=sq, vqprev=bb.get(f"{tag}qq{b - 1}_1") if t == 0 else None,
                  prol=prol_sv if (b == 0 and t == 0) else None):
                bb.w(a, xl, vqprev, prol)
                bb.i(a.activation(out=prodflat[:, t * D:(t + 1) * D], in_=xblk[:, t], func=AF.Square,
                                  accum_out=ROWS[:, CS + 4 + t:CS + 5 + t]), sq)
        r1 = bb.mark("sv", f"{tag}rows1_{b}")

        @bb.on("vector")
        def _(v, r1=r1, sa_need=bb.ev(f"{tag}sq{b}_{NT - 1}")):
            bb.w(v, sa_need)
            v.drain()
            r = ROWS
            c = CS
            v.tensor_scalar_mul(out=r[:, c:c + NT], in0=r[:, c:c + NT], scalar1=1.0 / D)
            v.drain()
            v.tensor_scalar_mul(out=r[:, c + 12:c + 12 + NT], in0=r[:, c:c + NT], scalar1=-1.0)
            v.tensor_mul(out=r[:, c + 16:c + 16 + NT], in0=r[:, c:c + NT], in1=r[:, c:c + NT])
            v.drain()
            v.scalar_tensor_tensor(out=r[:, c + 16:c + 16 + NT], in0=r[:, c + 4:c + 4 + NT], scalar=1.0 / D,
                                   in1=r[:, c + 16:c + 16 + NT], op0=ALU.mult, op1=ALU.subtract)
            v.drain()
            v.tensor_scalar_add(out=r[:, c + 16:c + 16 + NT], in0=r[:, c + 16:c + 16 + NT], scalar1=EPS)
            v.drain()
            bb.i(v.engine_nop(), r1)
        r2e = bb.mark("sa", f"{tag}rows2_{b}")

        @bb.on("scalar")
        def _(a, r1=r1, r2e=r2e):
            bb.w(a, r1)
            bb.i(a.activation(out=ROWS[:, CS + 16:CS + 16 + NT], in_=ROWS[:, CS + 16:CS + 16 + NT],
                              func=AF.Sqrt), r2e)
        r3 = bb.mark("sv", f"{tag}rows3_{b}")

        @bb.on("vector")
        def _(v, r2e=r2e, r3=r3):
            bb.w(v, r2e)
            v.reciprocal(out=ROWS[:, CS + 16:CS + 16 + NT], in_=ROWS[:, CS + 16:CS + 16 + NT])
            v.drain()
            bb.i(v.engine_nop(), r3)

        for t in range(NT):
            xc = bb.mark("sa", f"{tag}xc{b}_{t}")

            @bb.on("scalar")
            def _(a, t=t, xc=xc, r1=bb.ev(f"{tag}rows1_{b}"), qzprev=bb.get(f"{tag}qz{b}_{t - 1}")):
                bb.w(a, r1, qzprev)
                bb.i(a.activation(out=xcbuf[:], in_=xblk[:, t], func=AF.Identity,
                                  bias=ROWS[:, CS + 12 + t:CS + 13 + t], scale=1.0), xc)
            mm_ = bb.mark("sv", f"{tag}mx{b}_{t}")

            @bb.on("vector")
            def _(v, t=t, xc=xc, mm_=mm_):
                bb.w(v, xc)
                bb.i(v.reduce_max(out=ROWS[:, CS + 20 + t:CS + 21 + t], in_=xcbuf[:], axis=AX,
                                  apply_absolute_value=True), mm_)
            qz = bb.mark("sv", f"{tag}qz{b}_{t}")

            @bb.on("vector")
            def _(v, t=t, qz=qz, r3=bb.ev(f"{tag}rows3_{b}"), trprev=bb.get(f"{tag}tr{b - 1}_{t}")):
                bb.w(v, r3, trprev)
                v.drain()
                r = ROWS
                c = CS
                v.tensor_mul(out=r[:, c + 24 + t:c + 25 + t], in0=r[:, c + 16 + t:c + 17 + t],
                             in1=r[:, c + 20 + t:c + 21 + t])
                v.drain()
                v.tensor_scalar_max(out=r[:, c + 24 + t:c + 25 + t], in0=r[:, c + 24 + t:c + 25 + t],
                                    scalar1=EPS)
                v.drain()
                v.tensor_scalar_mul(out=r[:, c + 28 + t:c + 29 + t], in0=r[:, c + 24 + t:c + 25 + t],
                                    scalar1=1.0 / 127.0)
                v.reciprocal(out=r[:, c + 24 + t:c + 25 + t], in_=r[:, c + 24 + t:c + 25 + t])
                v.drain()
                v.tensor_scalar_mul(out=r[:, c + 24 + t:c + 25 + t], in0=r[:, c + 24 + t:c + 25 + t],
                                    scalar1=127.0)
                v.drain()
                v.tensor_mul(out=r[:, c + 24 + t:c + 25 + t], in0=r[:, c + 24 + t:c + 25 + t],
                             in1=r[:, c + 16 + t:c + 17 + t])
                v.drain()
                v.tensor_mul(out=r[:, c + 8 + t:c + 9 + t], in0=r[:, c + 12 + t:c + 13 + t],
                             in1=r[:, c + 24 + t:c + 25 + t])
                v.drain()
                v.tensor_scalar(out=xcbuf[:], in0=xblk[:, t], scalar1=r[:, c + 24 + t:c + 25 + t],
                                scalar2=r[:, c + 8 + t:c + 9 + t], op0=ALU.mult, op1=ALU.add)
                v.drain()
                v.tensor_scalar(out=xcbuf[:], in0=xcbuf[:], scalar1=MAGIC, scalar2=MAGIC,
                                op0=ALU.add, op1=ALU.subtract)
                v.drain()
                bb.i(v.tensor_scalar_mul(out=xq_tm[:, t], in0=xcbuf[:],
                                         scalar1=r[:, c + 28 + t:c + 29 + t]), qz)
            tr = bb.mark("sq", f"{tag}tr{b}_{t}")

            @bb.on("tensor")
            def _(t_, t=t, qz=qz, tr=tr, prevev=bb.get(f"{tag}trev{b}_{t - 1}")
                  or bb.get(f"{tag}trev{b - 1}_{NT - 1}") or ctx.get("ptb_chain"),
                  pquiet=ctx.pop("psum_quiet", None)):
                bb.w(t_, qz, bb.ev("consts"), prevev, pquiet)
                for j in range(DC):
                    mm = t_.transpose(out=PTB[:, j * 128:(j + 1) * 128],
                                      in_=xq_tm[:, t, j * 128:(j + 1) * 128], identity=IDB[:])
                bb.i(mm, tr)
            tre = bb.mark("sa", f"{tag}trev{b}_{t}")

            @bb.on("scalar")
            def _(a, t=t, tr=tr, tre=tre, mmu_prev=bb.get(f"{tag}mmu{b - 1}_{HC0 - 1}") if t == 0 else None):
                bb.w(a, tr, mmu_prev)
                bb.i(a.copy(out=xq_fm[:, :, t * 128:(t + 1) * 128],
                            in_=PTB[:].rearrange("p (dc c) -> p dc c", dc=DC)), tre)

        xq_ready = bb.ev(f"{tag}trev{b}_{NT - 1}")

        for hc in range(HC0):
            g = hc // 4
            gi = b * NGB + g
            par = gi % 2
            sl = hc % 4
            pg, pu = (0, 1) if hc % 2 == 0 else (4, 5)
            mmg = bb.mark("sq", f"{tag}mmg{b}_{hc}")
            mmu = bb.mark("sq", f"{tag}mmu{b}_{hc}")

            @bb.on("tensor")
            def _(t_, hc=hc, par=par, sl=sl, pg=pg, pu=pu, mmg=mmg, mmu=mmu,
                  wl=bb.ev(f"{tag}wl{gi}"), xq_ready=xq_ready,
                  bankfree=bb.get(f"{tag}prd{b}_{hc - 2}"),
                  dnprev=bb.get(f"{tag}dnall{b - 1}") if hc == 0 else None):
                bb.w(t_, wl, xq_ready, bankfree, dnprev)
                for d in range(DC):
                    mm = t_.matmul(out=bank(ctx, pg, TB), lhsT=wgs[:, par, sl, d], rhs=xq_fm[:, d],
                                   start=(d == 0), stop=(d == DC - 1))
                bb.i(mm, mmg)
                for d in range(DC):
                    mm = t_.matmul(out=bank(ctx, pu, TB), lhsT=wus[:, par, sl, d], rhs=xq_fm[:, d],
                                   start=(d == 0), stop=(d == DC - 1))
                bb.i(mm, mmu)
            if hc % 4 == 3:
                bb.events[f"{tag}wfree{gi}"] = mmu
            sil = bb.mark("sa", f"{tag}sil{b}_{hc}")

            @bb.on("scalar")
            def _(a, hc=hc, pg=pg, sil=sil, mmg=mmg, gfree=bb.get(f"{tag}prd{b}_{hc - 2}")):
                bb.w(a, mmg, gfree)
                bb.i(a.activation(out=gbuf[:, hc % 2], in_=bank(ctx, pg, TB), func=AF.Silu,
                                  scale=ROWS[:, WBARG:WBARG + 1]), sil)
            prd = bb.mark("sv", f"{tag}prd{b}_{hc}")

            @bb.on("vector")
            def _(v, hc=hc, pu=pu, sil=sil, mmu=mmu, prd=prd):
                bb.w(v, sil, mmu)
                bb.i(v.scalar_tensor_tensor(out=prod[:, hc], in0=bank(ctx, pu, TB),
                                            scalar=ROWS[:, WBARU:WBARU + 1], in1=gbuf[:, hc % 2],
                                            op0=ALU.mult, op1=ALU.mult), prd)

        sm = bb.mark("sv", f"{tag}sm{b}")

        @bb.on("vector")
        def _(v, sm=sm):
            v.drain()
            bb.i(v.reduce_max(out=m2c[:], in_=prod.rearrange("p h tb -> p tb h"), axis=AX,
                              apply_absolute_value=True), sm)
        a2t = bb.mark("sq", f"{tag}a2t{b}")

        @bb.on("tensor")
        def _(t_, a2t=a2t, sm=sm, b6free=bb.get(f"{tag}r2e{b - 1}")):
            bb.w(t_, sm, bb.ev("consts"), b6free)
            for t in range(NT):
                mm = t_.transpose(out=PSBIG[:, 3072 + t * 128:3072 + (t + 1) * 128],
                                  in_=m2c[:, t * 128:(t + 1) * 128], identity=IDF[:])
            bb.i(mm, a2t)
        a2c = bb.mark("sv", f"{tag}a2c{b}")

        @bb.on("vector")
        def _(v, a2c=a2c, a2t=a2t):
            bb.w(v, a2t)
            r = ROWS
            c = CS
            for t in range(NT):
                v.reduce_max(out=r[:, c + 32 + t:c + 33 + t],
                             in_=PSBIG[:, 3072 + t * 128:3072 + (t + 1) * 128], axis=AX)
            v.drain()
            v.tensor_scalar_max(out=r[:, c + 32:c + 32 + NT], in0=r[:, c + 32:c + 32 + NT], scalar1=EPS)
            v.drain()
            for t in range(NT):
                v.tensor_mul(out=r[:, c + 36 + t:c + 37 + t], in0=r[:, c + 32 + t:c + 33 + t],
                             in1=r[:, WBARD:WBARD + 1])
            v.reciprocal(out=r[:, c + 40:c + 40 + NT], in_=r[:, c + 32:c + 32 + NT])
            v.drain()
            v.tensor_scalar_mul(out=r[:, c + 36:c + 36 + NT], in0=r[:, c + 36:c + 36 + NT], scalar1=0.5 / 127.0)
            v.tensor_scalar_mul(out=r[:, c + 40:c + 40 + NT], in0=r[:, c + 40:c + 40 + NT], scalar1=127.0)
            v.drain()
            bb.i(v.engine_nop(), a2c)
        r2t = bb.mark("sq", f"{tag}r2t{b}")

        @bb.on("tensor")
        def _(t_, r2t=r2t, a2c=a2c):
            bb.w(t_, a2c)
            bb.i(t_.transpose(out=PSBIG[:, 3072:3072 + 128],
                              in_=ROWS[:, CS + 40:CS + 40 + 128], identity=IDF[:]), r2t)
        r2s = bb.mark("sv", f"{tag}r2s{b}")

        @bb.on("vector")
        def _(v, r2s=r2s, r2t=r2t):
            bb.w(v, r2t)
            bb.i(v.tensor_copy(out=ROWS[0:NT, 480:608], in_=PSBIG[0:NT, 3072:3200]), r2s)
        r2d = bb.mark(f"d_r2{tag}", f"{tag}r2d{b}")

        @bb.on("scalar")
        def _(a, r2d=r2d, r2s=r2s):
            bb.w(a, r2s)
            bb.i(a.dma_start(out=r2[0:1, :].rearrange("p (a c) -> p a c", a=NT),
                             in_=ROWS[0:NT, 480:608][:, None, :]), r2d)
        r2b = bb.mark("sq", f"{tag}r2b{b}")

        @bb.on("tensor")
        def _(t_, r2b=r2b, r2d=r2d):
            bb.w(t_, r2d)
            bb.i(t_.matmul(out=PSBIG[:, 3072:3072 + TB], lhsT=ctx["ONESF"][0:1, 0:128],
                           rhs=r2[0:1, :], start=True, stop=True), r2b)
        rr = bb.mark("sv", f"{tag}r2e{b}")

        @bb.on("vector")
        def _(v, rr=rr, r2b=r2b):
            bb.w(v, r2b)
            bb.i(v.tensor_copy(out=r2[:], in_=PSBIG[:, 3072:3072 + TB]), rr)
        for hc in range(HC0):
            qq = bb.mark("sv", f"{tag}qq{b}_{hc}")

            @bb.on("vector")
            def _(v, hc=hc, qq=qq, dnprev=bb.get(f"{tag}dnall{b - 1}") if hc == 0 else None):
                bb.w(v, dnprev)
                if hc == 0:
                    v.drain()
                v.tensor_mul(out=prod[:, hc], in0=prod[:, hc], in1=r2[:])
                v.drain()
                bb.i(v.tensor_scalar(out=xq2[:, hc], in0=prod[:, hc], scalar1=MAGIC, scalar2=MAGIC,
                                     op0=ALU.add, op1=ALU.subtract), qq)

        qlast = bb.ev(f"{tag}qq{b}_{HC0 - 1}")
        for tp in range(NT // 2):
            t0, t1 = 2 * tp, 2 * tp + 1
            gtoks = [bb.mark("sq", f"{tag}dnw{(b * (NT // 2) + tp) * NDG + g}") for g in range(NDG - 1)]
            dn = bb.mark("sq", f"{tag}dn{b}_{tp}")
            for g in range(NDG - 1):
                bb.events[f"{tag}dfree{(b * (NT // 2) + tp) * NDG + g}"] = gtoks[g]
            bb.events[f"{tag}dfree{(b * (NT // 2) + tp) * NDG + (NDG - 1)}"] = dn

            @bb.on("tensor")
            def _(t_, tp=tp, t0=t0, t1=t1, dn=dn, qlast=qlast, b=b, gtoks=gtoks,
                  evprev=bb.get(f"{tag}oev{b}_{2 * tp - 1}") or bb.get(f"{tag}oev{b - 1}_{NT - 1}")):
                bb.w(t_, qlast, evprev)
                for g in range(NDG):
                    di = (b * (NT // 2) + tp) * NDG + g
                    par = di % 2
                    bb.w(t_, bb.ev(f"{tag}dl{di}"))
                    for sl in range(4):
                        hc = g * 4 + sl
                        first = hc == 0
                        last = hc == HC0 - 1
                        for ti, t in enumerate((t0, t1)):
                            for dh in range(2):
                                mm = t_.matmul(out=bank(ctx, 2 + ti * 2 + dh),
                                               lhsT=xq2[:, hc, t * 128:(t + 1) * 128],
                                               rhs=wds[:, par, sl, dh * 512:(dh + 1) * 512],
                                               start=first, stop=last)
                    if g < NDG - 1:
                        bb.i(mm, gtoks[g])
                bb.i(mm, dn)
            for ti, t in enumerate((t0, t1)):
                ev = bb.mark("sv", f"{tag}oev{b}_{t}")

                @bb.on("vector")
                def _(v, t=t, ti=ti, dn=dn, ev=ev, a2c=a2c,
                      stfree=bb.get(f"{tag}ost{b}_{t - 2}") or bb.get(f"{tag}ost{b - 1}_{t - 2 + NT}")):
                    bb.w(v, dn, a2c, stfree)
                    r = ROWS
                    c = CS
                    for dh in range(2):
                        ins = v.scalar_tensor_tensor(out=obuf[:, t % 2, dh * 512:(dh + 1) * 512],
                                                     in0=bank(ctx, 2 + ti * 2 + dh),
                                                     scalar=r[:, c + 36 + t:c + 37 + t],
                                                     in1=xblk[:, t, dh * 512:(dh + 1) * 512],
                                                     op0=ALU.mult, op1=ALU.add)
                    bb.i(ins, ev)
                st = bb.mark(f"d_o{tag}{t % 2}", f"{tag}ost{b}_{t}")

                @bb.on("sync")
                def _(s, t=t, b=b, ev=ev, st=st):
                    bb.w(s, ev)
                    bb.i(s.dma_start(out=dst[b * TB + t * 128:b * TB + (t + 1) * 128, :],
                                     in_=obuf[:, t % 2]), st)
        bb.events[f"{tag}dnall{b}"] = bb.ev(f"{tag}dn{b}_{NT // 2 - 1}")
        bb.events[f"{tag}xfree{b}"] = bb.ev(f"{tag}oev{b}_{NT - 1}")
        if b == NBLK // 2 - 1:
            ctx[f"{tag}_half_st"] = [bb.total(f"d_o{tag}0"), bb.total(f"d_o{tag}1")]

    ctx["ptb_chain"] = bb.ev(f"{tag}trev{NBLK - 1}_{NT - 1}")
    ctx[f"{tag}_fin"] = [bb.total(f"d_o{tag}0"), bb.total(f"d_o{tag}1")]



def plan_attn0(ctx, tag, src, dst):
    bb, sc, io = ctx["bb"], ctx["sc"], ctx["io"]
    ROWS, PTB, IDB, IDF, PSBIG = ctx["ROWS"], ctx["PTB"], ctx["IDB"], ctx["IDF"], ctx["PSBIG"]
    S = ctx["seqlen"]
    bpc = ctx["bpc"]
    NTL = S // 128
    fa = Arena(ctx["F32A"], 14336)
    alibuf = fa.take(2 * S)
    xbuf = fa.take(2, D)
    obuf = fa.take(2, D)
    dump = fa.take(2, D)
    ba = Arena(ctx["BF16A"], 69632)
    xln_fm = ba.take(DC, S)
    q_fm = ba.take(DC, S)
    v_tm = ba.take(NTL, D)
    pbuf = ba.take(S)
    xlns = ba.take(2, D)
    wv_sb = ba.take(DC, D)
    wqk_sb = ba.take(2, DC, 128)
    CS = RSTAT

    ab1 = bb.mark("sg", f"{tag}ab1")

    @bb.on("gpsimd")
    def _(g, pprev=ctx.get(f"{tag}_prev_done")):
        bb.w(g, pprev)
        bb.i(g.iota(alibuf[:], pattern=[[1, 2 * S]], base=-S, channel_multiplier=-1,
                    allow_small_or_imprecise_dtypes=True), ab1)
    ab2 = bb.mark("sv", f"{tag}ab2")

    @bb.on("vector")
    def _(v):
        bb.w(v, bb.ev(f"{tag}ab1"), bb.ev("alibi_bc"))
        v.drain()
        bb.i(v.tensor_scalar_mul(out=alibuf[:], in0=alibuf[:],
                                 scalar1=ROWS[:, SLOPE:SLOPE + 1]), ab2)
    ab3 = bb.mark("sg", f"{tag}ab3")

    @bb.on("gpsimd")
    def _(g):
        bb.w(g, bb.ev(f"{tag}ab2"))
        bb.i(g.affine_select(alibuf[:], alibuf[:], pattern=[[-1, 2 * S]], base=S,
                             channel_multiplier=1, compare_op=ALU.is_ge, fill=-1e30), ab3)
    wvl = bb.mark(f"d_wv{tag}", f"{tag}wvl")

    @bb.on("sync")
    def _(s, prol=ctx.get("prologue_fin"), pprev=ctx.get(f"{tag}_prev_done")):
        bb.w(s, prol, pprev)
        bb.i(s.dma_start(out=wv_sb.rearrange("p (g a) d -> p g a d", g=DC // 4), in_=sc["wvT"].rearrange("g p a d -> p g a d")), wvl)

    for sq_ in range(bpc):
        t0 = sq_ * S
        pfx = f"{tag}s{sq_}"

        for tt in range(NTL):
            xl = bb.mark(f"d_x{tag}{tt % 2}", f"{pfx}xl{tt}")

            @bb.on("sync")
            def _(s, tt=tt, xl=xl, t0=t0, fname=f"{pfx}lx{tt - 2}",
                  pprev=ctx.get(f"{tag}_prev_done") if (sq_ == 0 and tt < 2) else None,
                  sprev=bb.get(f"{tag}s{sq_ - 1}oev{NTL - 2 + tt}") if (sq_ > 0 and tt < 2) else None):
                bb.wn(s, fname)
                bb.w(s, pprev, sprev)
                bb.i(s.dma_start(out=xbuf[:, tt % 2], in_=src[t0 + tt * 128:t0 + (tt + 1) * 128, :]), xl)
            g4 = tt % 4
            st = bb.mark("sv", f"{pfx}sum{tt}")

            @bb.on("vector")
            def _(v, tt=tt, xl=xl, st=st, g4=g4):
                bb.w(v, xl)
                bb.i(v.reduce_sum(out=ROWS[:, CS + g4:CS + g4 + 1], in_=xbuf[:, tt % 2], axis=AX), st)
            sqv = bb.mark("sa", f"{pfx}sq{tt}")

            @bb.on("scalar")
            def _(a, tt=tt, xl=xl, sqv=sqv, g4=g4):
                bb.w(a, xl)
                bb.i(a.activation(out=dump[:, tt % 2], in_=xbuf[:, tt % 2], func=AF.Square,
                                  accum_out=ROWS[:, CS + 4 + g4:CS + 5 + g4]), sqv)
            rw = bb.mark("sv", f"{pfx}rw{tt}")

            @bb.on("vector")
            def _(v, rw=rw, sqv=sqv, g4=g4):
                bb.w(v, sqv)
                v.drain()
                r = ROWS
                c = CS
                v.tensor_scalar_mul(out=r[:, c + g4:c + g4 + 1], in0=r[:, c + g4:c + g4 + 1], scalar1=1.0 / D)
                v.drain()
                v.tensor_mul(out=r[:, c + 16 + g4:c + 17 + g4], in0=r[:, c + g4:c + g4 + 1],
                             in1=r[:, c + g4:c + g4 + 1])
                v.drain()
                v.scalar_tensor_tensor(out=r[:, c + 16 + g4:c + 17 + g4], in0=r[:, c + 4 + g4:c + 5 + g4],
                                       scalar=1.0 / D, in1=r[:, c + 16 + g4:c + 17 + g4],
                                       op0=ALU.mult, op1=ALU.subtract)
                v.drain()
                bb.i(v.tensor_scalar_add(out=r[:, c + 16 + g4:c + 17 + g4],
                                         in0=r[:, c + 16 + g4:c + 17 + g4], scalar1=EPS), rw)
            rq = bb.mark("sa", f"{pfx}rq{tt}")

            @bb.on("scalar")
            def _(a, rw=rw, rq=rq, g4=g4):
                bb.w(a, rw)
                bb.i(a.activation(out=ROWS[:, CS + 16 + g4:CS + 17 + g4],
                                  in_=ROWS[:, CS + 16 + g4:CS + 17 + g4], func=AF.Sqrt), rq)
            rv = bb.mark("sv", f"{pfx}rv{tt}")

            @bb.on("vector")
            def _(v, rq=rq, rv=rv, g4=g4):
                bb.w(v, rq)
                v.drain()
                r = ROWS
                c = CS
                v.reciprocal(out=r[:, c + 16 + g4:c + 17 + g4], in_=r[:, c + 16 + g4:c + 17 + g4])
                v.drain()
                v.tensor_mul(out=r[:, c + 8 + g4:c + 9 + g4], in0=r[:, c + g4:c + g4 + 1],
                             in1=r[:, c + 16 + g4:c + 17 + g4])
                v.drain()
                v.tensor_scalar_mul(out=r[:, c + 8 + g4:c + 9 + g4], in0=r[:, c + 8 + g4:c + 9 + g4],
                                    scalar1=-1.0)
                v.drain()
                bb.i(v.engine_nop(), rv)
            lx = bb.mark("sa", f"{pfx}lx{tt}")

            @bb.on("scalar")
            def _(a, tt=tt, rv=rv, lx=lx, g4=g4, trprev=bb.get(f"{pfx}tr{tt - 2}")):
                bb.w(a, rv, trprev)
                bb.i(a.activation(out=xlns[:, tt % 2], in_=xbuf[:, tt % 2], func=AF.Identity,
                                  bias=ROWS[:, CS + 8 + g4:CS + 9 + g4],
                                  scale=ROWS[:, CS + 16 + g4:CS + 17 + g4]), lx)
            tr = bb.mark("sq", f"{pfx}tr{tt}")

            @bb.on("tensor")
            def _(t_, tt=tt, lx=lx, tr=tr,
                  prevev=bb.get(f"{pfx}trev{tt - 1}") or bb.get(f"{tag}ptb_prev") or ctx.get("ptb_chain"),
                  pquiet=ctx.pop("psum_quiet", None)):
                bb.w(t_, lx, bb.ev("consts"), prevev, pquiet)
                for j in range(DC):
                    mm = t_.transpose(out=PTB[:, j * 128:(j + 1) * 128],
                                      in_=xlns[:, tt % 2, j * 128:(j + 1) * 128], identity=IDB[:])
                bb.i(mm, tr)
            tre = bb.mark("sa", f"{pfx}trev{tt}")

            @bb.on("scalar")
            def _(a, tt=tt, tr=tr, tre=tre):
                bb.w(a, tr)
                bb.i(a.copy(out=xln_fm[:, :, tt * 128:(tt + 1) * 128],
                            in_=PTB[:].rearrange("p (dc c) -> p dc c", dc=DC)), tre)
        bb.events[f"{tag}ptb_prev"] = bb.ev(f"{pfx}trev{NTL - 1}")
        xln_done = bb.ev(f"{pfx}trev{NTL - 1}")

        NG = max(1, S // 512)
        for jc in range(DC):
            ql = bb.mark(f"d_qk{tag}{jc % 2}", f"{pfx}qkl{jc}")

            @bb.on("sync")
            def _(s, jc=jc, ql=ql, fname=f"{pfx}qmm{jc - 2}_{NG - 1}", wvl=wvl,
                  first=(jc < 2 and sq_ == 0)):
                bb.wn(s, fname)
                if first:
                    bb.w(s, wvl)
                bb.i(s.dma_start(out=wqk_sb[:, jc % 2], in_=sc["wqkb"][jc]), ql)
            for grp in range(NG):
                w = min(512, S - grp * 512)
                qmm = bb.mark("sq", f"{pfx}qmm{jc}_{grp}")

                @bb.on("tensor")
                def _(t_, jc=jc, grp=grp, w=w, qmm=qmm, ql=ql, xln_done=xln_done,
                      evprev=bb.get(f"{pfx}qe{jc}_{grp - 1}") or bb.get(f"{pfx}qe{jc - 1}_{NG - 1}")):
                    bb.w(t_, ql, xln_done, evprev)
                    for d in range(DC):
                        mm = t_.matmul(out=PSBIG[:, 0:w],
                                       lhsT=wqk_sb[:, jc % 2, d],
                                       rhs=xln_fm[:, d, grp * 512:grp * 512 + w],
                                       start=(d == 0), stop=(d == DC - 1))
                    bb.i(mm, qmm)
                qe = bb.mark("sa", f"{pfx}qe{jc}_{grp}")

                @bb.on("scalar")
                def _(a, jc=jc, grp=grp, w=w, qmm=qmm, qe=qe):
                    bb.w(a, qmm)
                    bb.i(a.mul(out=q_fm[:, jc, grp * 512:grp * 512 + w],
                               in_=PSBIG[:, 0:w], mul=1.0 / 32.0), qe)
        qdone = bb.ev(f"{pfx}qe{DC - 1}_{NG - 1}")

        for tt in range(NTL):
            vmm = bb.mark("sq", f"{pfx}vmm{tt}")

            @bb.on("tensor")
            def _(t_, tt=tt, vmm=vmm, xln_done=xln_done, wvl=wvl,
                  evprev=bb.get(f"{pfx}ve{tt - 1}") or bb.get(f"{tag}s{sq_ - 1}oev{NTL - 1}")):
                bb.w(t_, xln_done, wvl, evprev)
                for dh in range(2):
                    for d in range(DC):
                        mm = t_.matmul(out=PSBIG[:, 2048 + dh * 512:2048 + (dh + 1) * 512],
                                       lhsT=xln_fm[:, d, tt * 128:(tt + 1) * 128],
                                       rhs=wv_sb[:, d, dh * 512:(dh + 1) * 512],
                                       start=(d == 0), stop=(d == DC - 1))
                bb.i(mm, vmm)
            ve = bb.mark("sv", f"{pfx}ve{tt}")

            @bb.on("vector")
            def _(v, tt=tt, vmm=vmm, ve=ve):
                bb.w(v, vmm)
                bb.i(v.tensor_copy(out=v_tm[:, tt, :], in_=PSBIG[:, 2048:3072]), ve)
        vdone = bb.ev(f"{pfx}ve{NTL - 1}")

        def emit_ptr_ctx(i):
            nch = i + 1
            ngp = (nch + 7) // 8
            for gp in range(ngp):
                ptr = bb.mark("sq", f"{pfx}ptr{i}_{gp}")

                @bb.on("tensor")
                def _(t_, i=i, gp=gp, nch=nch, ptr=ptr, ex=bb.ev(f"{pfx}ex{i}"),
                      pev=bb.get(f"{pfx}pte{i}_{gp - 1}") or bb.get(f"{pfx}pte{i - 1}_last") or bb.get(f"{tag}ptb_prev")):
                    bb.w(t_, ex, bb.ev("consts"), pev)
                    for j in range(min(8, nch - gp * 8)):
                        c = gp * 8 + j
                        mm = t_.transpose(out=PTB[:, j * 128:(j + 1) * 128],
                                          in_=pbuf[:, c * 128:(c + 1) * 128], identity=IDB[:])
                    bb.i(mm, ptr)
                pte = bb.mark("sa", f"{pfx}pte{i}_{gp}")

                @bb.on("scalar")
                def _(a, i=i, gp=gp, nch=nch, ptr=ptr, pte=pte):
                    bb.w(a, ptr)
                    nw = min(8, nch - gp * 8) * 128
                    bb.i(a.copy(out=pbuf[:, gp * 1024:gp * 1024 + nw], in_=PTB[:, 0:nw]), pte)
            bb.events[f"{pfx}pte{i}_last"] = bb.ev(f"{pfx}pte{i}_{ngp - 1}")
            ctx_mm = bb.mark("sq", f"{pfx}ctx{i}")

            @bb.on("tensor")
            def _(t_, i=i, nch=nch, ctx_mm=ctx_mm, pte=bb.ev(f"{pfx}pte{i}_{ngp - 1}"),
                  oevprev=bb.get(f"{pfx}oev{i - 1}"), vdone=vdone):
                bb.w(t_, pte, oevprev, vdone)
                for kc in range(nch):
                    for dh in range(2):
                        mm = t_.matmul(out=PSBIG[:, 2048 + dh * 512:2048 + (dh + 1) * 512],
                                       lhsT=pbuf[:, kc * 128:(kc + 1) * 128],
                                       rhs=v_tm[:, kc, dh * 512:(dh + 1) * 512],
                                       start=(kc == 0), stop=(kc == nch - 1))
                bb.i(mm, ctx_mm)
            xr = bb.mark(f"d_xr{tag}{i % 2}", f"{pfx}xr{i}")

            @bb.on("sync")
            def _(s, i=i, xr=xr, t0=t0, fname=f"{pfx}oev{i - 2}",
                  lxlast=bb.get(f"{pfx}lx{NTL - 2 + i}") if i < 2 else None):
                bb.wn(s, fname)
                bb.w(s, lxlast)
                bb.i(s.dma_start(out=xbuf[:, i % 2], in_=src[t0 + i * 128:t0 + (i + 1) * 128, :]), xr)
            oev = bb.mark("sv", f"{pfx}oev{i}")

            @bb.on("vector")
            def _(v, i=i, oev=oev, ctx_mm=ctx_mm, xr=xr,
                  stprev=bb.get(f"{pfx}ost{i - 2}") or bb.get(f"{tag}s{sq_ - 1}ost{NTL - 2 + i}")):
                bb.w(v, ctx_mm, xr, stprev)
                r = ROWS
                for dh in range(2):
                    ins = v.scalar_tensor_tensor(out=obuf[:, i % 2, dh * 512:(dh + 1) * 512],
                                                 in0=PSBIG[:, 2048 + dh * 512:2048 + (dh + 1) * 512],
                                                 scalar=r[:, CS + 40 + (i % 8):CS + 41 + (i % 8)],
                                                 in1=xbuf[:, i % 2, dh * 512:(dh + 1) * 512],
                                                 op0=ALU.mult, op1=ALU.add)
                bb.i(ins, oev)
            ost = bb.mark(f"d_o{tag}{i % 2}", f"{pfx}ost{i}")

            @bb.on("sync")
            def _(s, i=i, oev=oev, ost=ost, t0=t0):
                bb.w(s, oev)
                bb.i(s.dma_start(out=dst[t0 + i * 128:t0 + (i + 1) * 128, :], in_=obuf[:, i % 2]), ost)

        for i in range(NTL):
            ki = 128 * (i + 1)
            nkb = (ki + 511) // 512
            smm = bb.mark("sq", f"{pfx}smm{i}")

            @bb.on("tensor")
            def _(t_, i=i, ki=ki, nkb=nkb, smm=smm, qdone=qdone, vdone=vdone,
                  exprev=bb.get(f"{pfx}ex{i - 1}")):
                bb.w(t_, qdone, vdone, exprev)
                for kb in range(nkb):
                    w = min(512, ki - kb * 512)
                    for d in range(DC):
                        mm = t_.matmul(out=PSBIG[:, kb * 512:kb * 512 + w],
                                       lhsT=q_fm[:, d, i * 128:(i + 1) * 128],
                                       rhs=xln_fm[:, d, kb * 512:kb * 512 + w],
                                       start=(d == 0), stop=(d == DC - 1))
                bb.i(mm, smm)
            al = bb.mark("sv", f"{pfx}al{i}")

            @bb.on("vector")
            def _(v, i=i, ki=ki, al=al, smm=smm, ab3=bb.ev(f"{tag}ab3")):
                bb.w(v, smm, ab3)
                v.tensor_add(out=PSBIG[:, 0:ki], in0=PSBIG[:, 0:ki],
                             in1=alibuf[:, S - i * 128:S - i * 128 + ki])
                v.drain()
                v.reduce_max(out=ROWS[:, CS + 24 + (i % 8):CS + 25 + (i % 8)],
                             in_=PSBIG[:, 0:ki], axis=AX, negate=True)
                bb.i(v.engine_nop(), al)
            ex = bb.mark("sa", f"{pfx}ex{i}")

            @bb.on("scalar")
            def _(a, i=i, ki=ki, ex=ex, al=al,
                  ctxprev=bb.get(f"{pfx}ctx{i - 1}") or bb.get(f"{tag}s{sq_ - 1}ctx{NTL - 1}")):
                bb.w(a, al, ctxprev)
                bb.i(a.activation(out=pbuf[:, 0:ki], in_=PSBIG[:, 0:ki], func=AF.Exp,
                                  bias=ROWS[:, CS + 24 + (i % 8):CS + 25 + (i % 8)], scale=1.0,
                                  accum_out=ROWS[:, CS + 32 + (i % 8):CS + 33 + (i % 8)]), ex)
            rc = bb.mark("sv", f"{pfx}rc{i}")

            @bb.on("vector")
            def _(v, i=i, rc=rc, ex=ex):
                bb.w(v, ex)
                v.drain()
                v.reciprocal(out=ROWS[:, CS + 40 + (i % 8):CS + 41 + (i % 8)],
                             in_=ROWS[:, CS + 32 + (i % 8):CS + 33 + (i % 8)])
                v.drain()
                bb.i(v.engine_nop(), rc)
            emit_ptr_ctx(i)
        bb.events[f"{tag}ptb_prev"] = bb.ev(f"{pfx}pte{NTL - 1}_last")
    ctx["ptb_chain"] = bb.ev(f"{tag}ptb_prev"[0:0] + f"{tag}s{bpc - 1}pte{NTL - 1}_last"[0:99]) if False else bb.events[f"{tag}ptb_prev"]
    ctx[f"{tag}_fin"] = [bb.total(f"d_o{tag}0"), bb.total(f"d_o{tag}1")]



def plan_phases(ctx):
    bb, io, sc = ctx["bb"], ctx["io"], ctx["sc"]
    upto = ctx["upto"]
    fin = []
    if upto == 50:
        plan_attn1(ctx, "E", io["x_in"], io["y_out"])
        fin += ctx["E_fin"]
    elif upto == 1:
        plan_ffn0(ctx, "A", io["x_in"], io["y_out"])
        fin += ctx["A_fin"]
    elif upto == 2:
        plan_ffn0(ctx, "A", io["x_in"], sc["s_a"])
        ctx["B_prev_done"] = ctx["A_fin"]
        plan_attn0(ctx, "B", sc["s_a"], io["y_out"])
        fin += ctx["B_fin"]
    else:
        plan_ffn0(ctx, "A", io["x_in"], sc["s_a"])
        ctx["B_prev_done"] = ctx["A_fin"]
        plan_attn0(ctx, "B", sc["s_a"], sc["s_b"])
        ctx["C_prev_done"] = ctx["B_fin"]
        plan_ffn0(ctx, "C", sc["s_b"], sc["cc_src"])
        plan_gather(ctx)
        plan_relayout(ctx)
        ctx["D_prev_done"] = ctx["C_fin"] + ctx["gather_fin"]
        if upto == 45:
            plan_ffn1(ctx, "D", sc["s_g"], io["y_out"])
            fin += ctx["D_fin"]
        elif upto == 4:
            plan_ffn1(ctx, "D", sc["s_g"], io["y_out"])
            fin += ctx["D_fin"]
        elif upto == 5:
            plan_ffn1(ctx, "D", sc["s_g"], sc["s_d"])
            ctx["E_prev_done"] = ctx["D_fin"]
            plan_attn1(ctx, "E", sc["s_d"], io["y_out"])
            fin += ctx["E_fin"]
        else:
            plan_ffn1(ctx, "D", sc["s_g"], sc["s_d"])
            ctx["E_prev_done"] = ctx["D_fin"]
            plan_attn1(ctx, "E", sc["s_d"], sc["s_e"])
            ctx["F_prev_done"] = ctx["E_fin"]
            plan_ffn1(ctx, "F", sc["s_e"], io["y_out"])
            fin += ctx["F_fin"]

    @bb.on("sync")
    def _(s):
        for tok in fin:
            bb.w(s, tok)



def plan_ffn1(ctx, tag, src, dst):
    bb, sc = ctx["bb"], ctx["sc"]
    ROWS, PTB, IDB, PSBIG = ctx["ROWS"], ctx["PTB"], ctx["IDB"], ctx["PSBIG"]
    NBLK, TB, NT = ctx["NBLK"], ctx["TB"], ctx["NT"]
    fa = Arena(ctx["F32A"], 14336)
    xblk2 = fa.take(2, NT, D)
    gbuf = fa.take(2, TB)
    obuf = fa.take(2, D)
    dump = fa.take(2, D)
    ba = Arena(ctx["BF16A"], 69632)
    xlns = ba.take(2, D)
    xln_fm = ba.take(DC, TB)
    harena = ba.take(HC1, TB)
    wgs = ba.take(2, 4, DC, 128)
    wus = ba.take(2, 4, DC, 128)
    wds = ba.take(2, 4, D)
    CS = RSTAT
    WG, WU, WD = sc["w1gT"], sc["w1uT"], sc["w1dT"]
    NGB = HC1 // 4

    for b in range(NBLK):
        xblk = xblk2[:, b % 2]
        xl = bb.mark(f"d_x{tag}", f"{tag}xl{b}")

        @bb.on("sync")
        def _(s, b=b, xl=xl, xblk=xblk, freev=bb.get(f"{tag}xfree{b - 2}"),
              pprev=ctx.get(f"{tag}_prev_done") if b == 0 else None):
            bb.w(s, freev, pprev)
            bb.i(s.dma_start(out=xblk,
                             in_=src[b * TB:(b + 1) * TB, :].rearrange("(a p) d -> p a d", p=128)), xl)
        for g in range(NGB):
            gi = b * NGB + g
            par = gi % 2
            wl0 = bb.mark(f"d_w{tag}{par}")
            wl = bb.mark(f"d_w{tag}{par}", f"{tag}wl{gi}")

            @bb.on("sync")
            def _(s, g=g, par=par, wl0=wl0, wl=wl, fname=f"{tag}wfree{gi - 2}",
                  pprev=ctx.get(f"{tag}_prev_done") if gi < 2 else None):
                bb.wn(s, fname)
                bb.w(s, pprev)
                bb.i(s.dma_start(out=wgs[:, par], in_=WG[g]), wl0)
                bb.i(s.dma_start(out=wus[:, par], in_=WU[g]), wl)
        for g in range(NGB):
            di = b * NGB + g
            par = di % 2
            dl = bb.mark(f"d_d{tag}{par}", f"{tag}dl{di}")

            @bb.on("sync")
            def _(s, g=g, par=par, dl=dl, fname=f"{tag}dfree{di - 2}",
                  pprev=ctx.get(f"{tag}_prev_done") if di < 2 else None):
                bb.wn(s, fname)
                bb.w(s, pprev)
                bb.i(s.dma_start(out=wds[:, par], in_=WD[g]), dl)

        for t in range(NT):
            st = bb.mark("sv", f"{tag}sum{b}_{t}")

            @bb.on("vector")
            def _(v, t=t, xl=xl, st=st, xblk=xblk):
                bb.w(v, xl)
                bb.i(v.reduce_sum(out=ROWS[:, CS + t:CS + t + 1], in_=xblk[:, t], axis=AX), st)
            sq = bb.mark("sa", f"{tag}sq{b}_{t}")

            @bb.on("scalar")
            def _(a, t=t, xl=xl, sq=sq, xblk=xblk):
                bb.w(a, xl)
                bb.i(a.activation(out=dump[:, t % 2], in_=xblk[:, t], func=AF.Square,
                                  accum_out=ROWS[:, CS + 4 + t:CS + 5 + t]), sq)
        r1 = bb.mark("sv", f"{tag}rows1_{b}")

        @bb.on("vector")
        def _(v, r1=r1, sa_need=bb.ev(f"{tag}sq{b}_{NT - 1}")):
            bb.w(v, sa_need)
            v.drain()
            r = ROWS
            c = CS
            v.tensor_scalar_mul(out=r[:, c:c + NT], in0=r[:, c:c + NT], scalar1=1.0 / D)
            v.drain()
            v.tensor_mul(out=r[:, c + 16:c + 16 + NT], in0=r[:, c:c + NT], in1=r[:, c:c + NT])
            v.drain()
            v.scalar_tensor_tensor(out=r[:, c + 16:c + 16 + NT], in0=r[:, c + 4:c + 4 + NT],
                                   scalar=1.0 / D, in1=r[:, c + 16:c + 16 + NT],
                                   op0=ALU.mult, op1=ALU.subtract)
            v.drain()
            v.tensor_scalar_add(out=r[:, c + 16:c + 16 + NT], in0=r[:, c + 16:c + 16 + NT], scalar1=EPS)
            v.drain()
            bb.i(v.engine_nop(), r1)
        r2e = bb.mark("sa", f"{tag}rows2_{b}")

        @bb.on("scalar")
        def _(a, r1=r1, r2e=r2e):
            bb.w(a, r1)
            bb.i(a.activation(out=ROWS[:, CS + 16:CS + 16 + NT], in_=ROWS[:, CS + 16:CS + 16 + NT],
                              func=AF.Sqrt), r2e)
        r3 = bb.mark("sv", f"{tag}rows3_{b}")

        @bb.on("vector")
        def _(v, r2e=r2e, r3=r3):
            bb.w(v, r2e)
            v.drain()
            r = ROWS
            c = CS
            v.reciprocal(out=r[:, c + 16:c + 16 + NT], in_=r[:, c + 16:c + 16 + NT])
            v.drain()
            v.tensor_mul(out=r[:, c + 8:c + 8 + NT], in0=r[:, c:c + NT], in1=r[:, c + 16:c + 16 + NT])
            v.drain()
            v.tensor_scalar_mul(out=r[:, c + 8:c + 8 + NT], in0=r[:, c + 8:c + 8 + NT], scalar1=-1.0)
            v.drain()
            bb.i(v.engine_nop(), r3)
        for t in range(NT):
            lx = bb.mark("sa", f"{tag}lx{b}_{t}")

            @bb.on("scalar")
            def _(a, t=t, r3=r3, lx=lx, xblk=xblk,
                  trprev=bb.get(f"{tag}tr{b}_{t - 2}") or bb.get(f"{tag}tr{b - 1}_{t}")):
                bb.w(a, r3, trprev)
                bb.i(a.activation(out=xlns[:, t % 2], in_=xblk[:, t], func=AF.Identity,
                                  bias=ROWS[:, CS + 8 + t:CS + 9 + t],
                                  scale=ROWS[:, CS + 16 + t:CS + 17 + t]), lx)
            tr = bb.mark("sq", f"{tag}tr{b}_{t}")

            @bb.on("tensor")
            def _(t_, t=t, lx=lx, tr=tr, prevev=bb.get(f"{tag}trev{b}_{t - 1}")
                  or bb.get(f"{tag}trev{b - 1}_{NT - 1}") or ctx.get("ptb_chain"),
                  pquiet=ctx.pop("psum_quiet", None)):
                bb.w(t_, lx, bb.ev("consts"), prevev, pquiet)
                for j in range(DC):
                    mm = t_.transpose(out=PTB[:, j * 128:(j + 1) * 128],
                                      in_=xlns[:, t % 2, j * 128:(j + 1) * 128], identity=IDB[:])
                bb.i(mm, tr)
            tre = bb.mark("sa", f"{tag}trev{b}_{t}")

            @bb.on("scalar")
            def _(a, t=t, tr=tr, tre=tre, mmu_prev=bb.get(f"{tag}mmu{b - 1}_{HC1 - 1}") if t == 0 else None):
                bb.w(a, tr, mmu_prev)
                bb.i(a.copy(out=xln_fm[:, :, t * 128:(t + 1) * 128],
                            in_=PTB[:].rearrange("p (dc c) -> p dc c", dc=DC)), tre)
        xq_ready = bb.ev(f"{tag}trev{b}_{NT - 1}")

        for hc in range(HC1):
            g = hc // 4
            gi = b * NGB + g
            par = gi % 2
            sl = hc % 4
            pg, pu = (0, 1) if hc % 2 == 0 else (4, 5)
            mmg = bb.mark("sq", f"{tag}mmg{b}_{hc}")
            mmu = bb.mark("sq", f"{tag}mmu{b}_{hc}")

            @bb.on("tensor")
            def _(t_, hc=hc, par=par, sl=sl, pg=pg, pu=pu, mmg=mmg, mmu=mmu,
                  wl=bb.ev(f"{tag}wl{gi}"), xq_ready=xq_ready,
                  bankfree=bb.get(f"{tag}prd{b}_{hc - 2}"),
                  dnprev=bb.get(f"{tag}dnall{b - 1}") if hc == 0 else None):
                bb.w(t_, wl, xq_ready, bankfree, dnprev)
                for d in range(DC):
                    mm = t_.matmul(out=bank(ctx, pg, TB), lhsT=wgs[:, par, sl, d], rhs=xln_fm[:, d],
                                   start=(d == 0), stop=(d == DC - 1))
                bb.i(mm, mmg)
                for d in range(DC):
                    mm = t_.matmul(out=bank(ctx, pu, TB), lhsT=wus[:, par, sl, d], rhs=xln_fm[:, d],
                                   start=(d == 0), stop=(d == DC - 1))
                bb.i(mm, mmu)
            if hc % 4 == 3:
                bb.events[f"{tag}wfree{gi}"] = mmu
            sil = bb.mark("sa", f"{tag}sil{b}_{hc}")

            @bb.on("scalar")
            def _(a, hc=hc, pg=pg, sil=sil, mmg=mmg, gfree=bb.get(f"{tag}prd{b}_{hc - 2}")):
                bb.w(a, mmg, gfree)
                bb.i(a.activation(out=gbuf[:, hc % 2], in_=bank(ctx, pg, TB), func=AF.Silu,
                                  scale=1.0 / 127.0), sil)
            prd = bb.mark("sv", f"{tag}prd{b}_{hc}")

            @bb.on("vector")
            def _(v, hc=hc, pu=pu, sil=sil, mmu=mmu, prd=prd):
                bb.w(v, sil, mmu)
                bb.i(v.scalar_tensor_tensor(out=harena[:, hc], in0=bank(ctx, pu, TB),
                                            scalar=1.0 / 127.0, in1=gbuf[:, hc % 2],
                                            op0=ALU.mult, op1=ALU.mult), prd)
        hdone = bb.ev(f"{tag}prd{b}_{HC1 - 1}")

        for tp in range(NT // 2):
            t0, t1 = 2 * tp, 2 * tp + 1
            gtoks = [bb.mark("sq", f"{tag}dnw{(b * (NT // 2) + tp) * NGB + g}") for g in range(NGB - 1)]
            dn = bb.mark("sq", f"{tag}dn{b}_{tp}")
            for g in range(NGB - 1):
                bb.events[f"{tag}dfree{(b * (NT // 2) + tp) * NGB + g}"] = gtoks[g]
            bb.events[f"{tag}dfree{(b * (NT // 2) + tp) * NGB + (NGB - 1)}"] = dn

            @bb.on("tensor")
            def _(t_, tp=tp, t0=t0, t1=t1, dn=dn, hdone=hdone, b=b, gtoks=gtoks,
                  evprev=bb.get(f"{tag}oev{b}_{2 * tp - 1}") or bb.get(f"{tag}oev{b - 1}_{NT - 1}")):
                bb.w(t_, hdone, evprev)
                for g in range(NGB):
                    di = (b * (NT // 2) + tp) * NGB + g
                    par = di % 2
                    bb.w(t_, bb.ev(f"{tag}dl{di}"))
                    for sl in range(4):
                        hc = g * 4 + sl
                        first = hc == 0
                        last = hc == HC1 - 1
                        for ti, t in enumerate((t0, t1)):
                            for dh in range(2):
                                mm = t_.matmul(out=bank(ctx, 2 + ti * 2 + dh),
                                               lhsT=harena[:, hc, t * 128:(t + 1) * 128],
                                               rhs=wds[:, par, sl, dh * 512:(dh + 1) * 512],
                                               start=first, stop=last)
                    if g < NGB - 1:
                        bb.i(mm, gtoks[g])
                bb.i(mm, dn)
            for ti, t in enumerate((t0, t1)):
                ev = bb.mark("sv", f"{tag}oev{b}_{t}")

                @bb.on("vector")
                def _(v, t=t, ti=ti, dn=dn, ev=ev, xblk=xblk,
                      stfree=bb.get(f"{tag}ost{b}_{t - 2}") or bb.get(f"{tag}ost{b - 1}_{t - 2 + NT}")):
                    bb.w(v, dn, stfree)
                    for dh in range(2):
                        ins = v.scalar_tensor_tensor(out=obuf[:, t % 2, dh * 512:(dh + 1) * 512],
                                                     in0=bank(ctx, 2 + ti * 2 + dh),
                                                     scalar=0.5 / 127.0,
                                                     in1=xblk[:, t, dh * 512:(dh + 1) * 512],
                                                     op0=ALU.mult, op1=ALU.add)
                    bb.i(ins, ev)
                st = bb.mark(f"d_o{tag}{t % 2}", f"{tag}ost{b}_{t}")

                @bb.on("sync")
                def _(s, t=t, b=b, ev=ev, st=st):
                    bb.w(s, ev)
                    bb.i(s.dma_start(out=dst[b * TB + t * 128:b * TB + (t + 1) * 128, :],
                                     in_=obuf[:, t % 2]), st)
        bb.events[f"{tag}dnall{b}"] = bb.ev(f"{tag}dn{b}_{NT // 2 - 1}")
        bb.events[f"{tag}xfree{b}"] = bb.ev(f"{tag}oev{b}_{NT - 1}")

    ctx["ptb_chain"] = bb.ev(f"{tag}trev{NBLK - 1}_{NT - 1}")
    ctx[f"{tag}_fin"] = [bb.total(f"d_o{tag}0"), bb.total(f"d_o{tag}1")]



def plan_gather(ctx):
    bb, sc = ctx["bb"], ctx["sc"]
    TOK, half, spc = ctx["TOK"], ctx["half"], ctx["spc"]
    S = ctx["seqlen"]
    cc0 = bb.mark("sg", "cc0")

    @bb.on("gpsimd")
    def _(g, h0=ctx.get("C_half_st"), glibn=None):
        bb.w(g, h0)
        bb.i(g.collective_compute("AllGather", ALU.bypass, ins=[sc["cc_src"][0:half]],
                                  outs=[sc["cc_d0"][:]], replica_groups=[list(range(NCORES))]), cc0)
    cc1 = bb.mark("sg", "cc1")

    @bb.on("gpsimd")
    def _(g, h1=ctx.get("C_fin")):
        bb.w(g, h1)
        bb.i(g.collective_compute("AllGather", ALU.bypass, ins=[sc["cc_src"][half:TOK]],
                                  outs=[sc["cc_d1"][:]], replica_groups=[list(range(NCORES))]), cc1)
    ctx["cc_done"] = [bb.ev("cc0"), bb.ev("cc1")]


def plan_relayout(ctx):
    bb, sc = ctx["bb"], ctx["sc"]
    TOK, half, spc = ctx["TOK"], ctx["half"], ctx["spc"]
    S = ctx["seqlen"]
    bpc = ctx["bpc"]
    assert bpc % 2 == 0, "gather halves need even batches-per-core"
    LO = bpc // 2
    NCH = 4 if spc % 4 == 0 else 1
    sw = spc // NCH
    for h2 in range(2):
        src = sc["cc_d0"] if h2 == 0 else sc["cc_d1"]
        ccev = bb.ev("cc0") if h2 == 0 else bb.ev("cc1")
        for ch in range(NCH):
            gl = bb.mark(f"d_gl{h2}", f"gl{h2}_{ch}")

            @bb.on("sync")
            def _(s, h2=h2, src=src, ch=ch, gl=gl, ccev=ccev, sw=sw, LO=LO):
                bb.w(s, ccev)
                if "pid_sync" not in ctx:
                    ctx["pid_sync"] = s.partition_id()
                pid = ctx["pid_sync"]
                dstv = (sc["s_g"].rearrange("(sv hi h2 lo) d -> sv hi h2 lo d", hi=8, h2=2, lo=LO)
                        [ch * sw:(ch + 1) * sw, :, h2])
                srcv = (src.rearrange("(hi lo sblk sv) d -> sblk sv hi lo d", hi=8, lo=LO, sv=sw)
                        [bass.ts(pid * (spc // sw) + ch, 1)][0])
                bb.i(s.dma_start(out=dstv, in_=srcv), gl)
    ctx["gather_fin"] = [bb.total("d_gl0"), bb.total("d_gl1")]



def plan_attn1(ctx, tag, src, dst):
    bb, sc, io = ctx["bb"], ctx["sc"], ctx["io"]
    ROWS, PTB, IDB, PSBIG = ctx["ROWS"], ctx["PTB"], ctx["IDB"], ctx["PSBIG"]
    ONESF = ctx["ONESF"]
    TOK, TB, NT = ctx["TOK"], ctx["TB"], ctx["NT"]
    NSB = TOK // TB
    NPG = TB // 128
    fa = Arena(ctx["F32A"], 14336)
    xblk = fa.take(NT, D)
    obuf = fa.take(2, D)
    dump = fa.take(2, D)
    usb = fa.take(512)
    wsb = fa.take(512)
    ssb = fa.take(512)
    grow = fa.take(512)
    brow = fa.take(512)
    ba = Arena(ctx["BF16A"], 69632)
    xlns = ba.take(2, D)
    xln_fm = ba.take(DC, TB)
    q_fm = ba.take(DC, TB)
    k_fe = ba.take(DC, TB)
    k_fo = ba.take(DC, TB)
    v_tm = ba.take(NT, D)
    probs = ba.take(2, NHEAD * 128)
    bvrow = ba.take(D)
    wq_sb = ba.take(DC, DC, 128)
    wk_sb = ba.take(DC, DC, 128)
    wv_sb = ba.take(DC, D)
    CS = RSTAT
    BQ8 = RTMP + 12

    wla = bb.mark(f"d_wq{tag}")
    wlb = bb.mark(f"d_wq{tag}")
    wl = bb.mark(f"d_wq{tag}", f"{tag}wl")

    @bb.on("sync")
    def _(s, prol=ctx.get("prologue_fin"), pprev=ctx.get(f"{tag}_prev_done"), wla=wla, wlb=wlb, wl=wl):
        bb.w(s, prol, pprev)
        bb.i(s.dma_start(out=wq_sb.rearrange("p (g a) dc c -> p g a dc c", g=DC // 4),
                         in_=sc["wqT"].rearrange("g p a dc c -> p g a dc c")), wla)
        bb.i(s.dma_start(out=wk_sb.rearrange("p (g a) dc c -> p g a dc c", g=DC // 4),
                         in_=sc["wkT"].rearrange("g p a dc c -> p g a dc c")), wlb)
        bb.i(s.dma_start(out=wv_sb.rearrange("p (g a) d -> p g a d", g=DC // 4),
                         in_=sc["wvT1"].rearrange("g p a d -> p g a d")), wl)
    pre_v = bb.mark("sv", f"{tag}prev")

    @bb.on("vector")
    def _(v, pprev=ctx.get(f"{tag}_prev_done")):
        bb.w(v, bb.ev("gb_bc"), pprev)
        v.memset(probs.rearrange("p a b -> p (a b)"), 0.0)
        v.memset(k_fe.rearrange("p a b -> p (a b)")[64:128, :], 0.0)
        v.memset(k_fo.rearrange("p a b -> p (a b)")[0:64, :], 0.0)
        v.drain()
        r = ROWS
        v.tensor_scalar_mul(out=r[:, BQ8:BQ8 + DC], in0=r[:, RBQ:RBQ + DC], scalar1=1.0 / 8.0)
        v.drain()
        for h in range(NHEAD):
            v.tensor_scalar_mul(out=grow[:, h * 32:(h + 1) * 32], in0=ONESF[:, 0:32],
                                scalar1=r[:, RGAM + h:RGAM + h + 1])
            v.tensor_scalar_mul(out=brow[:, h * 32:(h + 1) * 32], in0=ONESF[:, 0:32],
                                scalar1=r[:, RBET + h:RBET + h + 1])
        v.drain()
        bb.i(v.engine_nop(), pre_v)
    bvl2 = bb.mark("d_bv", f"{tag}bvl2")

    @bb.on("scalar")
    def _(a, pprev=ctx.get(f"{tag}_prev_done")):
        bb.w(a, pprev)
        srca = io["bqkv"].rearrange("(h s r) -> h s r", h=NHEAD, s=3, r=64)[:, 2]
        bb.i(a.dma_start(out=dump[0:1, 0, :].rearrange("p (h r) -> p h r", h=NHEAD),
                         in_=srca[None, :, :]), bvl2)
    bvm = bb.mark("sq", f"{tag}bvm")

    @bb.on("tensor")
    def _(t_, bvl2=bvl2, pquiet=ctx.pop("psum_quiet", None)):
        bb.w(t_, bvl2, bb.ev("consts"), pquiet)
        bb.i(t_.matmul(out=PSBIG[:, 3072:3072 + 512], lhsT=ONESF[0:1, 0:128],
                       rhs=dump[0:1, 0, 0:512], start=True, stop=True), bvm)
    bvm2 = bb.mark("sq", f"{tag}bvm2")

    @bb.on("tensor")
    def _(t_):
        bb.w(t_, bb.ev(f"{tag}bve1"))
        bb.i(t_.matmul(out=PSBIG[:, 3072:3072 + 512], lhsT=ONESF[0:1, 0:128],
                       rhs=dump[0:1, 0, 512:1024], start=True, stop=True), bvm2)
    bve1 = bb.mark("sv", f"{tag}bve1")

    @bb.on("vector")
    def _(v, bvm=bvm):
        bb.w(v, bvm)
        bb.i(v.tensor_copy(out=bvrow[:, 0:512], in_=PSBIG[:, 3072:3584]), bve1)
    bve2 = bb.mark("sv", f"{tag}bve2")

    @bb.on("vector")
    def _(v, bvm2=bvm2):
        bb.w(v, bvm2)
        bb.i(v.tensor_copy(out=bvrow[:, 512:1024], in_=PSBIG[:, 3072:3584]), bve2)

    for sb in range(NSB):
        pfx = f"{tag}b{sb}"
        xl = bb.mark(f"d_x{tag}", f"{pfx}xl")

        @bb.on("sync")
        def _(s, sb=sb, xl=xl, freev=bb.get(f"{tag}b{sb - 1}oev1"), wl=wl if sb == 0 else None):
            bb.w(s, freev, wl)
            bb.i(s.dma_start(out=xblk,
                             in_=src[sb * TB:(sb + 1) * TB, :].rearrange("(a p) d -> p a d", p=128)), xl)
        for t in range(NT):
            st = bb.mark("sv", f"{pfx}sum{t}")

            @bb.on("vector")
            def _(v, t=t, xl=xl, st=st):
                bb.w(v, xl)
                bb.i(v.reduce_sum(out=ROWS[:, CS + t:CS + t + 1], in_=xblk[:, t], axis=AX), st)
            sq = bb.mark("sa", f"{pfx}sq{t}")

            @bb.on("scalar")
            def _(a, t=t, xl=xl, sq=sq, bv0=bb.ev(f"{tag}bve2") if sb == 0 and t == 0 else None):
                bb.w(a, xl, bv0)
                bb.i(a.activation(out=dump[:, t % 2], in_=xblk[:, t], func=AF.Square,
                                  accum_out=ROWS[:, CS + 4 + t:CS + 5 + t]), sq)
        r1 = bb.mark("sv", f"{pfx}rows1")

        @bb.on("vector")
        def _(v, r1=r1, sa_need=bb.ev(f"{pfx}sq{NT - 1}")):
            bb.w(v, sa_need)
            v.drain()
            r = ROWS
            c = CS
            v.tensor_scalar_mul(out=r[:, c:c + NT], in0=r[:, c:c + NT], scalar1=1.0 / D)
            v.drain()
            v.tensor_mul(out=r[:, c + 16:c + 16 + NT], in0=r[:, c:c + NT], in1=r[:, c:c + NT])
            v.drain()
            v.scalar_tensor_tensor(out=r[:, c + 16:c + 16 + NT], in0=r[:, c + 4:c + 4 + NT],
                                   scalar=1.0 / D, in1=r[:, c + 16:c + 16 + NT],
                                   op0=ALU.mult, op1=ALU.subtract)
            v.drain()
            v.tensor_scalar_add(out=r[:, c + 16:c + 16 + NT], in0=r[:, c + 16:c + 16 + NT], scalar1=EPS)
            v.drain()
            bb.i(v.engine_nop(), r1)
        r2e = bb.mark("sa", f"{pfx}rows2")

        @bb.on("scalar")
        def _(a, r1=r1, r2e=r2e):
            bb.w(a, r1)
            bb.i(a.activation(out=ROWS[:, CS + 16:CS + 16 + NT], in_=ROWS[:, CS + 16:CS + 16 + NT],
                              func=AF.Sqrt), r2e)
        r3 = bb.mark("sv", f"{pfx}rows3")

        @bb.on("vector")
        def _(v, r2e=r2e, r3=r3):
            bb.w(v, r2e)
            v.drain()
            r = ROWS
            c = CS
            v.reciprocal(out=r[:, c + 16:c + 16 + NT], in_=r[:, c + 16:c + 16 + NT])
            v.drain()
            v.tensor_mul(out=r[:, c + 8:c + 8 + NT], in0=r[:, c:c + NT], in1=r[:, c + 16:c + 16 + NT])
            v.drain()
            v.tensor_scalar_mul(out=r[:, c + 8:c + 8 + NT], in0=r[:, c + 8:c + 8 + NT], scalar1=-1.0)
            v.drain()
            bb.i(v.engine_nop(), r3)
        for t in range(NT):
            lx = bb.mark("sa", f"{pfx}lx{t}")

            @bb.on("scalar")
            def _(a, t=t, r3=r3, lx=lx,
                  trprev=bb.get(f"{pfx}tr{t - 2}") or bb.get(f"{tag}b{sb - 1}tr{t}")):
                bb.w(a, r3, trprev)
                bb.i(a.activation(out=xlns[:, t % 2], in_=xblk[:, t], func=AF.Identity,
                                  bias=ROWS[:, CS + 8 + t:CS + 9 + t],
                                  scale=ROWS[:, CS + 16 + t:CS + 17 + t]), lx)
            tr = bb.mark("sq", f"{pfx}tr{t}")

            @bb.on("tensor")
            def _(t_, t=t, lx=lx, tr=tr, prevev=bb.get(f"{pfx}trev{t - 1}")
                  or bb.get(f"{tag}b{sb - 1}trev{NT - 1}") or ctx.get("ptb_chain")):
                bb.w(t_, lx, bb.ev("consts"), prevev)
                for j in range(DC):
                    mm = t_.transpose(out=PTB[:, j * 128:(j + 1) * 128],
                                      in_=xlns[:, t % 2, j * 128:(j + 1) * 128], identity=IDB[:])
                bb.i(mm, tr)
            tre = bb.mark("sa", f"{pfx}trev{t}")

            @bb.on("scalar")
            def _(a, t=t, tr=tr, tre=tre):
                bb.w(a, tr)
                bb.i(a.copy(out=xln_fm[:, :, t * 128:(t + 1) * 128],
                            in_=PTB[:].rearrange("p (dc c) -> p dc c", dc=DC)), tre)
        xln_done = bb.ev(f"{pfx}trev{NT - 1}")

        for kind, wsb_ in (("q", wq_sb), ("k", wk_sb)):
            for jc in range(DC):
                qmm = bb.mark("sq", f"{pfx}{kind}mm{jc}")

                @bb.on("tensor")
                def _(t_, kind=kind, wsb_=wsb_, jc=jc, qmm=qmm, xln_done=xln_done, wl=wl,
                      evprev=bb.get(f"{pfx}{kind}e{jc - 1}")
                      or (bb.get(f"{pfx}qe{DC - 1}") if kind == "k"
                          else bb.get(f"{tag}b{sb - 1}scq{NPG - 1}"))):
                    bb.w(t_, xln_done, wl, evprev)
                    for d in range(DC):
                        mm = t_.matmul(out=PSBIG[:, 0:TB], lhsT=wsb_[:, jc, d], rhs=xln_fm[:, d],
                                       start=(d == 0), stop=(d == DC - 1))
                    bb.i(mm, qmm)
                qe = bb.mark("sa", f"{pfx}{kind}e{jc}")

                @bb.on("scalar")
                def _(a, kind=kind, jc=jc, qmm=qmm, qe=qe):
                    bb.w(a, qmm)
                    if kind == "q":
                        bb.i(a.activation(out=q_fm[:, jc], in_=PSBIG[:, 0:TB], func=AF.Identity,
                                          bias=ROWS[:, BQ8 + jc:BQ8 + jc + 1], scale=1.0 / 8.0), qe)
                    else:
                        a.activation(out=k_fe[0:64, jc], in_=PSBIG[0:64, 0:TB], func=AF.Identity,
                                     bias=ROWS[0:64, RBQ + 8 + jc:RBQ + 9 + jc], scale=1.0)
                        bb.i(a.activation(out=k_fo[64:128, jc], in_=PSBIG[64:128, 0:TB],
                                          func=AF.Identity,
                                          bias=ROWS[64:128, RBQ + 8 + jc:RBQ + 9 + jc], scale=1.0), qe)
        qkdone = bb.ev(f"{pfx}ke{DC - 1}")

        for t in range(NT):
            vmm = bb.mark("sq", f"{pfx}vmm{t}")

            @bb.on("tensor")
            def _(t_, t=t, vmm=vmm, xln_done=xln_done, wl=wl,
                  evprev=bb.get(f"{pfx}ve{t - 1}") or bb.get(f"{tag}b{sb - 1}oev1")):
                bb.w(t_, xln_done, wl, evprev)
                for dh in range(2):
                    for d in range(DC):
                        mm = t_.matmul(out=PSBIG[:, 2048 + dh * 512:2048 + (dh + 1) * 512],
                                       lhsT=xln_fm[:, d, t * 128:(t + 1) * 128],
                                       rhs=wv_sb[:, d, dh * 512:(dh + 1) * 512],
                                       start=(d == 0), stop=(d == DC - 1))
                bb.i(mm, vmm)
            ve = bb.mark("sv", f"{pfx}ve{t}")

            @bb.on("vector")
            def _(v, t=t, vmm=vmm, ve=ve):
                bb.w(v, vmm)
                bb.i(v.scalar_tensor_tensor(out=v_tm[:, t], in0=PSBIG[:, 2048:3072],
                                            scalar=1.0, in1=bvrow, op0=ALU.mult, op1=ALU.add), ve)
        vdone = bb.ev(f"{pfx}ve{NT - 1}")

        for pg in range(NPG):
            smm = bb.mark("sq", f"{pfx}smm{pg}")

            @bb.on("tensor")
            def _(t_, pg=pg, smm=smm, qkdone=qkdone,
                  scqprev=bb.get(f"{pfx}scq{pg - 1}") or bb.get(f"{tag}b{sb - 1}scq{NPG - 1}"),
                  prevv=bb.ev(f"{tag}prev") if sb == 0 and pg == 0 else None):
                bb.w(t_, qkdone, scqprev, prevv)
                for h in range(NHEAD):
                    kf = k_fe if h % 2 == 0 else k_fo
                    mm = t_.matmul(out=PSBIG[:, h * 128:(h + 1) * 128],
                                   lhsT=kf[:, h // 2, pg * 128:(pg + 1) * 128],
                                   rhs=q_fm[:, h // 2, pg * 128:(pg + 1) * 128],
                                   start=True, stop=True)
                bb.i(mm, smm)
            scq = bb.mark("sa", f"{pfx}scq{pg}")

            @bb.on("scalar")
            def _(a, pg=pg, smm=smm, scq=scq,
                  polprev=bb.get(f"{pfx}pol{pg - 1}") or bb.get(f"{tag}b{sb - 1}pol{NPG - 1}")):
                bb.w(a, smm, polprev)
                for pp in range(4):
                    pv = (PSBIG[32 * pp:32 * pp + 32, 0:NHEAD * 128]
                          .rearrange("p (h c) -> p h c", h=NHEAD)[:, :, 32 * pp:32 * pp + 32])
                    a.activation(out=usb[32 * pp:32 * pp + 32, :]
                                 .rearrange("p (h c) -> p h c", h=NHEAD),
                                 in_=pv, func=AF.Square)
                    ins = a.copy(out=ssb[32 * pp:32 * pp + 32, :]
                                 .rearrange("p (h c) -> p h c", h=NHEAD), in_=pv)
                bb.i(ins, scq)
            pol = bb.mark("sv", f"{pfx}pol{pg}")

            @bb.on("vector")
            def _(v, pg=pg, scq=scq, pol=pol, prevv=bb.ev(f"{tag}prev") if sb == 0 and pg == 0 else None):
                bb.w(v, scq, prevv)
                v.drain()
                v.tensor_scalar(out=wsb[:], in0=usb[:], scalar1=1.0 / 480.0, scalar2=-1.0 / 48.0,
                                op0=ALU.mult, op1=ALU.add)
                v.drain()
                v.tensor_mul(out=wsb[:], in0=wsb[:], in1=usb[:])
                v.drain()
                v.tensor_scalar_add(out=wsb[:], in0=wsb[:], scalar1=0.25)
                v.drain()
                v.tensor_mul(out=wsb[:], in0=wsb[:], in1=ssb[:])
                v.drain()
                v.tensor_scalar_add(out=wsb[:], in0=wsb[:], scalar1=0.5)
                v.drain()
                v.tensor_mul(out=wsb[:], in0=wsb[:], in1=grow[:])
                v.drain()
                for pp in range(4):
                    ins = v.tensor_add(
                        out=probs[32 * pp:32 * pp + 32, pg, :]
                        .rearrange("p (h c) -> p h c", h=NHEAD)[:, :, 32 * pp:32 * pp + 32],
                        in0=wsb[32 * pp:32 * pp + 32, :].rearrange("p (h s) -> p h s", h=NHEAD),
                        in1=brow[32 * pp:32 * pp + 32, :].rearrange("p (h s) -> p h s", h=NHEAD))
                bb.i(ins, pol)
            cmm = bb.mark("sq", f"{pfx}cmm{pg}")

            @bb.on("tensor")
            def _(t_, pg=pg, cmm=cmm, pol=pol, vdone=vdone,
                  oevprev=(bb.get(f"{tag}b{sb - 1}oev{NT - 1}") if pg == 0
                           else bb.get(f"{pfx}oev{pg - 1}"))):
                bb.w(t_, pol, vdone, oevprev)
                for h in range(NHEAD):
                    mm = t_.matmul(out=PSBIG[:, 2048 + h * 64:2048 + (h + 1) * 64],
                                   lhsT=probs[:, pg, h * 128:(h + 1) * 128],
                                   rhs=v_tm[:, pg, h * 64:(h + 1) * 64],
                                   start=True, stop=True)
                bb.i(mm, cmm)
            oev = bb.mark("sv", f"{pfx}oev{pg}")

            @bb.on("vector")
            def _(v, pg=pg, oev=oev, cmm=cmm, stprev=bb.get(f"{tag}b{sb - 1}ost{pg}")):
                bb.w(v, cmm, stprev)
                bb.i(v.scalar_tensor_tensor(out=obuf[:, pg], in0=PSBIG[:, 2048:3072],
                                            scalar=1.0, in1=xblk[:, pg],
                                            op0=ALU.mult, op1=ALU.add), oev)
            ost = bb.mark(f"d_o{tag}{pg}", f"{pfx}ost{pg}")

            @bb.on("sync")
            def _(s, pg=pg, sb=sb, oev=oev, ost=ost):
                bb.w(s, oev)
                bb.i(s.dma_start(out=dst[sb * TB + pg * 128:sb * TB + (pg + 1) * 128, :],
                                 in_=obuf[:, pg]), ost)
        bb.events[f"{pfx}oev1"] = bb.ev(f"{pfx}oev{NT - 1}")

    ctx["ptb_chain"] = bb.ev(f"{tag}b{NSB - 1}trev{NT - 1}")
    ctx[f"{tag}_fin"] = [bb.total(f"d_o{tag}0"), bb.total(f"d_o{tag}1")]


def kernel(L0_attn_x, L0_attn_W_qk, L0_attn_W_v, L0_attn_alibi_slope,
           L0_ffn_w_gate, L0_ffn_w_up, L0_ffn_w_down,
           L1_attn_Wqkv, L1_attn_bqkv, L1_attn_seq_gamma, L1_attn_seq_beta,
           L1_ffn_w_gate, L1_ffn_w_up, L1_ffn_w_down):
    from concourse.bass_utils import run_bass_kernel_spmd
    NB, SL = L0_attn_x.shape[0], L0_attn_x.shape[1]
    bpc = NB // NCORES
    spc = SL // NCORES
    TOK = bpc * SL

    x = np.ascontiguousarray(L0_attn_x, dtype=np.float32)
    common = dict(
        w_qk=np.ascontiguousarray(L0_attn_W_qk, np.float32),
        w_v=np.ascontiguousarray(L0_attn_W_v, np.float32),
        alibi=np.ascontiguousarray(L0_attn_alibi_slope, np.float32).reshape(1),
        w0g=np.ascontiguousarray(L0_ffn_w_gate, np.float32),
        w0u=np.ascontiguousarray(L0_ffn_w_up, np.float32),
        w0d=np.ascontiguousarray(L0_ffn_w_down, np.float32),
        wqkv=np.ascontiguousarray(L1_attn_Wqkv, np.float32),
        bqkv=np.ascontiguousarray(L1_attn_bqkv, np.float32).reshape(3 * D),
        gamma=np.ascontiguousarray(L1_attn_seq_gamma, np.float32).reshape(NHEAD),
        beta=np.ascontiguousarray(L1_attn_seq_beta, np.float32).reshape(NHEAD),
        w1g=np.ascontiguousarray(L1_ffn_w_gate, np.int32),
        w1u=np.ascontiguousarray(L1_ffn_w_up, np.int32),
        w1d=np.ascontiguousarray(L1_ffn_w_down, np.int32),
    )
    in_maps = []
    for c in range(NCORES):
        xc = np.ascontiguousarray(x[c * bpc:(c + 1) * bpc].reshape(TOK, D))
        in_maps.append(dict(x_in=xc, **common))

    nc = build(NB, SL, upto=6)
    res = run_bass_kernel_spmd(nc, in_maps, core_ids=list(range(NCORES)))

    out = np.empty((NB, SL, D), np.float32)
    for c in range(NCORES):
        sh = np.asarray(res.results[c]["y_out"]).reshape(spc, NB, D)
        out[:, c * spc:(c + 1) * spc, :] = sh.transpose(1, 0, 2)
    return out
